# version 27
# baseline (speedup 1.0000x reference)
import numpy as np
import concourse.bass as bass
import concourse.mybir as mybir
from concourse.bass_utils import run_bass_kernel_spmd

F32 = mybir.dt.float32
BF16 = mybir.dt.bfloat16
I32 = mybir.dt.int32
AF = mybir.ActivationFunctionType
ALU = mybir.AluOpType

B, T, D = 4, 4096, 1024
H, KD = 16, 64
NC = 8
TH = T // 2
TT = 512
NTT = TH // TT
CT = D // 128
FI = 3072
FOT = FI // 128
QMAX = 32767.0
EPS = 1e-5

ENG_ATTR = {"pe": "tensor", "dve": "vector", "act": "scalar",
            "pool": "gpsimd", "sp": "sync"}


NDMASEM = 24


class Prog:
    def __init__(self):
        self.items = []
        self.tot = {}
        self.state = {}
        self.wm = {e: {} for e in ENG_ATTR}
        self.bufghosts = {}
        self.dcur = 0
        self.ncoll = 0

    def emit(self, eng, fn, reads=(), writes=(), inc=1, sem=None, defer=False):
        if inc == 16 and sem is None:
            sem = f"d{self.dcur}"
            self.dcur = (self.dcur + 1) % NDMASEM
        sem = sem or eng
        waits = {}
        if inc == 16 and self.tot.get(sem, 0) > 0:
            waits[sem] = self.tot[sem]

        def need(s, v):
            if s == "pe" and eng == "pe":
                return
            if v > self.wm[eng].get(s, 0):
                waits[s] = max(waits.get(s, 0), v)

        def need_w(w):
            if w is None:
                return
            if isinstance(w, list):
                for r in w:
                    need(*r)
            else:
                need(*w)

        def hazard(k):
            st = self.state.get(k)
            if st:
                need_w(st[0])
                for s, v in st[1]:
                    need(s, v)

        for k in reads:
            st = self.state.get(k)
            if st:
                need_w(st[0])
        for k in writes:
            hazard(k)
            if k.startswith("B:"):
                for g in self.bufghosts.get(k.split(":", 2)[1], ()):
                    hazard(g)
        for s, v in waits.items():
            self.wm[eng][s] = v
        if defer:
            ref = (sem, self.tot.get(sem, 0) + 1)
            self.items.append((eng, waits, fn, sem, 0))
        else:
            self.tot[sem] = self.tot.get(sem, 0) + inc
            ref = (sem, self.tot[sem])
            self.items.append((eng, waits, fn, sem, inc))
        for k in reads:
            self.state.setdefault(k, [None, []])[1].append(ref)
        for k in writes:
            self.state[k] = [ref, []]

    def merge_deps(self, newkey, oldkeys):
        refs = []
        for k in oldkeys:
            st = self.state.pop(k, None)
            if st:
                if st[0] is not None:
                    refs.extend(st[0] if isinstance(st[0], list) else [st[0]])
                refs.extend(st[1])
        self.state[newkey] = [refs, []]

    def reads_of(self, key):
        return self.state.get(key)

    def replay(self, block, sems):
        for ename in ("sp", "pe", "dve", "act", "pool"):
            sub = [it for it in self.items if it[0] == ename]

            def runner(engobj, sub=sub):
                for _, waits, fn, sem, inc in sub:
                    for s in sorted(waits):
                        engobj.wait_ge(sems[s], waits[s])
                    ins = fn(engobj)
                    if inc:
                        ins.then_inc(sems[sem], inc)
            getattr(block, ENG_ATTR[ename])(runner)


class Arena:
    def __init__(self, nc, prog, name, nbytes):
        self.t = nc.alloc_sbuf_tensor(name, [128, nbytes], mybir.dt.int8)
        self.prog = prog
        self.free_list = [(0, nbytes)]
        self.owners = {}
        self.ghosts = []
        self.peak = 0

    def alloc(self, name, nbytes, align=64, end=False):
        nbytes = -(-nbytes // align) * align
        fits = [(sz, i) for i, (off, sz) in enumerate(self.free_list)
                if nbytes <= sz]
        for _, i in sorted(fits, reverse=end):
            off, sz = self.free_list[i]
            if end:
                aoff = off + sz - nbytes
                self.free_list[i:i + 1] = (
                    [(off, sz - nbytes)] if sz > nbytes else [])
            else:
                aoff = off
                self.free_list[i:i + 1] = (
                    [(off + nbytes, sz - nbytes)] if sz > nbytes else [])
            self.owners[name] = (aoff, nbytes)
            self.peak = max(self.peak, max(
                o + s for o, s in self.owners.values()))
            self.prog.bufghosts[name] = self.ghost_keys(aoff, nbytes)
            return Buf(self, name, aoff, nbytes)
        raise MemoryError(f"arena alloc {name}: {nbytes} doesn't fit "
                          f"(free={self.free_list})")

    def free(self, buf):
        off, sz = self.owners.pop(buf.name)
        gk = f"ghost:{buf.name}:{off}"
        pref = "B:" + buf.name + ":"
        self.prog.merge_deps(gk, [k for k in list(self.prog.state)
                                  if isinstance(k, str) and k.startswith(pref)])
        self.ghosts.append((off, sz, gk))
        self.free_list.append((off, sz))
        self.free_list.sort()
        merged = []
        for o, s in self.free_list:
            if merged and merged[-1][0] + merged[-1][1] == o:
                merged[-1] = (merged[-1][0], merged[-1][1] + s)
            else:
                merged.append((o, s))
        self.free_list = merged

    def ghost_keys(self, off, size):
        return tuple(gk for o, s, gk in self.ghosts
                     if o < off + size and off < o + s
                     and gk in self.prog.state)


class Buf:
    def __init__(self, arena, name, off, size):
        self.arena, self.name, self.off, self.size = arena, name, off, size

    def view(self, dtype, shape, byteoff=0):
        n = 1
        for s in shape:
            n *= s
        nb = n * (2 if dtype == BF16 else 4)
        assert byteoff + nb <= self.size, (self.name, shape, byteoff, self.size)
        ap = self.arena.t.ap()[:, self.off + byteoff: self.off + byteoff + nb]
        ap = ap.bitcast(dtype)
        if len(shape) > 1:
            ap = ap.rearrange(
                "p (" + " ".join(f"a{i}" for i in range(len(shape))) + ") -> p "
                + " ".join(f"a{i}" for i in range(len(shape))),
                **{f"a{i}": shape[i] for i in range(len(shape))})
        return ap

    def key(self, sub=""):
        return f"B:{self.name}:{sub}"

    def wfirst(self, sub=""):
        return (self.key(sub),) + self.arena.ghost_keys(self.off, self.size)


def build_nc(debug=False):
    nc = bass.Bass("TRN2", target_bir_lowering=False)
    p = Prog()

    def ein(name, shape, dt=F32):
        return nc.dram_tensor(name, shape, dt, kind="ExternalInput").ap()

    def idram(name, shape, dt):
        kind = "ExternalOutput" if debug else "Internal"
        return nc.dram_tensor(name, shape, dt, kind=kind).ap()

    xs = ein("xs", [TH + 1, D])
    wrT = ein("wrT", [D, D]); wkT = ein("wkT", [D, D])
    wvT = ein("wvT", [D, D]); woT = ein("woT", [D, D])
    tm_in = ein("tm", [3, 128, CT])
    decay = ein("decay", [128, CT]); tfirst = ein("tfirst", [128, CT])
    gnw = ein("gnw", [128, CT]); gnb = ein("gnb", [128, CT])
    wg = ein("w_gate", [FI, D]); wu = ein("w_up", [FI, D])
    wd = ein("w_down", [D, FI])
    selv = ein("selv", [8, 1])
    selp2 = ein("selp2", [8, 1])
    ones128 = ein("ones128", [1, 128])
    halfT = ein("halfT", [128, 2])
    halfB = ein("halfB", [2, 128])
    yout = nc.dram_tensor("y", [TH, D], F32, kind="ExternalOutput").ap()

    if debug:
        dbg_wkv = idram("dbg_wkv", [CT, 128, TH], BF16)
        dbg_out = idram("dbg_out", [CT, 128, TH], BF16)
        dbg_sc = idram("dbg_sc", [4, 128, 64], F32)
        dbg_p2 = idram("dbg_p2", [FOT, 128, TH], BF16)
        dbg_p3 = idram("dbg_p3", [FOT, 128, TH], BF16)
    a_dram = idram("a_dram", [CT, 128, TH], BF16)
    b_dram = idram("b_dram", [CT, 128, TH], BF16)
    wq_g = idram("wq_g", [FI, D], BF16)
    wq_u = idram("wq_u", [FI, D], BF16)
    wq_d = idram("wq_d", [D, FI], BF16)
    xln_dram = idram("xln_dram", [TH, D], BF16)
    xn0_dram = idram("xn0_dram", [D], F32)
    att_dram = idram("att_dram", [D, TH], BF16)
    res_dram = idram("res_dram", [TH, D], F32)
    xq_dram = idram("xq_dram", [TH, D], BF16)
    up_dram = idram("up_dram", [FOT, 128, TH], BF16)
    ffn_dram = idram("ffn_dram", [D, TH], BF16)
    carry_red = idram("carry_red", [2048], F32)
    cc_in = {}
    cc_out = {}
    for nm, ishape, oshape in (("carry", [1, 2048], [8, 2048]),
                               ("gnst", [1, 32], [8, 32]),
                               ("mx0", [1, 4], [8, 4]),
                               ("mx1", [1, 4], [8, 4]),
                               ("mx2", [1, 4], [8, 4]),
                               ("mx3", [1, 4], [8, 4]),
                               ("mx4", [1, 4], [8, 4])):
        cc_in[nm] = nc.dram_tensor(f"cc_{nm}_in", ishape, F32,
                                   kind="Internal").ap()
        cc_out[nm] = nc.dram_tensor(f"cc_{nm}_out", oshape, F32,
                                    kind="Internal").ap()

    SBB = 206 * 1024
    ar = Arena(nc, p, "arena", SBB)
    ps = [nc.alloc_psum_tensor(f"ps{i}", [128, 512], F32) for i in range(8)]
    PSK = [f"psum{i}" for i in range(8)]

    def dma(dst, src, reads, writes, eng="sp", slow=False):
        p.emit(eng, lambda e, d=dst, s=src, sl=slow: e.dma_start(
            out=d, in_=s, allow_slow_non_contiguous=sl),
               reads=reads, writes=writes, inc=16)

    def dmaT(dst, src, reads, writes, eng="sp"):
        p.emit(eng, lambda e, d=dst, s=src: e.dma_start_transpose(out=d, in_=s),
               reads=reads, writes=writes, inc=16)

    def ts(eng, out, in0, s1, s2, op0, op1, reads, writes, accum=None):
        if s2 is None:
            p.emit(eng, lambda e, o=out, i=in0, a=s1, oa=op0, ac=accum:
                   e.tensor_scalar(o, i, a, None, oa, accum_out=ac),
                   reads=reads, writes=writes)
        else:
            p.emit(eng, lambda e, o=out, i=in0, a=s1, b=s2, oa=op0, ob=op1,
                   ac=accum:
                   e.tensor_scalar(o, i, a, b, oa, ob, accum_out=ac),
                   reads=reads, writes=writes)

    def tt(eng, out, in0, in1, op, reads, writes):
        p.emit(eng, lambda e, o=out, a=in0, b=in1, oo=op:
               e.tensor_tensor(o, a, b, oo), reads=reads, writes=writes)

    def stt(eng, out, in0, s, in1, op0, op1, reads, writes, accum=None):
        p.emit(eng, lambda e, o=out, a=in0, sc=s, b=in1, oa=op0, ob=op1,
               ac=accum:
               e.scalar_tensor_tensor(o, a, sc, b, oa, ob, accum_out=ac),
               reads=reads, writes=writes)

    def act(eng, out, in_, func, reads, writes, bias=0.0, scale=1.0,
            accum=None):
        p.emit(eng, lambda e, o=out, i=in_, f=func, b=bias, s=scale, ac=accum:
               e.activation(o, i, f, bias=b, scale=s, accum_out=ac),
               reads=reads, writes=writes)

    def cp(eng, out, in_, reads, writes):
        p.emit(eng, lambda e, o=out, i=in_: e.tensor_copy(out=o, in_=i),
               reads=reads, writes=writes)

    def red(eng, out, in_, op, reads, writes, absval=None):
        p.emit(eng, lambda e, o=out, i=in_, oo=op, av=absval:
               e.tensor_reduce(o, i, mybir.AxisListType.X, oo,
                               apply_absolute_value=av),
               reads=reads, writes=writes)

    def mm(out, lhsT, rhs, start, stop, reads, writes, defer=True):
        p.emit("pe", lambda e, o=out, l=lhsT, r=rhs, st=start, sp_=stop:
               e.matmul(o, l, r, start=st, stop=sp_),
               reads=reads, writes=writes, defer=(defer and not stop))

    def scan(eng, out, d0, d1, init, reads, writes):
        p.emit(eng, lambda e, o=out, a=d0, b=d1, i=init:
               e.tensor_tensor_scan(o, a, b, i, ALU.mult, ALU.add),
               reads=reads, writes=writes)

    def coll(nm, reads):
        p.ncoll += 1
        p.emit("pool", lambda e, i=cc_in[nm], o=cc_out[nm]:
               e.collective_compute("AllGather", ALU.bypass,
                                    replica_groups=[list(range(NC))],
                                    ins=[i.opt()], outs=[o.opt()]),
               reads=reads + (f"ccin:{nm}",), writes=(f"ccout:{nm}",),
               inc=1, sem=f"cc{p.ncoll}")

    small = ar.alloc("small", 12 * CT * 4)
    SM = {"w": 0, "eu": 1, "tmr": 2, "tmk": 3, "tmv": 4, "gnw": 5, "gnb": 6,
          "A": 7, "Bb": 8, "t0": 9, "t1": 10, "t2": 11}

    def sm(nm):
        i = SM[nm]
        return small.view(F32, [12, CT])[:, i, :]

    consts = ar.alloc("consts", 2048)
    ones_sb = consts.view(F32, [128], byteoff=0)[:1, :]
    halfT_sb = consts.view(F32, [2], byteoff=512)
    halfB_sb = consts.view(F32, [128], byteoff=1024)[:2, :]
    selv_sb = consts.view(F32, [1], byteoff=1536)[:8, :]
    selp2_sb = consts.view(F32, [1], byteoff=1600)[:8, :]
    scal = ar.alloc("scal", 1024)

    def scv(i):
        return scal.view(F32, [256])[:, i:i + 1]

    mxrow = ar.alloc("mxrow", 1024)
    epsb = ar.alloc("epsb", 64)
    epsv = epsb.view(F32, [1])
    p.emit("pool", lambda e, o=epsv: e.memset(o, EPS),
           reads=(), writes=(epsb.key("v"),))
    EPSK = (epsb.key("v"),)

    prm = ar.alloc("prm", 2 * CT * 4)
    prm_v = prm.view(F32, [2, CT])
    dma(prm_v[:, 0, :], decay, reads=(), writes=prm.wfirst("decay"))
    dma(prm_v[:, 1, :], tfirst, reads=(), writes=(prm.key("tf"),))
    act("act", sm("t0"), prm_v[:, 0, :], AF.Exp,
        reads=(prm.key("decay"),), writes=small.wfirst("e1"))
    act("act", sm("w"), sm("t0"), AF.Exp, scale=-1.0,
        reads=(small.key("e1"),), writes=(small.key("w"),))
    act("act", sm("eu"), prm_v[:, 1, :], AF.Exp,
        reads=(prm.key("tf"),), writes=(small.key("eu"),))
    dma(small.view(F32, [12, CT])[:, 2:5, :],
        tm_in.rearrange("a p c -> p a c"),
        reads=(), writes=(small.key("tmr"), small.key("tmk"),
                          small.key("tmv")))
    dma(sm("gnw"), gnw, reads=(), writes=(small.key("gnw"),))
    dma(sm("gnb"), gnb, reads=(), writes=(small.key("gnb"),))
    dma(ones_sb, ones128, reads=(), writes=consts.wfirst("ones"))
    dma(halfT_sb, halfT, reads=(), writes=(consts.key("halfT"),))
    dma(halfB_sb, halfB, reads=(), writes=(consts.key("halfB"),))
    dma(selv_sb, selv, reads=(), writes=(consts.key("selv"),))
    dma(selp2_sb, selp2, reads=(), writes=(consts.key("selp2"),))

    rstore = ar.alloc("rstore", CT * TH * 2)
    rst_v = rstore.view(BF16, [CT, TH])
    wbuf = ar.alloc("wbuf", 3 * CT * D * 2)
    wb_v = wbuf.view(BF16, [3, CT, D])
    wtmp = ar.alloc("wtmp", CT * D * 4)
    for pr, src in enumerate((wrT, wkT, wvT)):
        wtf = wtmp.view(F32, [CT, D])
        dma(wtf, src.rearrange("(c p) d -> p c d", p=128),
            reads=tuple(wtmp.key(f"r{pr - 1}:{c}") for c in range(CT))
            if pr else (),
            writes=wtmp.wfirst(f"w{pr}") if pr == 0 else (wtmp.key(f"w{pr}"),))
        for c in range(CT):
            cp("dve" if c % 2 == 0 else "pool", wb_v[:, pr, c, :],
               wtf[:, c, :],
               reads=(wtmp.key(f"w{pr}"),),
               writes=wbuf.wfirst(f"{pr}:{c}") if pr == 0 and c == 0
               else (wbuf.key(f"{pr}:{c}"),))
            p.state[wtmp.key(f"r{pr}:{c}")] = [
                p.state[wbuf.key(f"{pr}:{c}")][0], []]

    ar.free(wtmp)
    stats = ar.alloc("stats", 4096)
    XNW = 528
    xnf = [ar.alloc(f"xnf{i}", CT * XNW * 2) for i in range(2)]

    prow = ar.alloc("prow", 4096 + 64)
    prow_f = prow.view(F32, [1024])[:1, :]
    prow_s = prow.view(F32, [16], byteoff=4096)[:1, :]
    dma(prow_f, xs[0:1, :], reads=(), writes=prow.wfirst("x"))
    red("dve", prow_s[:, 0:1], prow_f, ALU.add,
        reads=(prow.key("x"),), writes=(prow.key("sum"),))
    act("act", prow.view(F32, [1024], byteoff=0)[:1, :], prow_f, AF.Square,
        reads=(prow.key("x"),), writes=(prow.key("x"),),
        accum=prow_s[:, 1:2])
    p.state[prow.key("sq")] = [p.state[prow.key("x")][0], []]
    dma(prow_f, xs[0:1, :], reads=(prow.key("sq"),), writes=(prow.key("x2"),))
    ts("dve", prow_s[:, 2:3], prow_s[:, 0:1], 1.0 / D, None, ALU.mult, None,
       reads=(prow.key("sum"),), writes=(prow.key("mean"),))
    act("act", prow_s[:, 5:6], prow_s[:, 2:3], AF.Square,
        reads=(prow.key("mean"),), writes=(prow.key("m2"),))
    stt("dve", prow_s[:, 3:4], prow_s[:, 1:2], 1.0 / D, prow_s[:, 5:6],
        ALU.mult, ALU.subtract,
        reads=(prow.key("sq"), prow.key("m2")), writes=(prow.key("var"),))
    act("act", prow_s[:, 6:7], prow_s[:, 3:4], AF.Ln, bias=epsv[:1, :],
        reads=(prow.key("var"),) + EPSK, writes=(prow.key("lnv"),))
    act("act", prow_s[:, 7:8], prow_s[:, 6:7], AF.Exp, scale=-0.5,
        reads=(prow.key("lnv"),), writes=(prow.key("rstd"),))
    ts("dve", prow_f, prow_f, prow_s[:, 2:3], prow_s[:, 7:8],
       ALU.subtract, ALU.mult,
       reads=(prow.key("x2"), prow.key("mean"), prow.key("rstd")),
       writes=(prow.key("xn0"),))
    dma(xn0_dram[None, :], prow_f, reads=(prow.key("xn0"),),
        writes=("xn0_dram",))
    x0tmp = stats.view(F32, [64])[:, 32:40]
    dma(x0tmp, xn0_dram.rearrange("(c p) -> p c", p=128),
        reads=("xn0_dram",), writes=(stats.key("xn0"),), slow=True)
    xnfv = [xnf[i].view(BF16, [CT, XNW]) for i in range(2)]
    cp("dve", xnfv[0][:, :, 15], x0tmp,
       reads=(stats.key("xn0"),), writes=xnf[0].wfirst("prevcol"))
    ar.free(prow)

    xtok = ar.alloc("xtok", 4 * 1024 * 4)
    xln_t = ar.alloc("xln_t", 4 * 1024 * 2)
    sqs = ar.alloc("sqs", 1024 * 4)
    mixb = ar.alloc("mixb", 2 * 3 * CT * TT * 2)
    dscr = ar.alloc("dscr", CT * TT * 2)
    abst = ar.alloc("abst", 2 * CT * TT * 2)
    asp = abst.view(BF16, [2, CT, TT])

    def stat_v(i):
        return stats.view(F32, [64])[:, i * 4:(i + 1) * 4]

    def a1_S0(ti):
        xkey = xtok.key(f"t{ti}")
        dma(xtok.view(F32, [4, 1024]),
            xs[1 + ti * TT: 1 + (ti + 1) * TT, :]
            .rearrange("(a p) d -> p a d", p=128),
            reads=(xtok.key(f"done{ti - 1}"),) if ti else (),
            writes=xtok.wfirst(f"t{ti}") if ti == 0 else (xkey,))

    def a1_S1(ti):
        xkey = xtok.key(f"t{ti}")
        xl_done = []
        for pt in range(4):
            sv = stat_v(pt)
            xv = xtok.view(F32, [4, 1024])[:, pt, :]
            red("dve", sv[:, 0:1], xv, ALU.add,
                reads=(xkey,), writes=(stats.key(f"s{pt}"),))
            act("act", sqs.view(F32, [1024]), xv, AF.Square,
                reads=(xkey,), writes=(sqs.key("w"),) if ti or pt
                else sqs.wfirst("w"), accum=sv[:, 1:2])
            p.state[stats.key(f"q{pt}")] = [p.state[sqs.key("w")][0], []]
            ts("dve", sv[:, 2:3], sv[:, 0:1], 1.0 / D, None, ALU.mult, None,
               reads=(stats.key(f"s{pt}"),), writes=(stats.key(f"m{pt}"),))
            act("act", sv[:, 0:1], sv[:, 2:3], AF.Square,
                reads=(stats.key(f"m{pt}"),), writes=(stats.key(f"mm{pt}"),))
            stt("dve", sv[:, 1:2], sv[:, 1:2], 1.0 / D, sv[:, 0:1],
                ALU.mult, ALU.subtract,
                reads=(stats.key(f"q{pt}"), stats.key(f"mm{pt}")),
                writes=(stats.key(f"v{pt}"),))
            act("act", sv[:, 1:2], sv[:, 1:2], AF.Ln, bias=epsv,
                reads=(stats.key(f"v{pt}"),) + EPSK, writes=(stats.key(f"lv{pt}"),))
            act("act", sv[:, 1:2], sv[:, 1:2], AF.Exp, scale=-0.5,
                reads=(stats.key(f"lv{pt}"),), writes=(stats.key(f"r{pt}"),))
            ts("dve", xln_t.view(BF16, [4, 1024])[:, pt, :], xv,
               sv[:, 2:3], sv[:, 1:2], ALU.subtract, ALU.mult,
               reads=(xkey, stats.key(f"m{pt}"), stats.key(f"r{pt}")),
               writes=xln_t.wfirst(f"{pt}") if ti == 0 and pt == 0
               else (xln_t.key(f"{pt}"),))
            xl_done.append(p.state[xln_t.key(f"{pt}")][0])
        p.state[xtok.key(f"done{ti}")] = [list(xl_done), []]
        dma(xln_dram[ti * TT:(ti + 1) * TT, :]
            .rearrange("(a p) d -> p a d", p=128),
            xln_t.view(BF16, [4, 1024]),
            reads=tuple(xln_t.key(f"{pt}") for pt in range(4)),
            writes=(f"xln_dram:{ti}",))
        ib = ti % 2
        for c in range(CT):
            dmaT(xnfv[ib][:, c, 16:16 + TT],
                 xln_dram[ti * TT:(ti + 1) * TT, c * 128:(c + 1) * 128],
                 reads=(f"xln_dram:{ti}",) + ((xnf[ib].key(f"used{ti - 2}"),)
                                              if ti >= 2 else ()),
                 writes=xnf[ib].wfirst(f"d:{c}") if ti < 2 and c == 0
                 else (xnf[ib].key(f"d:{c}"),))

    def a1_S2(ti):
        ib = ti % 2
        xnk = tuple(xnf[ib].key(f"d:{c}") for c in range(CT))
        prevk = (xnf[ib].key("prevcol"),)
        sb = ti % 2
        dsv = dscr.view(BF16, [CT, TT])
        mixv = mixb.view(BF16, [2, 3, CT, TT])[:, sb]
        for c in range(CT):
            tt("dve" if c % 2 else "pool", dsv[:, c, :],
               xnfv[ib][:, c, 16:16 + TT], xnfv[ib][:, c, 15:15 + TT],
               ALU.subtract,
               reads=(xnk[c],) + prevk,
               writes=dscr.wfirst(f"{c}") if ti == 0 and c == 0
               else (dscr.key(f"{c}"),))
        for pr in range(3):
            tmk_ = small.key(("tmr", "tmk", "tmv")[pr])
            tmv_ = small.view(F32, [12, CT])[:, 2 + pr, :]
            for c in range(CT):
                stt("dve", mixv[:, pr, c, :],
                    dsv[:, c, :], tmv_[:, c:c + 1],
                    xnfv[ib][:, c, 15:15 + TT], ALU.mult, ALU.add,
                    reads=(dscr.key(f"{c}"), tmk_, xnk[c]) + prevk,
                    writes=mixb.wfirst(f"{sb}:0:0")
                    if ti < 2 and pr == 0 and c == 0
                    else (mixb.key(f"{sb}:{pr}:{c}"),))
        ob = (ti + 1) % 2
        cp("dve", xnfv[ob][:, :, 15], xnfv[ib][:, :, 16 + TT - 1],
           reads=xnk, writes=(xnf[ob].key("prevcol"),)
           if ti > 0 else xnf[ob].wfirst("prevcol"))
        p.state[xnf[ib].key(f"used{ti}")] = [[
            p.state[mixb.key(f"{sb}:{pr}:{c}")][0]
            for pr in range(3) for c in range(CT)], []]

    def a1_S3(ti):
        ib = ti % 2
        sb = ti % 2
        mixv = mixb.view(BF16, [2, 3, CT, TT])[:, sb]
        for pr in range(3):
            for co in range(CT):
                bank = (pr * CT + co) % 8
                pk = PSK[bank]
                for k in range(CT):
                    mm(ps[bank].ap(), wb_v[:, pr, k, co * 128:(co + 1) * 128],
                       mixv[:, pr, k, :],
                       start=(k == 0), stop=(k == CT - 1),
                       reads=(wbuf.key(f"{pr}:{k}"),
                              mixb.key(f"{sb}:{pr}:{k}")),
                       writes=(pk,))
                if pr == 0:
                    act("act", rst_v[:, co, ti * TT:(ti + 1) * TT],
                        ps[bank].ap(), AF.Identity, reads=(pk,),
                        writes=rstore.wfirst(f"{co}:{ti}")
                        if ti == 0 and co == 0 else (rstore.key(f"{co}:{ti}"),))
                elif pr == 1:
                    act("act", asp[:, 1, co, :], ps[bank].ap(), AF.Exp,
                        reads=(pk,),
                        writes=abst.wfirst(f"b{co}") if ti == 0 and co == 0
                        else (abst.key(f"b{co}"),))
                else:
                    tt("dve", asp[:, 0, co, :], asp[:, 1, co, :],
                       ps[bank].ap(), ALU.mult,
                       reads=(abst.key(f"b{co}"), pk),
                       writes=abst.wfirst(f"a{co}") if ti == 0 and co == 0
                       else (abst.key(f"a{co}"),))
        dma(a_dram[:, :, ti * TT:(ti + 1) * TT].rearrange("c p t -> p c t"),
            asp[:, 0, :, :],
            reads=tuple(abst.key(f"a{c}") for c in range(CT)),
            writes=(f"a_dram:{ti}",))
        dma(b_dram[:, :, ti * TT:(ti + 1) * TT].rearrange("c p t -> p c t"),
            asp[:, 1, :, :],
            reads=tuple(abst.key(f"b{c}") for c in range(CT)),
            writes=(f"b_dram:{ti}",))
        for c in range(CT):
            p.state[abst.key(f"b{c}")][1].append(p.state[f"b_dram:{ti}"][0])
            p.state[abst.key(f"a{c}")][1].append(p.state[f"a_dram:{ti}"][0])

    a1_S0(0)
    a1_S1(0)
    for ti in range(NTT):
        if ti + 1 < NTT:
            a1_S0(ti + 1)
        a1_S2(ti)
        if ti + 1 < NTT:
            a1_S1(ti + 1)
        a1_S3(ti)

    for bb in (xtok, xln_t, sqs, mixb, dscr, xnf[0], xnf[1], abst):
        ar.free(bb)
    p.merge_deps("a_dram:all", [f"a_dram:{i}" for i in range(NTT)])
    p.merge_deps("b_dram:all", [f"b_dram:{i}" for i in range(NTT)])

    DQI = 512
    NDQS = 2
    deq = ar.alloc("deq", NDQS * (DQI * 4 * 3 + DQI * 2 + 512), end=True)
    DSLOT = DQI * 4 * 3 + DQI * 2 + 512

    def dequant_gen(tag, w_src, wq2, O, I):
        nb = DQI // 32
        units = [(oc, ic) for oc in range(O // 128) for ic in range(I // DQI)]
        for ui, (oc, ic) in enumerate(units):
            s = ui % NDQS
            bo = s * DSLOT
            wn = deq.view(F32, [DQI], byteoff=bo)
            q3 = deq.view(F32, [DQI], byteoff=bo + DQI * 4)
            qiw = deq.view(I32, [DQI], byteoff=bo + DQI * 8)
            st_ = deq.view(BF16, [DQI], byteoff=bo + DQI * 12)
            bm = deq.view(F32, [nb * 2], byteoff=bo + DQI * 14)
            K = lambda nm: deq.key(f"{tag}{nm}{s}")
            dma(wn, w_src[oc * 128:(oc + 1) * 128, ic * DQI:(ic + 1) * DQI],
                reads=(K("done"),) if ui >= NDQS else (),
                writes=(K("w"),))
            red("dve", bm[:, 0:nb], wn.rearrange("p (n b) -> p n b", b=32),
                ALU.max, absval=True,
                reads=(K("w"),), writes=(K("bm"),))
            ts("dve", bm[:, 0:nb], bm[:, 0:nb], 1.0 / 7.0, 1e-12,
               ALU.mult, ALU.max,
               reads=(K("bm"),), writes=(K("sw"),))
            p.emit("dve", lambda e, o=bm[:, nb:2 * nb], i=bm[:, 0:nb]:
                   e.reciprocal(o, i),
                   reads=(K("sw"),), writes=(K("inv"),))
            tt("pool", q3.rearrange("p (n b) -> p n b", b=32),
               wn.rearrange("p (n b) -> p n b", b=32),
               bm[:, nb:2 * nb].rearrange("p n -> p n ()")
               .broadcast_to([128, nb, 32]), ALU.mult,
               reads=(K("w"), K("inv")), writes=(K("q"),))
            cp("dve", qiw, q3, reads=(K("q"),), writes=(K("qi"),))
            tt("dve", st_.rearrange("p (n b) -> p n b", b=32),
               qiw.rearrange("p (n b) -> p n b", b=32),
               bm[:, 0:nb].rearrange("p n -> p n ()")
               .broadcast_to([128, nb, 32]), ALU.mult,
               reads=(K("qi"), K("sw")), writes=(K("st"),))
            dma(wq2.rearrange("(o p) i -> o p i", p=128)[oc, :,
                ic * DQI:(ic + 1) * DQI], st_,
                reads=(K("st"),), writes=(f"wq{tag}:{oc}:{ic}",))
            p.state[K("done")] = [[
                p.state[f"wq{tag}:{oc}:{ic}"][0],
                p.state[K("st")][0]], []]
            yield
        p.merge_deps(f"wq{tag}:all",
                     [f"wq{tag}:{oc}:{ic}" for oc in range(O // 128)
                      for ic in range(I // DQI)])

    gen_g = dequant_gen("g", wg, wq_g, FI, D)
    gen_u = dequant_gen("u", wu, wq_u, FI, D)
    gen_d = dequant_gen("d", wd, wq_d, D, FI)

    def pump(gg, n=1):
        for _ in range(n):
            if next(gg, "END") == "END":
                break

    ar.free(wbuf)
    ab2 = [ar.alloc(f"ab2_{i}", 2 * TH * 2) for i in range(3)]
    numb_ = [ar.alloc(f"numb{i}", (TH + 16) * 4) for i in range(2)]
    denb_ = [ar.alloc(f"denb{i}", (TH + 16) * 4) for i in range(2)]
    carp = ar.alloc("carp", 2 * CT * 4)
    outst = ar.alloc("outst", CT * TH * 2)
    out_v = outst.view(BF16, [CT, TH])
    scr = ar.alloc("scr", 6 * TH * 4)

    def scrv(i):
        return scr.view(F32, [6, TH])[:, i, :]

    def plk(i):
        return scr.key(f"pl{i}")

    wv_ = sm("w")
    euv = sm("eu")

    def ld_ab(ct, phase):
        ib = ct % 3
        v = ab2[ib].view(BF16, [2, TH])
        prior = ab2[ib].key(f"used:{phase}:{ct - 3}") if ct >= 3 else (
            ab2[ib].key(f"used:p1:{max(c for c in range(CT) if c % 3 == ib)}")
            if phase == "p2" else None)
        rd = ("a_dram:all", "b_dram:all") + ((prior,) if prior else ())
        first = (phase == "p1" and ct < 3)
        dma(v[:, 0, :], a_dram[ct], reads=rd,
            writes=ab2[ib].wfirst(f"a:{ct}") if first
            else (ab2[ib].key(f"a:{ct}"),))
        dma(v[:, 1, :], b_dram[ct], reads=rd,
            writes=ab2[ib].wfirst(f"b:{ct}") if first
            else (ab2[ib].key(f"b:{ct}"),))
        return v, ab2[ib].key(f"a:{ct}"), ab2[ib].key(f"b:{ct}")

    cv = carp.view(F32, [2 * CT])
    for ct in range(CT):
        v, ak, bk = ld_ab(ct, "p1")
        numb, denb = numb_[ct % 2], denb_[ct % 2]
        nv = numb.view(F32, [TH + 16])
        dv = denb.view(F32, [TH + 16])
        scan("dve", nv[:, 16:16 + TH],
             wv_[:, ct:ct + 1].broadcast_to([128, TH]), v[:, 0, :], 0.0,
             reads=(ak, small.key("w")),
             writes=numb.wfirst("s") if ct == 0 else (numb.key("s"),))
        scan("dve", dv[:, 16:16 + TH],
             wv_[:, ct:ct + 1].broadcast_to([128, TH]), v[:, 1, :], 0.0,
             reads=(bk, small.key("w")),
             writes=denb.wfirst("s") if ct == 0 else (denb.key("s"),))
        cp("dve", cv[:, ct:ct + 1], nv[:, 16 + TH - 1:16 + TH],
           reads=(numb.key("s"),),
           writes=carp.wfirst(f"n{ct}") if ct == 0 else (carp.key(f"n{ct}"),))
        cp("pool", cv[:, CT + ct:CT + ct + 1], dv[:, 16 + TH - 1:16 + TH],
           reads=(denb.key("s"),), writes=(carp.key(f"d{ct}"),))
        p.state[ab2[ct % 3].key(f"used:p1:{ct}")] = [[
            p.state[numb.key("s")][0], p.state[denb.key("s")][0]], []]

    CARPK = tuple(carp.key(f"{sd}{c}") for sd in "nd" for c in range(CT))
    if debug:
        dma(dbg_sc[1, :, 32:48], cv, reads=CARPK, writes=("dbgsc1c",))
    dma(cc_in["carry"].rearrange("o (p c) -> (o p) c", p=128),
        cv, reads=CARPK, writes=("ccin:carry",))
    coll("carry", reads=())
    gsb = None
    gsb_v = scr.view(F32, [6, TH])[:8, 4, :]
    dma(gsb_v, cc_out["carry"], reads=("ccout:carry",),
        writes=scr.wfirst("pl4"))
    cirow = None
    cirow_v = scr.view(F32, [6, TH])[:1, 5, :]
    for j in range(4):
        mm(ps[4].ap()[:1, :], selv_sb, gsb_v[:, j * 512:(j + 1) * 512],
           start=True, stop=True,
           reads=(consts.key("selv"), plk(4)), writes=(PSK[4],),
           defer=False)
        cp("dve", cirow_v[:, j * 512:(j + 1) * 512], ps[4].ap()[:1, :],
           reads=(PSK[4],),
           writes=(plk(5),) if j else scr.wfirst("pl5"))
    dma(carry_red[None, :], cirow_v, reads=(plk(5),),
        writes=("carry_red",))
    cin2 = ar.alloc("cin2", 2 * CT * 4)
    cin2_v = cin2.view(F32, [2 * CT])
    dma(cin2_v, carry_red.rearrange("(p c) -> p c", p=128),
        reads=("carry_red",), writes=cin2.wfirst("c"))

    stat2 = ar.alloc("stat2", 2 * CT * 4)
    st2v = stat2.view(F32, [2 * CT])

    def a2_T0(ct):
        v, ak, bk = ld_ab(ct, "p2")
        numb, denb = numb_[ct % 2], denb_[ct % 2]
        nv = numb.view(F32, [TH + 16])
        dv = denb.view(F32, [TH + 16])
        s = ct % 2
        cp("dve", nv[:, 15:16], cin2_v[:, ct:ct + 1],
           reads=(cin2.key("c"),), writes=(numb.key("seed"),))
        cp("pool", dv[:, 15:16], cin2_v[:, CT + ct:CT + ct + 1],
           reads=(cin2.key("c"),), writes=(denb.key("seed"),))
        scan("dve", nv[:, 16:16 + TH],
             wv_[:, ct:ct + 1].broadcast_to([128, TH]), v[:, 0, :],
             cin2_v[:, ct:ct + 1],
             reads=(ak, small.key("w"), cin2.key("c")),
             writes=(numb.key("s"),))
        scan("dve", dv[:, 16:16 + TH],
             wv_[:, ct:ct + 1].broadcast_to([128, TH]), v[:, 1, :],
             cin2_v[:, CT + ct:CT + ct + 1],
             reads=(bk, small.key("w"), cin2.key("c")),
             writes=(denb.key("s"),))
        stt("dve", scrv(3 * s), v[:, 0, :], euv[:, ct:ct + 1],
            nv[:, 15:15 + TH], ALU.mult, ALU.add,
            reads=(ak, small.key("eu"), numb.key("s"), numb.key("seed")),
            writes=scr.wfirst(f"pl{3 * s}") if ct < 2 else (plk(3 * s),))
        ts("pool", scrv(3 * s + 1), v[:, 1, :], euv[:, ct:ct + 1], 1e-9,
           ALU.mult, ALU.add,
           reads=(bk, small.key("eu")),
           writes=(plk(3 * s + 1),) if ct >= 2
           else scr.wfirst(f"pl{3 * s + 1}"))
        tt("pool", scrv(3 * s + 1), scrv(3 * s + 1), dv[:, 15:15 + TH],
           ALU.add,
           reads=(plk(3 * s + 1), denb.key("s"), denb.key("seed")),
           writes=(plk(3 * s + 1),))
        p.state[ab2[ct % 3].key(f"used:p2:{ct}")] = [[
            p.state[plk(3 * s)][0], p.state[plk(3 * s + 1)][0]], []]

    def a2_T1(ct):
        s = ct % 2
        act("act", scrv(3 * s + 2), scrv(3 * s + 1), AF.Ln,
            reads=(plk(3 * s + 1),),
            writes=(plk(3 * s + 2),) if ct >= 2
            else scr.wfirst(f"pl{3 * s + 2}"))
        act("act", scrv(3 * s + 2), scrv(3 * s + 2), AF.Exp, scale=-1.0,
            reads=(plk(3 * s + 2),), writes=(plk(3 * s + 2),))

    def a2_T2(ct):
        s = ct % 2
        tt("dve", out_v[:, ct, :], scrv(3 * s), scrv(3 * s + 2), ALU.mult,
           reads=(plk(3 * s), plk(3 * s + 2)),
           writes=outst.wfirst(f"wkv{ct}") if ct == 0
           else (outst.key(f"wkv{ct}"),))
        if debug:
            dma(dbg_wkv[ct], out_v[:, ct, :],
                reads=(outst.key(f"wkv{ct}"),), writes=(f"dbgwkv:{ct}",))

    def a2_T3(ct):
        act("act", rst_v[:, ct, :], rst_v[:, ct, :], AF.Sigmoid,
            reads=tuple(rstore.key(f"{ct}:{i}") for i in range(NTT)),
            writes=(rstore.key(f"sig{ct}"),))

    def a2_T4(ct):
        s = ct % 2
        stt("dve", out_v[:, ct, :], out_v[:, ct, :], 1.0, rst_v[:, ct, :],
            ALU.mult, ALU.mult,
            reads=(outst.key(f"wkv{ct}"), rstore.key(f"sig{ct}")),
            writes=(outst.key(f"o{ct}"),),
            accum=st2v[:, ct:ct + 1])
        p.state[stat2.key(f"s{ct}")] = [p.state[outst.key(f"o{ct}")][0], []]
        act("act", scrv(3 * s + 1), out_v[:, ct, :], AF.Square,
            reads=(outst.key(f"o{ct}"),),
            writes=(plk(3 * s + 1),),
            accum=st2v[:, CT + ct:CT + ct + 1])
        p.state[stat2.key(f"q{ct}")] = [p.state[plk(3 * s + 1)][0], []]

    A2S = [a2_T0, a2_T1, a2_T2, a2_T3, a2_T4]
    for t in range(CT + 4):
        for k in (4, 3, 2, 1, 0):
            u = t - k
            if 0 <= u < CT:
                A2S[k](u)
        pump(gen_g, 2)

    if debug:
        for ct in range(CT):
            dma(dbg_out[ct], out_v[:, ct, :],
                reads=(outst.key(f"o{ct}"),), writes=(f"dbgout:{ct}",))
        dma(dbg_sc[0, :, 0:16], st2v,
            reads=tuple(stat2.key(f"{sd}{c}") for sd in "sq"
                        for c in range(CT)),
            writes=("dbgsc0",))
        dma(dbg_sc[1, :, 0:16], cin2_v,
            reads=(cin2.key("c"),), writes=("dbgsc1",))
        dma(dbg_sc[1, :, 16:32], cv,
            reads=CARPK, writes=("dbgsc1b",))
    STK = tuple(stat2.key(f"{sd}{c}") for sd in "sq" for c in range(CT))
    hsum = ar.alloc("hsum", 256)
    hs_v = hsum.view(F32, [2 * CT])[:2, :]
    mm(ps[5].ap()[:2, 0:2 * CT], halfT_sb, st2v, start=True, stop=True,
       reads=(consts.key("halfT"),) + STK, writes=(PSK[5],), defer=False)
    cp("dve", hs_v, ps[5].ap()[:2, 0:2 * CT],
       reads=(PSK[5],), writes=hsum.wfirst("h"))
    dma(cc_in["gnst"].rearrange("o (h c) -> (o h) c", h=2), hs_v,
        reads=(hsum.key("h"),), writes=("ccin:gnst",))
    coll("gnst", reads=())
    gst = ar.alloc("gst", 2048)
    gst_v = gst.view(F32, [32])[:8, :]
    dma(gst_v, cc_out["gnst"], reads=("ccout:gnst",), writes=gst.wfirst("g"))
    tot_v = gst.view(F32, [32], byteoff=128)[:1, :]
    mm(ps[5].ap()[:1, 0:32], selp2_sb, gst_v, start=True, stop=True,
       reads=(consts.key("selp2"), gst.key("g")), writes=(PSK[5],),
       defer=False)
    cp("dve", tot_v, ps[5].ap()[:1, 0:32],
       reads=(PSK[5],), writes=(gst.key("tot"),))
    NTK = float(T * KD)
    tot2 = gst.view(F32, [2, 2, CT], byteoff=128)[:1, :, :, :]
    mu_v = gst.view(F32, [2, CT], byteoff=256)[:1, :, :]
    rs_v = gst.view(F32, [2, CT], byteoff=320)[:1, :, :]
    ts("dve", mu_v, tot2[:, :, 0, :], 1.0 / NTK, None, ALU.mult, None,
       reads=(gst.key("tot"),), writes=(gst.key("mu"),))
    act("act", rs_v, mu_v, AF.Square,
        reads=(gst.key("mu"),), writes=(gst.key("mu2"),))
    stt("dve", rs_v, tot2[:, :, 1, :], 1.0 / NTK, rs_v,
        ALU.mult, ALU.subtract,
        reads=(gst.key("tot"), gst.key("mu2")), writes=(gst.key("var"),))
    act("act", rs_v, rs_v, AF.Ln, bias=epsv[:1, :],
        reads=(gst.key("var"),) + EPSK, writes=(gst.key("lnv"),))
    act("act", rs_v, rs_v, AF.Exp, scale=-0.5,
        reads=(gst.key("lnv"),), writes=(gst.key("rstd"),))
    mrb = gst.view(F32, [2, 2 * CT], byteoff=384)[:2, :, :]
    for hh in range(2):
        dma(cc_in["gnst"][:, hh * 16:(hh + 1) * 16]
            .rearrange("o (w c) -> o w c", w=2),
            gst.view(F32, [2, 2, CT], byteoff=256)[:1, :, hh, :],
            reads=(gst.key("mu"), gst.key("rstd")),
            writes=("hb_dram", "ccin:gnst") if hh == 0
            else ("hb_dram2", "ccin:gnst"))
    dma(gst.view(F32, [2 * CT], byteoff=384)[:2, :],
        cc_in["gnst"].rearrange("o (h c2) -> (o h) c2", h=2),
        reads=("hb_dram", "hb_dram2"), writes=(gst.key("mrb"),))
    mm(ps[5].ap()[:, 0:2 * CT], halfB_sb,
       gst.view(F32, [2 * CT], byteoff=384)[:2, :],
       start=True, stop=True,
       reads=(consts.key("halfB"), gst.key("mrb")), writes=(PSK[5],),
       defer=False)
    murs = ar.alloc("murs", 2 * CT * 4)
    murs_v = murs.view(F32, [2, CT])
    cp("dve", murs.view(F32, [2 * CT]), ps[5].ap()[:, 0:2 * CT],
       reads=(PSK[5],), writes=murs.wfirst("m"))
    tt("dve", sm("A"), murs_v[:, 1, :], sm("gnw"), ALU.mult,
       reads=(murs.key("m"), small.key("gnw")), writes=(small.key("A"),))
    tt("dve", sm("t1"), murs_v[:, 0, :], sm("A"), ALU.mult,
       reads=(murs.key("m"), small.key("A")), writes=(small.key("muA"),))
    tt("dve", sm("Bb"), sm("gnb"), sm("t1"), ALU.subtract,
       reads=(small.key("gnb"), small.key("muA")), writes=(small.key("Bb"),))
    if debug:
        dma(dbg_sc[2, :, 0:CT * 2], murs.view(F32, [2 * CT]),
            reads=(murs.key("m"),), writes=("dbgsc2",))
        dma(dbg_sc[3, :, 0:CT], sm("A"), reads=(small.key("A"),),
            writes=("dbgsc3",))
        dma(dbg_sc[3, :, CT:2 * CT], sm("Bb"), reads=(small.key("Bb"),),
            writes=("dbgsc4",))
    for ct in range(CT):
        ts("dve", out_v[:, ct, :], out_v[:, ct, :],
           sm("A")[:, ct:ct + 1], sm("Bb")[:, ct:ct + 1], ALU.mult, ALU.add,
           reads=(outst.key(f"o{ct}"), small.key("A"), small.key("Bb")),
           writes=(outst.key(f"y{ct}"),))

    for bb in (ab2[0], ab2[1], ab2[2], numb_[0], numb_[1], denb_[0],
               denb_[1], carp, cin2, hsum, gst, murs, stat2,
               rstore):
        ar.free(bb)
    wob = ar.alloc("wob", CT * D * 2)
    wob_v = wob.view(BF16, [CT, D])
    for hlf in range(2):
        sv = scr.view(F32, [4, 1024])
        rd = () if hlf == 0 else (wob.key("wo:done0"),)
        dma(sv, woT.rearrange("(c p) d -> p c d", p=128)[:, hlf * 4:
                                                         (hlf + 1) * 4, :],
            reads=rd, writes=(plk(0), plk(1)))
        for c in range(4):
            cp("dve" if c % 2 else "pool", wob_v[:, hlf * 4 + c, :],
               sv[:, c, :],
               reads=(plk(0), plk(1)),
               writes=wob.wfirst(f"wo:{hlf * 4 + c}") if hlf == 0 and c == 0
               else (wob.key(f"wo:{hlf * 4 + c}"),))
        p.state[wob.key(f"wo:done{hlf}")] = [[
            p.state[wob.key(f"wo:{hlf * 4 + c}")][0] for c in range(4)], []]

    attst = ar.alloc("attst", 2 * CT * TT * 2)
    for ti in range(NTT):
        sb = ti % 2
        att_sv = attst.view(BF16, [2, CT, TT])[:, sb]
        for co in range(CT):
            bank = co % 8
            pk = PSK[bank]
            for k in range(CT):
                mm(ps[bank].ap(), wob_v[:, k, co * 128:(co + 1) * 128],
                   out_v[:, k, ti * TT:(ti + 1) * TT],
                   start=(k == 0), stop=(k == CT - 1),
                   reads=(wob.key(f"wo:{k}"), outst.key(f"y{k}")),
                   writes=(pk,))
            act("act", att_sv[:, co, :], ps[bank].ap(), AF.Identity,
                reads=(pk,),
                writes=attst.wfirst(f"{sb}:{co}") if ti < 2 and co == 0
                else (attst.key(f"{sb}:{co}"),))
            pump(gen_g, 1)
        dma(att_dram[:, ti * TT:(ti + 1) * TT]
            .rearrange("(c p) t -> p c t", p=128),
            att_sv,
            reads=tuple(attst.key(f"{sb}:{co}") for co in range(CT)),
            writes=(f"att_dram:{ti}",))
        for co in range(CT):
            p.state[attst.key(f"{sb}:{co}")][1].append(
                p.state[f"att_dram:{ti}"][0])
    p.merge_deps("att_dram:all", [f"att_dram:{i}" for i in range(NTT)])

    for bb in (outst, attst, scr, wob):
        ar.free(bb)

    NG = 4
    xres = [ar.alloc(f"xres{s}", 4 * 1024 * 4 + 4 * 1024 * 2) for s in range(2)]
    t1st = ar.alloc("t1st", 16 * 1024 * 2)
    t1_v = t1st.view(BF16, [16, 1024])
    mxacc = ar.alloc("mxacc", 256)
    sqs2 = [ar.alloc(f"sqs2{s}", 1024 * 4) for s in range(2)]
    t1f = [ar.alloc(f"t1f{s}", 1024 * 4) for s in range(2)]
    stats2 = ar.alloc("stats2", 512)

    for g in range(NG):
        s = g % 2
        res_g = xres[s].view(F32, [4, 1024], byteoff=0)
        att_g = xres[s].view(BF16, [4, 1024], byteoff=16384)
        sv = stats2.view(F32, [2, 4, 8])[:, s, :, :]
        K = lambda nm: xres[s].key(nm)
        KS = lambda nm: stats2.key(f"{nm}{s}")
        rd = (K("cons"),) if g >= 2 else ()
        dma(res_g, xs[1 + g * TT:1 + (g + 1) * TT, :]
            .rearrange("(a p) d -> p a d", p=128),
            reads=rd, writes=(K("x"),))
        for pt in range(4):
            i = g * 4 + pt
            dmaT(att_g[:, pt, :], att_dram[:, i * 128:(i + 1) * 128],
                 reads=("att_dram:all",) + rd,
                 writes=(K(f"a{pt}"),))
        tt("dve", res_g, res_g, att_g, ALU.add,
           reads=(K("x"), K("a0"), K("a1"), K("a2"), K("a3")),
           writes=(K("r"),))
        dma(res_dram[g * TT:(g + 1) * TT, :]
            .rearrange("(a p) d -> p a d", p=128), res_g,
            reads=(K("r"),), writes=(f"res_dram:{g}",))
        red("dve", sv[:, :, 0], res_g, ALU.add,
            reads=(K("r"),), writes=(KS("s"),))
        for pt in range(4):
            act("act", sqs2[s].view(F32, [1024]), res_g[:, pt, :], AF.Square,
                reads=(K("r"),), writes=(sqs2[s].key("w"),),
                accum=sv[:, pt, 1:2])
        p.state[KS("q")] = [p.state[sqs2[s].key("w")][0], []]
        ts("dve", sv[:, :, 2], sv[:, :, 0], 1.0 / D, None, ALU.mult, None,
           reads=(KS("s"),), writes=(KS("m"),))
        act("act", sv[:, :, 3], sv[:, :, 2], AF.Square,
            reads=(KS("m"),), writes=(KS("mm"),))
        stt("dve", sv[:, :, 1], sv[:, :, 1], 1.0 / D, sv[:, :, 3],
            ALU.mult, ALU.subtract,
            reads=(KS("q"), KS("mm")), writes=(KS("v"),))
        act("act", sv[:, :, 1], sv[:, :, 1], AF.Ln, bias=epsv,
            reads=(KS("v"),) + EPSK, writes=(KS("lv"),))
        act("act", sv[:, :, 1], sv[:, :, 1], AF.Exp, scale=-0.5,
            reads=(KS("lv"),), writes=(KS("r"),))
        cons = []
        for pt in range(4):
            i = g * 4 + pt
            ts("dve", t1f[s].view(F32, [1024]), res_g[:, pt, :],
               sv[:, pt, 2:3], sv[:, pt, 1:2], ALU.subtract, ALU.mult,
               reads=(K("r"), KS("m"), KS("r")),
               writes=(t1f[s].key("w"),))
            red("dve", mxacc.view(F32, [16])[:, i:i + 1],
                t1f[s].view(F32, [1024]), ALU.max, absval=True,
                reads=(t1f[s].key("w"),), writes=(mxacc.key(f"{i}"),))
            act("act", t1_v[:, i, :], t1f[s].view(F32, [1024]), AF.Identity,
                reads=(t1f[s].key("w"),), writes=(t1st.key(f"{i}"),))
            cons.append(p.state[t1st.key(f"{i}")][0])
            cons.append(p.state[mxacc.key(f"{i}")][0])
        cons.append(p.state[f"res_dram:{g}"][0])
        p.state[K("cons")] = [cons, []]
        pump(gen_u, 4)
    p.merge_deps("res_dram:all", [f"res_dram:{g}" for g in range(NG)])
    for bb in (sqs2[0], sqs2[1], t1f[0], t1f[1], stats2):
        ar.free(bb)

    def global_scalar_max(nm, lane_ap, lane_keys, s_slot):
        mr = mxrow.view(F32, [128])[:1, :]
        m8 = mxrow.view(F32, [8], byteoff=512)[:1, :]
        dma(carry_red[None, 0:128].rearrange("o (p c) -> (o p) c", p=128),
            lane_ap, reads=lane_keys, writes=("carry_red", mxrow.key("b")))
        dma(mr, carry_red[None, 0:128],
            reads=(mxrow.key("b"),), writes=(mxrow.key("l"),))
        p.emit("dve", lambda e, o=m8, i=mr: e.max(o, i),
               reads=(mxrow.key("l"),), writes=(mxrow.key("m8"),))
        dma(cc_in[nm][:, 0:4], m8[:, 0:4],
            reads=(mxrow.key("m8"),), writes=(f"ccin:{nm}",))
        coll(nm, reads=())
        gv = mxrow.view(F32, [32], byteoff=576)[:1, :]
        dma(gv, cc_out[nm], reads=(f"ccout:{nm}",), writes=(mxrow.key("g"),))
        gvr = gv.rearrange("p (r c) -> p r c", c=4)[:, :, 0]
        p.emit("dve", lambda e, o=m8, i=gvr: e.max(o, i),
               reads=(mxrow.key("g"),), writes=(mxrow.key("gm"),))
        sloc = mxrow.view(F32, [4], byteoff=704)[:1, 0:1]
        iloc = mxrow.view(F32, [4], byteoff=704)[:1, 1:2]
        ts("dve", sloc, m8[:, 0:1], 1.0 / QMAX, 1e-12, ALU.mult, ALU.max,
           reads=(mxrow.key("gm"),), writes=(mxrow.key("s"),))
        p.emit("dve", lambda e, o=iloc, i=sloc: e.reciprocal(o, i),
               reads=(mxrow.key("s"),), writes=(mxrow.key("i"),))
        mm(ps[6].ap()[:, 0:2], ones_sb, mxrow.view(F32, [4], byteoff=704)[:1, 0:2],
           start=True, stop=True,
           reads=(consts.key("ones"), mxrow.key("s"), mxrow.key("i")),
           writes=(PSK[6],), defer=False)
        cp("dve", scal.view(F32, [256])[:, s_slot:s_slot + 2],
           ps[6].ap()[:, 0:2],
           reads=(PSK[6],), writes=(scal.key(nm),))
        return scal.key(nm)

    mx1 = ar.alloc("mx1", 64)
    p.emit("dve", lambda e, o=mx1.view(F32, [8])[:, 0:8],
           i=mxacc.view(F32, [16]): e.max(o, i),
           reads=tuple(mxacc.key(f"{i}") for i in range(16)),
           writes=mx1.wfirst("m"))
    kx = global_scalar_max("mx0", mx1.view(F32, [8])[:, 0:1],
                           (mx1.key("m"),), 0)
    SX, IX = 0, 1

    qscr = [ar.alloc(f"qscr{s}", 4 * 1024 * 4 * 2) for s in range(2)]
    xqt = [ar.alloc(f"xqt{s}", 4 * 1024 * 2) for s in range(2)]
    for g in range(NG):
        s = g % 2
        qf = qscr[s].view(F32, [4, 1024], byteoff=0)
        qi = qscr[s].view(I32, [4, 1024], byteoff=16384)
        xq_tv = xqt[s].view(BF16, [4, 1024])
        t1g = t1st.view(BF16, [16, 1024])[:, g * 4:(g + 1) * 4, :]
        rd = tuple(t1st.key(f"{g * 4 + pt}") for pt in range(4)) + (kx,)
        if g >= 2:
            rd = rd + (xqt[s].key("cons"),)
        ts("dve", qf, t1g, scv(IX), None, ALU.mult, None,
           reads=rd, writes=(qscr[s].key("f"),))
        cp("dve", qi, qf, reads=(qscr[s].key("f"),),
           writes=(qscr[s].key("i"),))
        cp("dve", xq_tv, qi, reads=(qscr[s].key("i"),),
           writes=(xqt[s].key("v"),))
        dma(xq_dram[g * TT:(g + 1) * TT, :]
            .rearrange("(a p) d -> p a d", p=128), xq_tv,
            reads=(xqt[s].key("v"),), writes=(f"xq_dram:{g}",))
        p.state[xqt[s].key("cons")] = [[p.state[f"xq_dram:{g}"][0]], []]
        pump(gen_u, 2)
    p.merge_deps("xq_dram:all", [f"xq_dram:{g}" for g in range(NG)])
    for bb in (t1st, mxacc, mx1, qscr[0], qscr[1], xqt[0], xqt[1],
               xres[0], xres[1], stats):
        ar.free(bb)

    lmx = ar.alloc("lmx", 128)
    gkabs = ar.alloc("gkabs", 64)
    xqf = ar.alloc("xqf", CT * TH * 2)
    xqf_v = xqf.view(BF16, [CT, TH])
    for c in range(CT):
        dmaT(xqf_v[:, c, :], xq_dram[:, c * 128:(c + 1) * 128],
             reads=("xq_dram:all",),
             writes=xqf.wfirst(f"{c}") if c == 0 else (xqf.key(f"{c}"),))
    XQK = [xqf.key(f"{c}") for c in range(CT)]

    gatest_ = [ar.alloc(f"gatest{i}", (FOT // 2) * TH * 2) for i in range(2)]

    class _GV:
        def __getitem__(self, idx):
            _, ot, sl = idx
            return gatest_[ot // 12].view(BF16, [FOT // 2, TH])[:, ot % 12, sl]

    gate_v = _GV()

    class _GK:
        def key(self, sub):
            ot = int(sub.split(":")[1]) if ":" in sub else int(sub)
            return gatest_[ot // 12].key(sub)
        def wfirst(self, sub):
            ot = int(sub.split(":")[1]) if ":" in sub else int(sub)
            return gatest_[ot // 12].wfirst(sub)

    gatest = _GK()
    FH = FI // 2
    wlhs = ar.alloc("wlhs", CT * FH * 2)
    wlhs_v = wlhs.view(BF16, [CT, FH])
    lmx_f = lmx.view(F32, [8])
    lmx_m8 = lmx.view(BF16, [8], byteoff=32)

    p.emit("dve", lambda e, o=lmx_f: e.memset(o, 0.0),
           reads=(), writes=lmx.wfirst("g0") + tuple(
               lmx.key(f"{nm}0") for nm in ("u", "p2", "p3")))

    def absmax8_lane(src_ap, src_key, lane_i, lane_nm, slot=0, eng="dve"):
        lane_key = lmx.key(f"{lane_nm}0")
        si = lane_i * 2 + (slot % 2)
        sc = gkabs.view(F32, [8])[:, si:si + 1]
        skey = gkabs.key(f"s{si}")
        red(eng, sc, src_ap, ALU.max, absval=True,
            reads=(src_key,), writes=(skey,))
        li = lane_i * 2
        tt("dve", lmx_f[:, li:li + 1], lmx_f[:, li:li + 1], sc, ALU.max,
           reads=(skey, lane_key), writes=(lane_key,))

    def lane_merged(lane_i, lane_nm):
        return lmx_f[:, lane_i * 2:lane_i * 2 + 1], (lmx.key(f"{lane_nm}0"),)

    pump(gen_g, 99)
    for hf in range(2):
        for c in range(CT):
            dmaT(wlhs_v[:, c, :], wq_g[hf * FH:(hf + 1) * FH,
                                       c * 128:(c + 1) * 128],
                 reads=("wqg:all",) + ((wlhs.key(f"mmg{hf - 1}"),)
                                       if hf else ()),
                 writes=(wlhs.key(f"g{c}"),))
        for ot in range(hf * 12, hf * 12 + 12):
            lo = (ot - hf * 12) * 128
            for k in range(CT):
                for t2 in range(NTT):
                    bank = t2 + (ot % 2) * 4
                    mm(ps[bank].ap(), wlhs_v[:, k, lo:lo + 128],
                       xqf_v[:, k, t2 * TT:(t2 + 1) * TT],
                       start=(k == 0), stop=(k == CT - 1),
                       reads=(wlhs.key(f"g{k}"), XQK[k]), writes=(PSK[bank],))
            for t2 in range(NTT):
                bank = t2 + (ot % 2) * 4
                act("act", gate_v[:, ot, t2 * TT:(t2 + 1) * TT],
                    ps[bank].ap(), AF.Identity, scale=scv(SX),
                    reads=(PSK[bank], kx),
                    writes=gatest.wfirst(f"{ot}") if ot == 0 and t2 == 0
                    else (gatest.key(f"{ot}"),))
            absmax8_lane(gate_v[:, ot, :], gatest.key(f"{ot}"), 0, "g",
                         ot % 2)
            pump(gen_u)
        p.state[wlhs.key(f"mmg{hf}")] = [[
            p.state[wlhs.key(f"g{c}")][0] for c in range(CT)], []]

    pump(gen_u, 99)
    gap, gk_ = lane_merged(0, "g")
    kg = global_scalar_max("mx1", gap, gk_, 2)
    SG, IG, SU, IU = 2, 3, 4, 5
    SGC = 8
    ts("dve", scv(SGC), scv(SG), 1.0 / 65536.0, None, ALU.mult, None,
       reads=(kg,), writes=(scal.key("sgc"),))

    QT = TH // 4
    NU = FOT * 4
    cscr = ar.alloc("cscr", 12 * QT * 4)
    cpl = cscr.view(F32, [12, QT])

    def cpf(s, i):
        return cpl[:, s * 3 + i, :]

    def cpiv(s, i):
        return cscr.view(I32, [12, QT])[:, s * 3 + i, :]

    def gvh(ot, q):
        return gate_v[:, ot, q * QT:(q + 1) * QT]

    def g_A(u):
        ot, q = divmod(u, 4)
        s = u % 4
        P = lambda n: cscr.key(f"P{s}{n}")
        ts("dve", cpiv(s, 0), gvh(ot, q), scv(IG), None, ALU.mult, None,
           reads=(gatest.key(f"{ot}"), kg),
           writes=(P(0),) if u >= 4 else cscr.wfirst(f"P{s}0"))

    def g_B(u):
        ot, q = divmod(u, 4)
        s = u % 4
        P = lambda n: cscr.key(f"P{s}{n}")
        act("act", cpf(s, 1), cpiv(s, 0), AF.Sigmoid, scale=scv(SG),
            reads=(P(0), kg),
            writes=(P(1),) if u >= 4 else cscr.wfirst(f"P{s}1"))

    def g_C(u):
        ot, q = divmod(u, 4)
        s = u % 4
        P = lambda n: cscr.key(f"P{s}{n}")
        ts("dve", cpiv(s, 2), cpf(s, 1), 65536.0, 65535.0, ALU.mult, ALU.min,
           reads=(P(1),), writes=(P(2),) if u >= 4 else cscr.wfirst(f"P{s}2"))
        stt("dve", gvh(ot, q), cpiv(s, 2), scv(SGC), cpiv(s, 0),
            ALU.mult, ALU.mult,
            reads=(P(2), P(0), scal.key("sgc"), gatest.key(f"{ot}")),
            writes=(gatest.key(f"p2:{ot}:{q}"),))
        absmax8_lane(gvh(ot, q), gatest.key(f"p2:{ot}:{q}"), 2, "p2", u)

    def chain_gen(stages, nu):
        ns = len(stages)
        for t in range(nu + 2 * (ns - 1)):
            for k in range(ns - 1, -1, -1):
                u = t - 2 * k
                if 0 <= u < nu:
                    stages[k](u)
            yield t

    gen_gc = chain_gen([g_A, g_B, g_C], NU)

    upst = ar.alloc("upst", 2 * TH * 2)
    up_sv2 = upst.view(BF16, [2, TH])
    for hf in range(2):
        for c in range(CT):
            dmaT(wlhs_v[:, c, :], wq_u[hf * FH:(hf + 1) * FH,
                                       c * 128:(c + 1) * 128],
                 reads=("wqu:all",) + ((wlhs.key(f"mmu{hf - 1}"),)
                                       if hf else (wlhs.key("mmg1"),)),
                 writes=(wlhs.key(f"u{c}"), wlhs.key(f"g{c}")))
        for ot in range(hf * 12, hf * 12 + 12):
            lo = (ot - hf * 12) * 128
            for k in range(CT):
                for t2 in range(NTT):
                    bank = t2 + (ot % 2) * 4
                    mm(ps[bank].ap(), wlhs_v[:, k, lo:lo + 128],
                       xqf_v[:, k, t2 * TT:(t2 + 1) * TT],
                       start=(k == 0), stop=(k == CT - 1),
                       reads=(wlhs.key(f"u{k}"), XQK[k]), writes=(PSK[bank],))
            sb = ot % 2
            up_sv = up_sv2[:, sb, :]
            for t2 in range(NTT):
                bank = t2 + (ot % 2) * 4
                act("act", up_sv[:, t2 * TT:(t2 + 1) * TT],
                    ps[bank].ap(), AF.Identity, scale=scv(SX),
                    reads=(PSK[bank], kx),
                    writes=upst.wfirst(f"v{sb}") if ot < 2 and t2 == 0
                    else (upst.key(f"v{sb}"),))
            absmax8_lane(up_sv, upst.key(f"v{sb}"), 1, "u", ot % 2)
            pump(gen_d, 2)
            pump(gen_gc, 4)
            dma(up_dram[ot], up_sv, reads=(upst.key(f"v{sb}"),),
                writes=(f"up_dram:{ot}",))
            p.state[upst.key(f"v{sb}")][1].append(
                p.state[f"up_dram:{ot}"][0])
        p.state[wlhs.key(f"mmu{hf}")] = [[
            p.state[wlhs.key(f"u{c}")][0] for c in range(CT)], []]
    p.merge_deps("up_dram:all", [f"up_dram:{ot}" for ot in range(FOT)])
    pump(gen_d, 99)
    pump(gen_gc, 99)

    uap, uk_ = lane_merged(1, "u")
    ku = global_scalar_max("mx2", uap, uk_, 4)

    for bb in (xqf, upst, wlhs, deq):
        ar.free(bb)

    p2ap, p2k_ = lane_merged(2, "p2")
    kg2 = global_scalar_max("mx3", p2ap, p2k_, 6)
    SG2, IG2 = 6, 7
    SGU = 12
    tt("dve", scv(SGU), scv(SG2), scv(SU), ALU.mult,
       reads=(kg2, ku), writes=(scal.key("sgu"),))

    upld = ar.alloc("upld", 6 * TH * 2)
    upld_v = upld.view(BF16, [6, TH])
    wdl = [ar.alloc(f"wdl{i}", (FOT // 4) * D * 2) for i in range(4)]

    def wdl_v(c):
        return wdl[c // 6].view(BF16, [FOT // 4, D])[:, c % 6, :]

    def wdl_key(c):
        return wdl[c // 6].key(f"{c}")

    for c in range(FOT):
        dmaT(wdl_v(c), wq_d[:, c * 128:(c + 1) * 128],
             reads=("wqd:all",),
             writes=wdl[c // 6].wfirst(f"{c}") if c % 6 == 0
             else (wdl_key(c),),
             eng="sp" if c % 2 else "act")


    def o_A(u):
        ot, q = divmod(u, 4)
        s = u % 4
        P = lambda n: cscr.key(f"P{s}{n}")
        if q == 0:
            ib = ot % 6
            dma(upld_v[:, ib, :], up_dram[ot], reads=("up_dram:all",),
                writes=(upld.key(f"s{ib}"),) if ot >= 6
                else upld.wfirst(f"s{ib}"))
        ts("dve", cpiv(s, 0), gvh(ot, q), scv(IG2), None, ALU.mult, None,
           reads=(gatest.key(f"p2:{ot}:{q}"), kg2), writes=(P(0),))

    def o_B(u):
        ot, q = divmod(u, 4)
        s = u % 4
        P = lambda n: cscr.key(f"P{s}{n}")
        ts("dve", cpiv(s, 2), upld_v[:, ot % 6, q * QT:(q + 1) * QT],
           scv(IU), None, ALU.mult, None,
           reads=(upld.key(f"s{ot % 6}"), ku), writes=(P(2),))

    def o_C(u):
        ot, q = divmod(u, 4)
        s = u % 4
        P = lambda n: cscr.key(f"P{s}{n}")
        stt("dve", gvh(ot, q), cpiv(s, 0), scv(SGU), cpiv(s, 2),
            ALU.mult, ALU.mult,
            reads=(P(0), P(2), scal.key("sgu"),
                   gatest.key(f"p2:{ot}:{q}")),
            writes=(gatest.key(f"p3:{ot}:{q}"),))
        absmax8_lane(gvh(ot, q), gatest.key(f"p3:{ot}:{q}"), 3, "p3", u)

    for _ in chain_gen([o_A, o_B, o_C], NU):
        pass

    p3ap, p3k_ = lane_merged(3, "p3")
    ko = global_scalar_max("mx4", p3ap, p3k_, 10)
    SO, IO = 10, 11

    for bb in (upld, gkabs):
        ar.free(bb)

    if debug:
        dma(dbg_sc[1, :, 48:64], scal.view(F32, [256])[:, 0:16],
            reads=(kx, kg, ku, kg2, ko, scal.key("sgc")), writes=("dbgscal",))

    def q_s0(u):
        ot, q = divmod(u, 4)
        s = u % 4
        P = lambda n: cscr.key(f"P{s}{n}")
        ts("dve", cpiv(s, 0), gvh(ot, q), scv(IO), None, ALU.mult, None,
           reads=(gatest.key(f"p3:{ot}:{q}"), ko), writes=(P(0),))

    def q_s1(u):
        ot, q = divmod(u, 4)
        s = u % 4
        P = lambda n: cscr.key(f"P{s}{n}")
        cp("dve", gvh(ot, q), cpiv(s, 0),
           reads=(P(0), gatest.key(f"p3:{ot}:{q}")),
           writes=(gatest.key(f"oq:{ot}:{q}"),))

    gen_q = chain_gen([q_s0, q_s1], NU)
    qdone = [-1]

    def q_pump_until(kk):
        while qdone[0] < kk:
            t = next(gen_q, None)
            if t is None:
                qdone[0] = FOT
            else:
                qdone[0] = (t - 5) // 4 if t >= 5 else -1

    dstg = ar.alloc("dstg", 2 * CT * TT * 2)
    for t2 in range(NTT):
        dstg_v = dstg.view(BF16, [2, CT, TT])[:, t2 % 2]
        for k in range(FOT):
            if t2 == 0:
                q_pump_until(k)
            rk = (wdl_key(k),) + tuple(
                gatest.key(f"oq:{k}:{q}") for q in range(4))
            for od in range(CT):
                mm(ps[od].ap(), wdl_v(k)[:, od * 128:(od + 1) * 128],
                   gate_v[:, k, t2 * TT:(t2 + 1) * TT],
                   start=(k == 0), stop=(k == FOT - 1),
                   reads=rk, writes=(PSK[od],))
        for od in range(CT):
            act("act", dstg_v[:, od, :], ps[od].ap(), AF.Identity,
                scale=scv(SO),
                reads=(PSK[od], ko),
                writes=dstg.wfirst(f"{t2 % 2}:{od}") if t2 < 2 and od == 0
                else (dstg.key(f"{t2 % 2}:{od}"),))
        dma(ffn_dram[:, t2 * TT:(t2 + 1) * TT]
            .rearrange("(c p) t -> p c t", p=128),
            dstg_v,
            reads=tuple(dstg.key(f"{t2 % 2}:{od}") for od in range(CT)),
            writes=(f"ffn_dram:{t2}",))
        for od in range(CT):
            p.state[dstg.key(f"{t2 % 2}:{od}")][1].append(
                p.state[f"ffn_dram:{t2}"][0])
    p.merge_deps("ffn_dram:all", [f"ffn_dram:{t2}" for t2 in range(NTT)])

    for bb in (gatest_[0], gatest_[1], wdl[0], wdl[1], wdl[2], wdl[3],
               lmx, dstg, cscr):
        ar.free(bb)

    fin = ar.alloc("fin", 1024 * 4 + 1024 * 2 + 1024 * 4)
    fres = fin.view(F32, [1024], byteoff=0)
    fffn = fin.view(BF16, [1024], byteoff=4096)
    fout = fin.view(F32, [1024], byteoff=4096 + 2048)
    YK = []
    for i in range(16):
        rd = (fin.key(f"cons{i - 1}"),) if i else ()
        dma(fres, res_dram[i * 128:(i + 1) * 128, :],
            reads=("res_dram:all",) + rd,
            writes=fin.wfirst(f"x{i}") if i == 0 else (fin.key(f"x{i}"),))
        dmaT(fffn, ffn_dram[:, i * 128:(i + 1) * 128],
             reads=("ffn_dram:all",) + rd,
             writes=(fin.key(f"f{i}"),))
        tt("dve", fout, fres, fffn, ALU.add,
           reads=(fin.key(f"x{i}"), fin.key(f"f{i}")),
           writes=(fin.key(f"o{i}"),))
        dma(yout[i * 128:(i + 1) * 128, :], fout,
            reads=(fin.key(f"o{i}"),), writes=(f"y:{i}",))
        p.state[fin.key(f"cons{i}")] = [[p.state[f"y:{i}"][0]], []]
        YK.append(f"y:{i}")

    p.emit("sp", lambda e: e.nop(), reads=tuple(YK), writes=(), inc=0)

    import contextlib
    semnames = sorted({it[3] for it in p.items}
                      | {s for it in p.items for s in it[1]})
    with contextlib.ExitStack() as stx:
        sems = {s: stx.enter_context(nc.semaphore(s)) for s in semnames}
        block = stx.enter_context(nc.Block())
        p.replay(block, sems)
    build_nc.stats = {"n_items": len(p.items),
                      "arena_peak": ar.peak,
                      "sem_totals": dict(p.tot)}
    return nc


_NC_CACHE = []


def kernel(L0_attn_x, L0_attn_time_mix_r, L0_attn_time_mix_k,
           L0_attn_time_mix_v, L0_attn_time_decay, L0_attn_time_first,
           L0_attn_W_r, L0_attn_W_k, L0_attn_W_v, L0_attn_W_o,
           L0_attn_gn_weight, L0_attn_gn_bias,
           L0_ffn_w_gate, L0_ffn_w_up, L0_ffn_w_down):
    f32 = np.float32
    x = np.ascontiguousarray(np.asarray(L0_attn_x, f32))

    def pc(v):
        return np.ascontiguousarray(
            np.asarray(v, f32).reshape(CT, 128).T)

    tm = np.ascontiguousarray(np.stack(
        [pc(L0_attn_time_mix_r), pc(L0_attn_time_mix_k),
         pc(L0_attn_time_mix_v)], axis=0))
    decay = pc(np.asarray(L0_attn_time_decay, f32).reshape(-1))
    tf = pc(np.asarray(L0_attn_time_first, f32).reshape(-1))
    gnw_ = pc(L0_attn_gn_weight)
    gnb_ = pc(L0_attn_gn_bias)
    wrT_ = np.ascontiguousarray(np.asarray(L0_attn_W_r, f32).T)
    wkT_ = np.ascontiguousarray(np.asarray(L0_attn_W_k, f32).T)
    wvT_ = np.ascontiguousarray(np.asarray(L0_attn_W_v, f32).T)
    woT_ = np.ascontiguousarray(np.asarray(L0_attn_W_o, f32).T)
    wg_ = np.ascontiguousarray(np.asarray(L0_ffn_w_gate, f32))
    wu_ = np.ascontiguousarray(np.asarray(L0_ffn_w_up, f32))
    wd_ = np.ascontiguousarray(np.asarray(L0_ffn_w_down, f32))
    ones128 = np.ones((1, 128), f32)
    halfT_ = np.zeros((128, 2), f32)
    halfT_[:64, 0] = 1.0
    halfT_[64:, 1] = 1.0
    halfB_ = np.ascontiguousarray(halfT_.T)

    in_maps = []
    for c in range(NC):
        b, h = c // 2, c % 2
        prev = (np.zeros((1, D), f32) if h == 0
                else x[b, TH - 1:TH, :])
        xs_ = np.ascontiguousarray(
            np.concatenate([prev, x[b, h * TH:(h + 1) * TH, :]], axis=0))
        selv_ = np.zeros((8, 1), f32)
        if h == 1:
            selv_[c - 1, 0] = 1.0
        selp2_ = np.zeros((8, 1), f32)
        selp2_[2 * b, 0] = 1.0
        selp2_[2 * b + 1, 0] = 1.0
        in_maps.append({
            "xs": xs_, "wrT": wrT_, "wkT": wkT_, "wvT": wvT_, "woT": woT_,
            "tm": tm, "decay": decay, "tfirst": tf,
            "gnw": gnw_, "gnb": gnb_,
            "w_gate": wg_, "w_up": wu_, "w_down": wd_,
            "selv": selv_, "selp2": selp2_, "ones128": ones128,
            "halfT": halfT_, "halfB": halfB_,
        })

    if not _NC_CACHE:
        _NC_CACHE.append(build_nc())
    nc = _NC_CACHE[0]
    res = run_bass_kernel_spmd(nc, in_maps, core_ids=list(range(NC)))
    y = np.empty((B, T, D), f32)
    for c in range(NC):
        b, h = c // 2, c % 2
        y[b, h * TH:(h + 1) * TH, :] = res.results[c]["y"]
    return y


if __name__ == "__main__":
    nc = build_nc()
    print("build ok:", build_nc.stats)


# version 28
# speedup vs baseline: 1.1040x; 1.1040x over previous
import numpy as np
import concourse.bass as bass
import concourse.mybir as mybir
from concourse.bass_utils import run_bass_kernel_spmd

F32 = mybir.dt.float32
BF16 = mybir.dt.bfloat16
I32 = mybir.dt.int32
AF = mybir.ActivationFunctionType
ALU = mybir.AluOpType

B, T, D = 4, 4096, 1024
H, KD = 16, 64
NC = 8
TH = T // 2
TT = 512
NTT = TH // TT
CT = D // 128
FI = 3072
FOT = FI // 128
QMAX = 32767.0
EPS = 1e-5

ENG_ATTR = {"pe": "tensor", "dve": "vector", "act": "scalar",
            "pool": "gpsimd", "sp": "sync"}


NDMASEM = 24


class Prog:
    def __init__(self):
        self.items = []
        self.tot = {}
        self.state = {}
        self.wm = {e: {} for e in ENG_ATTR}
        self.bufghosts = {}
        self.dcur = 0
        self.ncoll = 0

    def emit(self, eng, fn, reads=(), writes=(), inc=1, sem=None, defer=False):
        if inc == 16 and sem is None:
            sem = f"d{self.dcur}"
            self.dcur = (self.dcur + 1) % NDMASEM
        sem = sem or eng
        waits = {}
        if inc == 16 and self.tot.get(sem, 0) > 0:
            waits[sem] = self.tot[sem]

        def need(s, v):
            if s == "pe" and eng == "pe":
                return
            if v > self.wm[eng].get(s, 0):
                waits[s] = max(waits.get(s, 0), v)

        def need_w(w):
            if w is None:
                return
            if isinstance(w, list):
                for r in w:
                    need(*r)
            else:
                need(*w)

        def hazard(k):
            st = self.state.get(k)
            if st:
                need_w(st[0])
                for s, v in st[1]:
                    need(s, v)

        for k in reads:
            st = self.state.get(k)
            if st:
                need_w(st[0])
        for k in writes:
            hazard(k)
            if k.startswith("B:"):
                for g in self.bufghosts.get(k.split(":", 2)[1], ()):
                    hazard(g)
        for s, v in waits.items():
            self.wm[eng][s] = v
        if defer:
            ref = (sem, self.tot.get(sem, 0) + 1)
            self.items.append((eng, waits, fn, sem, 0))
        else:
            self.tot[sem] = self.tot.get(sem, 0) + inc
            ref = (sem, self.tot[sem])
            self.items.append((eng, waits, fn, sem, inc))
        for k in reads:
            self.state.setdefault(k, [None, []])[1].append(ref)
        for k in writes:
            self.state[k] = [ref, []]

    def merge_deps(self, newkey, oldkeys):
        refs = []
        for k in oldkeys:
            st = self.state.pop(k, None)
            if st:
                if st[0] is not None:
                    refs.extend(st[0] if isinstance(st[0], list) else [st[0]])
                refs.extend(st[1])
        self.state[newkey] = [refs, []]

    def reads_of(self, key):
        return self.state.get(key)

    def replay(self, block, sems):
        for ename in ("sp", "pe", "dve", "act", "pool"):
            sub = [it for it in self.items if it[0] == ename]

            def runner(engobj, sub=sub):
                for _, waits, fn, sem, inc in sub:
                    for s in sorted(waits):
                        engobj.wait_ge(sems[s], waits[s])
                    ins = fn(engobj)
                    if inc:
                        ins.then_inc(sems[sem], inc)
            getattr(block, ENG_ATTR[ename])(runner)


class Arena:
    def __init__(self, nc, prog, name, nbytes):
        self.t = nc.alloc_sbuf_tensor(name, [128, nbytes], mybir.dt.int8)
        self.prog = prog
        self.free_list = [(0, nbytes)]
        self.owners = {}
        self.ghosts = []
        self.peak = 0

    def alloc(self, name, nbytes, align=64, end=False):
        nbytes = -(-nbytes // align) * align
        fits = [(sz, i) for i, (off, sz) in enumerate(self.free_list)
                if nbytes <= sz]
        for _, i in sorted(fits, reverse=end):
            off, sz = self.free_list[i]
            if end:
                aoff = off + sz - nbytes
                self.free_list[i:i + 1] = (
                    [(off, sz - nbytes)] if sz > nbytes else [])
            else:
                aoff = off
                self.free_list[i:i + 1] = (
                    [(off + nbytes, sz - nbytes)] if sz > nbytes else [])
            self.owners[name] = (aoff, nbytes)
            self.peak = max(self.peak, max(
                o + s for o, s in self.owners.values()))
            self.prog.bufghosts[name] = self.ghost_keys(aoff, nbytes)
            return Buf(self, name, aoff, nbytes)
        raise MemoryError(f"arena alloc {name}: {nbytes} doesn't fit "
                          f"(free={self.free_list})")

    def free(self, buf):
        off, sz = self.owners.pop(buf.name)
        gk = f"ghost:{buf.name}:{off}"
        pref = "B:" + buf.name + ":"
        self.prog.merge_deps(gk, [k for k in list(self.prog.state)
                                  if isinstance(k, str) and k.startswith(pref)])
        self.ghosts.append((off, sz, gk))
        self.free_list.append((off, sz))
        self.free_list.sort()
        merged = []
        for o, s in self.free_list:
            if merged and merged[-1][0] + merged[-1][1] == o:
                merged[-1] = (merged[-1][0], merged[-1][1] + s)
            else:
                merged.append((o, s))
        self.free_list = merged

    def ghost_keys(self, off, size):
        return tuple(gk for o, s, gk in self.ghosts
                     if o < off + size and off < o + s
                     and gk in self.prog.state)


class Buf:
    def __init__(self, arena, name, off, size):
        self.arena, self.name, self.off, self.size = arena, name, off, size

    def view(self, dtype, shape, byteoff=0):
        n = 1
        for s in shape:
            n *= s
        nb = n * (2 if dtype == BF16 else 4)
        assert byteoff + nb <= self.size, (self.name, shape, byteoff, self.size)
        ap = self.arena.t.ap()[:, self.off + byteoff: self.off + byteoff + nb]
        ap = ap.bitcast(dtype)
        if len(shape) > 1:
            ap = ap.rearrange(
                "p (" + " ".join(f"a{i}" for i in range(len(shape))) + ") -> p "
                + " ".join(f"a{i}" for i in range(len(shape))),
                **{f"a{i}": shape[i] for i in range(len(shape))})
        return ap

    def key(self, sub=""):
        return f"B:{self.name}:{sub}"

    def wfirst(self, sub=""):
        return (self.key(sub),) + self.arena.ghost_keys(self.off, self.size)


def build_nc(debug=False):
    nc = bass.Bass("TRN2", target_bir_lowering=False)
    p = Prog()

    def ein(name, shape, dt=F32):
        return nc.dram_tensor(name, shape, dt, kind="ExternalInput").ap()

    def idram(name, shape, dt):
        kind = "ExternalOutput" if debug else "Internal"
        return nc.dram_tensor(name, shape, dt, kind=kind).ap()

    xs = ein("xs", [TH + 1, D])
    wrT = ein("wrT", [D, D]); wkT = ein("wkT", [D, D])
    wvT = ein("wvT", [D, D]); woT = ein("woT", [D, D])
    tm_in = ein("tm", [3, 128, CT])
    decay = ein("decay", [128, CT]); tfirst = ein("tfirst", [128, CT])
    gnw = ein("gnw", [128, CT]); gnb = ein("gnb", [128, CT])
    wg = ein("w_gate", [FI, D]); wu = ein("w_up", [FI, D])
    wd = ein("w_down", [D, FI])
    selv = ein("selv", [8, 1])
    selp2 = ein("selp2", [8, 1])
    ones128 = ein("ones128", [1, 128])
    halfT = ein("halfT", [128, 2])
    halfB = ein("halfB", [2, 128])
    yout = nc.dram_tensor("y", [TH, D], F32, kind="ExternalOutput").ap()

    if debug:
        dbg_wkv = idram("dbg_wkv", [CT, 128, TH], BF16)
        dbg_out = idram("dbg_out", [CT, 128, TH], BF16)
        dbg_sc = idram("dbg_sc", [4, 128, 64], F32)
        dbg_p2 = idram("dbg_p2", [FOT, 128, TH], BF16)
        dbg_p3 = idram("dbg_p3", [FOT, 128, TH], BF16)
    a_dram = idram("a_dram", [CT, 128, TH], BF16)
    b_dram = idram("b_dram", [CT, 128, TH], BF16)
    wq_g = idram("wq_g", [FI, D], BF16)
    wq_u = idram("wq_u", [FI, D], BF16)
    wq_d = idram("wq_d", [D, FI], BF16)
    xln_dram = idram("xln_dram", [TH, D], BF16)
    xn0_dram = idram("xn0_dram", [D], F32)
    att_dram = idram("att_dram", [D, TH], BF16)
    res_dram = idram("res_dram", [TH, D], F32)
    xq_dram = idram("xq_dram", [TH, D], BF16)
    up_dram = idram("up_dram", [FOT, 128, TH], BF16)
    ffn_dram = idram("ffn_dram", [D, TH], BF16)
    carry_red = idram("carry_red", [2048], F32)
    cc_in = {}
    cc_out = {}
    for nm, ishape, oshape in (("carry", [1, 2048], [8, 2048]),
                               ("gnst", [1, 32], [8, 32]),
                               ("mx0", [1, 4], [8, 4]),
                               ("mx1", [1, 4], [8, 4]),
                               ("mx2", [1, 4], [8, 4]),
                               ("mx3", [1, 4], [8, 4]),
                               ("mx4", [1, 4], [8, 4])):
        cc_in[nm] = nc.dram_tensor(f"cc_{nm}_in", ishape, F32,
                                   kind="Internal").ap()
        cc_out[nm] = nc.dram_tensor(f"cc_{nm}_out", oshape, F32,
                                    kind="Internal").ap()

    SBB = 206 * 1024
    ar = Arena(nc, p, "arena", SBB)
    ps = [nc.alloc_psum_tensor(f"ps{i}", [128, 512], F32) for i in range(8)]
    PSK = [f"psum{i}" for i in range(8)]

    def dma(dst, src, reads, writes, eng="sp", slow=False):
        p.emit(eng, lambda e, d=dst, s=src, sl=slow: e.dma_start(
            out=d, in_=s, allow_slow_non_contiguous=sl),
               reads=reads, writes=writes, inc=16)

    def dmaT(dst, src, reads, writes, eng="sp"):
        p.emit(eng, lambda e, d=dst, s=src: e.dma_start_transpose(out=d, in_=s),
               reads=reads, writes=writes, inc=16)

    def ts(eng, out, in0, s1, s2, op0, op1, reads, writes, accum=None):
        if s2 is None:
            p.emit(eng, lambda e, o=out, i=in0, a=s1, oa=op0, ac=accum:
                   e.tensor_scalar(o, i, a, None, oa, accum_out=ac),
                   reads=reads, writes=writes)
        else:
            p.emit(eng, lambda e, o=out, i=in0, a=s1, b=s2, oa=op0, ob=op1,
                   ac=accum:
                   e.tensor_scalar(o, i, a, b, oa, ob, accum_out=ac),
                   reads=reads, writes=writes)

    def tt(eng, out, in0, in1, op, reads, writes):
        p.emit(eng, lambda e, o=out, a=in0, b=in1, oo=op:
               e.tensor_tensor(o, a, b, oo), reads=reads, writes=writes)

    def stt(eng, out, in0, s, in1, op0, op1, reads, writes, accum=None):
        p.emit(eng, lambda e, o=out, a=in0, sc=s, b=in1, oa=op0, ob=op1,
               ac=accum:
               e.scalar_tensor_tensor(o, a, sc, b, oa, ob, accum_out=ac),
               reads=reads, writes=writes)

    def act(eng, out, in_, func, reads, writes, bias=0.0, scale=1.0,
            accum=None):
        p.emit(eng, lambda e, o=out, i=in_, f=func, b=bias, s=scale, ac=accum:
               e.activation(o, i, f, bias=b, scale=s, accum_out=ac),
               reads=reads, writes=writes)

    def cp(eng, out, in_, reads, writes):
        p.emit(eng, lambda e, o=out, i=in_: e.tensor_copy(out=o, in_=i),
               reads=reads, writes=writes)

    def red(eng, out, in_, op, reads, writes, absval=None):
        p.emit(eng, lambda e, o=out, i=in_, oo=op, av=absval:
               e.tensor_reduce(o, i, mybir.AxisListType.X, oo,
                               apply_absolute_value=av),
               reads=reads, writes=writes)

    def mm(out, lhsT, rhs, start, stop, reads, writes, defer=True):
        p.emit("pe", lambda e, o=out, l=lhsT, r=rhs, st=start, sp_=stop:
               e.matmul(o, l, r, start=st, stop=sp_),
               reads=reads, writes=writes, defer=(defer and not stop))

    def scan(eng, out, d0, d1, init, reads, writes):
        p.emit(eng, lambda e, o=out, a=d0, b=d1, i=init:
               e.tensor_tensor_scan(o, a, b, i, ALU.mult, ALU.add),
               reads=reads, writes=writes)

    def coll(nm, reads):
        p.ncoll += 1
        p.emit("pool", lambda e, i=cc_in[nm], o=cc_out[nm]:
               e.collective_compute("AllGather", ALU.bypass,
                                    replica_groups=[list(range(NC))],
                                    ins=[i.opt()], outs=[o.opt()]),
               reads=reads + (f"ccin:{nm}",), writes=(f"ccout:{nm}",),
               inc=1, sem=f"cc{p.ncoll}")

    small = ar.alloc("small", 12 * CT * 4)
    SM = {"w": 0, "eu": 1, "tmr": 2, "tmk": 3, "tmv": 4, "gnw": 5, "gnb": 6,
          "A": 7, "Bb": 8, "t0": 9, "t1": 10, "t2": 11}

    def sm(nm):
        i = SM[nm]
        return small.view(F32, [12, CT])[:, i, :]

    consts = ar.alloc("consts", 2048)
    ones_sb = consts.view(F32, [128], byteoff=0)[:1, :]
    halfT_sb = consts.view(F32, [2], byteoff=512)
    halfB_sb = consts.view(F32, [128], byteoff=1024)[:2, :]
    selv_sb = consts.view(F32, [1], byteoff=1536)[:8, :]
    selp2_sb = consts.view(F32, [1], byteoff=1600)[:8, :]
    scal = ar.alloc("scal", 1024)

    def scv(i):
        return scal.view(F32, [256])[:, i:i + 1]

    mxrow = ar.alloc("mxrow", 1024)
    epsb = ar.alloc("epsb", 64)
    epsv = epsb.view(F32, [1])
    p.emit("pool", lambda e, o=epsv: e.memset(o, EPS),
           reads=(), writes=(epsb.key("v"),))
    EPSK = (epsb.key("v"),)

    prm = ar.alloc("prm", 2 * CT * 4)
    prm_v = prm.view(F32, [2, CT])
    dma(prm_v[:, 0, :], decay, reads=(), writes=prm.wfirst("decay"))
    dma(prm_v[:, 1, :], tfirst, reads=(), writes=(prm.key("tf"),))
    act("act", sm("t0"), prm_v[:, 0, :], AF.Exp,
        reads=(prm.key("decay"),), writes=small.wfirst("e1"))
    act("act", sm("w"), sm("t0"), AF.Exp, scale=-1.0,
        reads=(small.key("e1"),), writes=(small.key("w"),))
    act("act", sm("eu"), prm_v[:, 1, :], AF.Exp,
        reads=(prm.key("tf"),), writes=(small.key("eu"),))
    dma(small.view(F32, [12, CT])[:, 2:5, :],
        tm_in.rearrange("a p c -> p a c"),
        reads=(), writes=(small.key("tmr"), small.key("tmk"),
                          small.key("tmv")))
    dma(sm("gnw"), gnw, reads=(), writes=(small.key("gnw"),))
    dma(sm("gnb"), gnb, reads=(), writes=(small.key("gnb"),))
    dma(ones_sb, ones128, reads=(), writes=consts.wfirst("ones"))
    dma(halfT_sb, halfT, reads=(), writes=(consts.key("halfT"),))
    dma(halfB_sb, halfB, reads=(), writes=(consts.key("halfB"),))
    dma(selv_sb, selv, reads=(), writes=(consts.key("selv"),))
    dma(selp2_sb, selp2, reads=(), writes=(consts.key("selp2"),))

    rstore = ar.alloc("rstore", CT * TH * 2)
    rst_v = rstore.view(BF16, [CT, TH])
    wbuf = ar.alloc("wbuf", 3 * CT * D * 2)
    wb_v = wbuf.view(BF16, [3, CT, D])
    wtmp = ar.alloc("wtmp", CT * D * 4)
    for pr, src in enumerate((wrT, wkT, wvT)):
        wtf = wtmp.view(F32, [CT, D])
        dma(wtf, src.rearrange("(c p) d -> p c d", p=128),
            reads=tuple(wtmp.key(f"r{pr - 1}:{c}") for c in range(CT))
            if pr else (),
            writes=wtmp.wfirst(f"w{pr}") if pr == 0 else (wtmp.key(f"w{pr}"),))
        for c in range(CT):
            cp("dve" if c % 2 == 0 else "pool", wb_v[:, pr, c, :],
               wtf[:, c, :],
               reads=(wtmp.key(f"w{pr}"),),
               writes=wbuf.wfirst(f"{pr}:{c}") if pr == 0 and c == 0
               else (wbuf.key(f"{pr}:{c}"),))
            p.state[wtmp.key(f"r{pr}:{c}")] = [
                p.state[wbuf.key(f"{pr}:{c}")][0], []]

    ar.free(wtmp)
    stats = ar.alloc("stats", 4096)
    XNW = 528
    xnf = [ar.alloc(f"xnf{i}", CT * XNW * 2) for i in range(2)]

    prow = ar.alloc("prow", 4096 + 64)
    prow_f = prow.view(F32, [1024])[:1, :]
    prow_s = prow.view(F32, [16], byteoff=4096)[:1, :]
    dma(prow_f, xs[0:1, :], reads=(), writes=prow.wfirst("x"))
    red("dve", prow_s[:, 0:1], prow_f, ALU.add,
        reads=(prow.key("x"),), writes=(prow.key("sum"),))
    act("act", prow.view(F32, [1024], byteoff=0)[:1, :], prow_f, AF.Square,
        reads=(prow.key("x"),), writes=(prow.key("x"),),
        accum=prow_s[:, 1:2])
    p.state[prow.key("sq")] = [p.state[prow.key("x")][0], []]
    dma(prow_f, xs[0:1, :], reads=(prow.key("sq"),), writes=(prow.key("x2"),))
    ts("dve", prow_s[:, 2:3], prow_s[:, 0:1], 1.0 / D, None, ALU.mult, None,
       reads=(prow.key("sum"),), writes=(prow.key("mean"),))
    act("act", prow_s[:, 5:6], prow_s[:, 2:3], AF.Square,
        reads=(prow.key("mean"),), writes=(prow.key("m2"),))
    stt("dve", prow_s[:, 3:4], prow_s[:, 1:2], 1.0 / D, prow_s[:, 5:6],
        ALU.mult, ALU.subtract,
        reads=(prow.key("sq"), prow.key("m2")), writes=(prow.key("var"),))
    act("act", prow_s[:, 6:7], prow_s[:, 3:4], AF.Ln, bias=epsv[:1, :],
        reads=(prow.key("var"),) + EPSK, writes=(prow.key("lnv"),))
    act("act", prow_s[:, 7:8], prow_s[:, 6:7], AF.Exp, scale=-0.5,
        reads=(prow.key("lnv"),), writes=(prow.key("rstd"),))
    ts("dve", prow_f, prow_f, prow_s[:, 2:3], prow_s[:, 7:8],
       ALU.subtract, ALU.mult,
       reads=(prow.key("x2"), prow.key("mean"), prow.key("rstd")),
       writes=(prow.key("xn0"),))
    dma(xn0_dram[None, :], prow_f, reads=(prow.key("xn0"),),
        writes=("xn0_dram",))
    x0tmp = stats.view(F32, [64])[:, 32:40]
    dma(x0tmp, xn0_dram.rearrange("(c p) -> p c", p=128),
        reads=("xn0_dram",), writes=(stats.key("xn0"),), slow=True)
    xnfv = [xnf[i].view(BF16, [CT, XNW]) for i in range(2)]
    cp("dve", xnfv[0][:, :, 15], x0tmp,
       reads=(stats.key("xn0"),), writes=xnf[0].wfirst("prevcol"))
    ar.free(prow)

    xtok = ar.alloc("xtok", 4 * 1024 * 4)
    xln_t = ar.alloc("xln_t", 4 * 1024 * 2)
    sqs = ar.alloc("sqs", 1024 * 4)
    mixb = ar.alloc("mixb", 2 * 3 * CT * TT * 2)
    dscr = ar.alloc("dscr", CT * TT * 2)
    abst = ar.alloc("abst", 2 * CT * TT * 2)
    asp = abst.view(BF16, [2, CT, TT])

    def stat_v(i):
        return stats.view(F32, [64])[:, i * 4:(i + 1) * 4]

    def a1_S0(ti):
        xkey = xtok.key(f"t{ti}")
        dma(xtok.view(F32, [4, 1024]),
            xs[1 + ti * TT: 1 + (ti + 1) * TT, :]
            .rearrange("(a p) d -> p a d", p=128),
            reads=(xtok.key(f"done{ti - 1}"),) if ti else (),
            writes=xtok.wfirst(f"t{ti}") if ti == 0 else (xkey,))

    def a1_S1(ti):
        xkey = xtok.key(f"t{ti}")
        xl_done = []
        for pt in range(4):
            sv = stat_v(pt)
            xv = xtok.view(F32, [4, 1024])[:, pt, :]
            red("dve", sv[:, 0:1], xv, ALU.add,
                reads=(xkey,), writes=(stats.key(f"s{pt}"),))
            act("act", sqs.view(F32, [1024]), xv, AF.Square,
                reads=(xkey,), writes=(sqs.key("w"),) if ti or pt
                else sqs.wfirst("w"), accum=sv[:, 1:2])
            p.state[stats.key(f"q{pt}")] = [p.state[sqs.key("w")][0], []]
            ts("dve", sv[:, 2:3], sv[:, 0:1], 1.0 / D, None, ALU.mult, None,
               reads=(stats.key(f"s{pt}"),), writes=(stats.key(f"m{pt}"),))
            act("act", sv[:, 0:1], sv[:, 2:3], AF.Square,
                reads=(stats.key(f"m{pt}"),), writes=(stats.key(f"mm{pt}"),))
            stt("dve", sv[:, 1:2], sv[:, 1:2], 1.0 / D, sv[:, 0:1],
                ALU.mult, ALU.subtract,
                reads=(stats.key(f"q{pt}"), stats.key(f"mm{pt}")),
                writes=(stats.key(f"v{pt}"),))
            act("act", sv[:, 1:2], sv[:, 1:2], AF.Ln, bias=epsv,
                reads=(stats.key(f"v{pt}"),) + EPSK, writes=(stats.key(f"lv{pt}"),))
            act("act", sv[:, 1:2], sv[:, 1:2], AF.Exp, scale=-0.5,
                reads=(stats.key(f"lv{pt}"),), writes=(stats.key(f"r{pt}"),))
            ts("dve", xln_t.view(BF16, [4, 1024])[:, pt, :], xv,
               sv[:, 2:3], sv[:, 1:2], ALU.subtract, ALU.mult,
               reads=(xkey, stats.key(f"m{pt}"), stats.key(f"r{pt}")),
               writes=xln_t.wfirst(f"{pt}") if ti == 0 and pt == 0
               else (xln_t.key(f"{pt}"),))
            xl_done.append(p.state[xln_t.key(f"{pt}")][0])
        p.state[xtok.key(f"done{ti}")] = [list(xl_done), []]
        dma(xln_dram[ti * TT:(ti + 1) * TT, :]
            .rearrange("(a p) d -> p a d", p=128),
            xln_t.view(BF16, [4, 1024]),
            reads=tuple(xln_t.key(f"{pt}") for pt in range(4)),
            writes=(f"xln_dram:{ti}",))
        ib = ti % 2
        for c in range(CT):
            dmaT(xnfv[ib][:, c, 16:16 + TT],
                 xln_dram[ti * TT:(ti + 1) * TT, c * 128:(c + 1) * 128],
                 reads=(f"xln_dram:{ti}",) + ((xnf[ib].key(f"used{ti - 2}"),)
                                              if ti >= 2 else ()),
                 writes=xnf[ib].wfirst(f"d:{c}") if ti < 2 and c == 0
                 else (xnf[ib].key(f"d:{c}"),))

    def a1_S2(ti):
        ib = ti % 2
        xnk = tuple(xnf[ib].key(f"d:{c}") for c in range(CT))
        prevk = (xnf[ib].key("prevcol"),)
        sb = ti % 2
        dsv = dscr.view(BF16, [CT, TT])
        mixv = mixb.view(BF16, [2, 3, CT, TT])[:, sb]
        for c in range(CT):
            tt("dve" if c % 2 else "pool", dsv[:, c, :],
               xnfv[ib][:, c, 16:16 + TT], xnfv[ib][:, c, 15:15 + TT],
               ALU.subtract,
               reads=(xnk[c],) + prevk,
               writes=dscr.wfirst(f"{c}") if ti == 0 and c == 0
               else (dscr.key(f"{c}"),))
        for pr in range(3):
            tmk_ = small.key(("tmr", "tmk", "tmv")[pr])
            tmv_ = small.view(F32, [12, CT])[:, 2 + pr, :]
            for c in range(CT):
                stt("dve", mixv[:, pr, c, :],
                    dsv[:, c, :], tmv_[:, c:c + 1],
                    xnfv[ib][:, c, 15:15 + TT], ALU.mult, ALU.add,
                    reads=(dscr.key(f"{c}"), tmk_, xnk[c]) + prevk,
                    writes=mixb.wfirst(f"{sb}:0:0")
                    if ti < 2 and pr == 0 and c == 0
                    else (mixb.key(f"{sb}:{pr}:{c}"),))
        ob = (ti + 1) % 2
        cp("dve", xnfv[ob][:, :, 15], xnfv[ib][:, :, 16 + TT - 1],
           reads=xnk, writes=(xnf[ob].key("prevcol"),)
           if ti > 0 else xnf[ob].wfirst("prevcol"))
        p.state[xnf[ib].key(f"used{ti}")] = [[
            p.state[mixb.key(f"{sb}:{pr}:{c}")][0]
            for pr in range(3) for c in range(CT)], []]

    def a1_S3(ti):
        ib = ti % 2
        sb = ti % 2
        mixv = mixb.view(BF16, [2, 3, CT, TT])[:, sb]
        for pr in range(3):
            for co in range(CT):
                bank = (pr * CT + co) % 8
                pk = PSK[bank]
                for k in range(CT):
                    mm(ps[bank].ap(), wb_v[:, pr, k, co * 128:(co + 1) * 128],
                       mixv[:, pr, k, :],
                       start=(k == 0), stop=(k == CT - 1),
                       reads=(wbuf.key(f"{pr}:{k}"),
                              mixb.key(f"{sb}:{pr}:{k}")),
                       writes=(pk,))
                if pr == 0:
                    act("act", rst_v[:, co, ti * TT:(ti + 1) * TT],
                        ps[bank].ap(), AF.Identity, reads=(pk,),
                        writes=rstore.wfirst(f"{co}:{ti}")
                        if ti == 0 and co == 0 else (rstore.key(f"{co}:{ti}"),))
                elif pr == 1:
                    act("act", asp[:, 1, co, :], ps[bank].ap(), AF.Exp,
                        reads=(pk,),
                        writes=abst.wfirst(f"b{co}") if ti == 0 and co == 0
                        else (abst.key(f"b{co}"),))
                else:
                    tt("dve", asp[:, 0, co, :], asp[:, 1, co, :],
                       ps[bank].ap(), ALU.mult,
                       reads=(abst.key(f"b{co}"), pk),
                       writes=abst.wfirst(f"a{co}") if ti == 0 and co == 0
                       else (abst.key(f"a{co}"),))
        dma(a_dram[:, :, ti * TT:(ti + 1) * TT].rearrange("c p t -> p c t"),
            asp[:, 0, :, :],
            reads=tuple(abst.key(f"a{c}") for c in range(CT)),
            writes=(f"a_dram:{ti}",))
        dma(b_dram[:, :, ti * TT:(ti + 1) * TT].rearrange("c p t -> p c t"),
            asp[:, 1, :, :],
            reads=tuple(abst.key(f"b{c}") for c in range(CT)),
            writes=(f"b_dram:{ti}",))
        for c in range(CT):
            p.state[abst.key(f"b{c}")][1].append(p.state[f"b_dram:{ti}"][0])
            p.state[abst.key(f"a{c}")][1].append(p.state[f"a_dram:{ti}"][0])

    a1_S0(0)
    a1_S1(0)
    for ti in range(NTT):
        if ti + 1 < NTT:
            a1_S0(ti + 1)
        a1_S2(ti)
        if ti + 1 < NTT:
            a1_S1(ti + 1)
        a1_S3(ti)

    for bb in (xtok, xln_t, sqs, mixb, dscr, xnf[0], xnf[1], abst):
        ar.free(bb)
    p.merge_deps("a_dram:all", [f"a_dram:{i}" for i in range(NTT)])
    p.merge_deps("b_dram:all", [f"b_dram:{i}" for i in range(NTT)])

    DQI = 256
    NDQS = 4
    deq = ar.alloc("deq", NDQS * (DQI * 4 * 3 + DQI * 2 + 512), end=True)
    DSLOT = DQI * 4 * 3 + DQI * 2 + 512

    def dequant_gen(tag, w_src, wq2, O, I):
        nb = DQI // 32
        units = [(oc, ic) for oc in range(O // 128) for ic in range(I // DQI)]
        nuu = len(units)
        wq2v = wq2.rearrange("(o p) i -> o p i", p=128)

        def views(ui):
            s = ui % NDQS
            bo = s * DSLOT
            return (s, bo,
                    deq.view(F32, [DQI], byteoff=bo),
                    deq.view(F32, [DQI], byteoff=bo + DQI * 4),
                    deq.view(I32, [DQI], byteoff=bo + DQI * 8),
                    deq.view(BF16, [DQI], byteoff=bo + DQI * 12),
                    deq.view(F32, [nb * 2], byteoff=bo + DQI * 14))

        def st_L(ui):
            oc, ic = units[ui]
            s, bo, wn, q3, qiw, st_, bm = views(ui)
            K = lambda nm: deq.key(f"{tag}{nm}{s}")
            dma(wn, w_src[oc * 128:(oc + 1) * 128, ic * DQI:(ic + 1) * DQI],
                reads=(K("done"),) if ui >= NDQS else (),
                writes=(K("w"),))

        def st_R(ui):
            s, bo, wn, q3, qiw, st_, bm = views(ui)
            K = lambda nm: deq.key(f"{tag}{nm}{s}")
            red("dve", bm[:, 0:nb], wn.rearrange("p (n b) -> p n b", b=32),
                ALU.max, absval=True,
                reads=(K("w"),), writes=(K("bm"),))
            ts("dve", bm[:, 0:nb], bm[:, 0:nb], 1.0 / 7.0, 1e-12,
               ALU.mult, ALU.max,
               reads=(K("bm"),), writes=(K("sw"),))
            p.emit("dve", lambda e, o=bm[:, nb:2 * nb], i=bm[:, 0:nb]:
                   e.reciprocal(o, i),
                   reads=(K("sw"),), writes=(K("inv"),))

        def st_Q(ui):
            s, bo, wn, q3, qiw, st_, bm = views(ui)
            K = lambda nm: deq.key(f"{tag}{nm}{s}")
            tt("pool", q3.rearrange("p (n b) -> p n b", b=32),
               wn.rearrange("p (n b) -> p n b", b=32),
               bm[:, nb:2 * nb].rearrange("p n -> p n ()")
               .broadcast_to([128, nb, 32]), ALU.mult,
               reads=(K("w"), K("inv")), writes=(K("q"),))

        def st_S(ui):
            oc, ic = units[ui]
            s, bo, wn, q3, qiw, st_, bm = views(ui)
            K = lambda nm: deq.key(f"{tag}{nm}{s}")
            cp("dve", qiw, q3, reads=(K("q"),), writes=(K("qi"),))
            tt("dve", st_.rearrange("p (n b) -> p n b", b=32),
               qiw.rearrange("p (n b) -> p n b", b=32),
               bm[:, 0:nb].rearrange("p n -> p n ()")
               .broadcast_to([128, nb, 32]), ALU.mult,
               reads=(K("qi"), K("sw")), writes=(K("st"),))
            dma(wq2v[oc, :, ic * DQI:(ic + 1) * DQI], st_,
                reads=(K("st"),), writes=(f"wq{tag}:{oc}:{ic}",))
            p.state[K("done")] = [[
                p.state[f"wq{tag}:{oc}:{ic}"][0],
                p.state[K("st")][0]], []]

        STG = [st_S, st_Q, st_R, st_L]
        for t in range(nuu + 3):
            for k, fn in enumerate(STG):
                u = t - (3 - k)
                if 0 <= u < nuu:
                    fn(u)
            yield
        p.merge_deps(f"wq{tag}:all",
                     [f"wq{tag}:{oc}:{ic}" for oc in range(O // 128)
                      for ic in range(I // DQI)])

    gen_g = dequant_gen("g", wg, wq_g, FI, D)
    gen_u = dequant_gen("u", wu, wq_u, FI, D)
    gen_d = dequant_gen("d", wd, wq_d, D, FI)

    def pump(gg, n=1):
        for _ in range(n):
            if next(gg, "END") == "END":
                break

    ar.free(wbuf)
    ab2 = [ar.alloc(f"ab2_{i}", 2 * TH * 2) for i in range(3)]
    numb_ = [ar.alloc(f"numb{i}", (TH + 16) * 4) for i in range(2)]
    denb_ = [ar.alloc(f"denb{i}", (TH + 16) * 4) for i in range(2)]
    carp = ar.alloc("carp", 2 * CT * 4)
    outst = ar.alloc("outst", CT * TH * 2)
    out_v = outst.view(BF16, [CT, TH])
    scr = ar.alloc("scr", 6 * TH * 4)

    def scrv(i):
        return scr.view(F32, [6, TH])[:, i, :]

    def plk(i):
        return scr.key(f"pl{i}")

    wv_ = sm("w")
    euv = sm("eu")

    def ld_ab(ct, phase):
        ib = ct % 3
        v = ab2[ib].view(BF16, [2, TH])
        prior = ab2[ib].key(f"used:{phase}:{ct - 3}") if ct >= 3 else (
            ab2[ib].key(f"used:p1:{max(c for c in range(CT) if c % 3 == ib)}")
            if phase == "p2" else None)
        rd = ("a_dram:all", "b_dram:all") + ((prior,) if prior else ())
        first = (phase == "p1" and ct < 3)
        dma(v[:, 0, :], a_dram[ct], reads=rd,
            writes=ab2[ib].wfirst(f"a:{ct}") if first
            else (ab2[ib].key(f"a:{ct}"),))
        dma(v[:, 1, :], b_dram[ct], reads=rd,
            writes=ab2[ib].wfirst(f"b:{ct}") if first
            else (ab2[ib].key(f"b:{ct}"),))
        return v, ab2[ib].key(f"a:{ct}"), ab2[ib].key(f"b:{ct}")

    cv = carp.view(F32, [2 * CT])
    for ct in range(CT):
        v, ak, bk = ld_ab(ct, "p1")
        numb, denb = numb_[ct % 2], denb_[ct % 2]
        nv = numb.view(F32, [TH + 16])
        dv = denb.view(F32, [TH + 16])
        scan("dve", nv[:, 16:16 + TH],
             wv_[:, ct:ct + 1].broadcast_to([128, TH]), v[:, 0, :], 0.0,
             reads=(ak, small.key("w")),
             writes=numb.wfirst("s") if ct == 0 else (numb.key("s"),))
        scan("dve", dv[:, 16:16 + TH],
             wv_[:, ct:ct + 1].broadcast_to([128, TH]), v[:, 1, :], 0.0,
             reads=(bk, small.key("w")),
             writes=denb.wfirst("s") if ct == 0 else (denb.key("s"),))
        cp("dve", cv[:, ct:ct + 1], nv[:, 16 + TH - 1:16 + TH],
           reads=(numb.key("s"),),
           writes=carp.wfirst(f"n{ct}") if ct == 0 else (carp.key(f"n{ct}"),))
        cp("pool", cv[:, CT + ct:CT + ct + 1], dv[:, 16 + TH - 1:16 + TH],
           reads=(denb.key("s"),), writes=(carp.key(f"d{ct}"),))
        p.state[ab2[ct % 3].key(f"used:p1:{ct}")] = [[
            p.state[numb.key("s")][0], p.state[denb.key("s")][0]], []]

    CARPK = tuple(carp.key(f"{sd}{c}") for sd in "nd" for c in range(CT))
    if debug:
        dma(dbg_sc[1, :, 32:48], cv, reads=CARPK, writes=("dbgsc1c",))
    dma(cc_in["carry"].rearrange("o (p c) -> (o p) c", p=128),
        cv, reads=CARPK, writes=("ccin:carry",))
    coll("carry", reads=())
    gsb = None
    gsb_v = scr.view(F32, [6, TH])[:8, 4, :]
    dma(gsb_v, cc_out["carry"], reads=("ccout:carry",),
        writes=scr.wfirst("pl4"))
    cirow = None
    cirow_v = scr.view(F32, [6, TH])[:1, 5, :]
    for j in range(4):
        mm(ps[4].ap()[:1, :], selv_sb, gsb_v[:, j * 512:(j + 1) * 512],
           start=True, stop=True,
           reads=(consts.key("selv"), plk(4)), writes=(PSK[4],),
           defer=False)
        cp("dve", cirow_v[:, j * 512:(j + 1) * 512], ps[4].ap()[:1, :],
           reads=(PSK[4],),
           writes=(plk(5),) if j else scr.wfirst("pl5"))
    dma(carry_red[None, :], cirow_v, reads=(plk(5),),
        writes=("carry_red",))
    cin2 = ar.alloc("cin2", 2 * CT * 4)
    cin2_v = cin2.view(F32, [2 * CT])
    dma(cin2_v, carry_red.rearrange("(p c) -> p c", p=128),
        reads=("carry_red",), writes=cin2.wfirst("c"))

    stat2 = ar.alloc("stat2", 2 * CT * 4)
    st2v = stat2.view(F32, [2 * CT])

    def a2_T0(ct):
        v, ak, bk = ld_ab(ct, "p2")
        numb, denb = numb_[ct % 2], denb_[ct % 2]
        nv = numb.view(F32, [TH + 16])
        dv = denb.view(F32, [TH + 16])
        s = ct % 2
        cp("dve", nv[:, 15:16], cin2_v[:, ct:ct + 1],
           reads=(cin2.key("c"),), writes=(numb.key("seed"),))
        cp("pool", dv[:, 15:16], cin2_v[:, CT + ct:CT + ct + 1],
           reads=(cin2.key("c"),), writes=(denb.key("seed"),))
        scan("dve", nv[:, 16:16 + TH],
             wv_[:, ct:ct + 1].broadcast_to([128, TH]), v[:, 0, :],
             cin2_v[:, ct:ct + 1],
             reads=(ak, small.key("w"), cin2.key("c")),
             writes=(numb.key("s"),))
        scan("dve", dv[:, 16:16 + TH],
             wv_[:, ct:ct + 1].broadcast_to([128, TH]), v[:, 1, :],
             cin2_v[:, CT + ct:CT + ct + 1],
             reads=(bk, small.key("w"), cin2.key("c")),
             writes=(denb.key("s"),))
        stt("dve", scrv(3 * s), v[:, 0, :], euv[:, ct:ct + 1],
            nv[:, 15:15 + TH], ALU.mult, ALU.add,
            reads=(ak, small.key("eu"), numb.key("s"), numb.key("seed")),
            writes=scr.wfirst(f"pl{3 * s}") if ct < 2 else (plk(3 * s),))
        ts("pool", scrv(3 * s + 1), v[:, 1, :], euv[:, ct:ct + 1], 1e-9,
           ALU.mult, ALU.add,
           reads=(bk, small.key("eu")),
           writes=(plk(3 * s + 1),) if ct >= 2
           else scr.wfirst(f"pl{3 * s + 1}"))
        tt("pool", scrv(3 * s + 1), scrv(3 * s + 1), dv[:, 15:15 + TH],
           ALU.add,
           reads=(plk(3 * s + 1), denb.key("s"), denb.key("seed")),
           writes=(plk(3 * s + 1),))
        p.state[ab2[ct % 3].key(f"used:p2:{ct}")] = [[
            p.state[plk(3 * s)][0], p.state[plk(3 * s + 1)][0]], []]

    def a2_T1(ct):
        s = ct % 2
        act("act", scrv(3 * s + 2), scrv(3 * s + 1), AF.Ln,
            reads=(plk(3 * s + 1),),
            writes=(plk(3 * s + 2),) if ct >= 2
            else scr.wfirst(f"pl{3 * s + 2}"))
        act("act", scrv(3 * s + 2), scrv(3 * s + 2), AF.Exp, scale=-1.0,
            reads=(plk(3 * s + 2),), writes=(plk(3 * s + 2),))

    def a2_T2(ct):
        s = ct % 2
        tt("dve", out_v[:, ct, :], scrv(3 * s), scrv(3 * s + 2), ALU.mult,
           reads=(plk(3 * s), plk(3 * s + 2)),
           writes=outst.wfirst(f"wkv{ct}") if ct == 0
           else (outst.key(f"wkv{ct}"),))
        if debug:
            dma(dbg_wkv[ct], out_v[:, ct, :],
                reads=(outst.key(f"wkv{ct}"),), writes=(f"dbgwkv:{ct}",))

    def a2_T3(ct):
        act("act", rst_v[:, ct, :], rst_v[:, ct, :], AF.Sigmoid,
            reads=tuple(rstore.key(f"{ct}:{i}") for i in range(NTT)),
            writes=(rstore.key(f"sig{ct}"),))

    def a2_T4(ct):
        s = ct % 2
        stt("dve", out_v[:, ct, :], out_v[:, ct, :], 1.0, rst_v[:, ct, :],
            ALU.mult, ALU.mult,
            reads=(outst.key(f"wkv{ct}"), rstore.key(f"sig{ct}")),
            writes=(outst.key(f"o{ct}"),),
            accum=st2v[:, ct:ct + 1])
        p.state[stat2.key(f"s{ct}")] = [p.state[outst.key(f"o{ct}")][0], []]
        act("act", scrv(3 * s + 1), out_v[:, ct, :], AF.Square,
            reads=(outst.key(f"o{ct}"),),
            writes=(plk(3 * s + 1),),
            accum=st2v[:, CT + ct:CT + ct + 1])
        p.state[stat2.key(f"q{ct}")] = [p.state[plk(3 * s + 1)][0], []]

    A2S = [a2_T0, a2_T1, a2_T2, a2_T3, a2_T4]
    for t in range(CT + 4):
        for k in (4, 3, 2, 1, 0):
            u = t - k
            if 0 <= u < CT:
                A2S[k](u)
        pump(gen_g, 4)

    if debug:
        for ct in range(CT):
            dma(dbg_out[ct], out_v[:, ct, :],
                reads=(outst.key(f"o{ct}"),), writes=(f"dbgout:{ct}",))
        dma(dbg_sc[0, :, 0:16], st2v,
            reads=tuple(stat2.key(f"{sd}{c}") for sd in "sq"
                        for c in range(CT)),
            writes=("dbgsc0",))
        dma(dbg_sc[1, :, 0:16], cin2_v,
            reads=(cin2.key("c"),), writes=("dbgsc1",))
        dma(dbg_sc[1, :, 16:32], cv,
            reads=CARPK, writes=("dbgsc1b",))
    STK = tuple(stat2.key(f"{sd}{c}") for sd in "sq" for c in range(CT))
    hsum = ar.alloc("hsum", 256)
    hs_v = hsum.view(F32, [2 * CT])[:2, :]
    mm(ps[5].ap()[:2, 0:2 * CT], halfT_sb, st2v, start=True, stop=True,
       reads=(consts.key("halfT"),) + STK, writes=(PSK[5],), defer=False)
    cp("dve", hs_v, ps[5].ap()[:2, 0:2 * CT],
       reads=(PSK[5],), writes=hsum.wfirst("h"))
    dma(cc_in["gnst"].rearrange("o (h c) -> (o h) c", h=2), hs_v,
        reads=(hsum.key("h"),), writes=("ccin:gnst",))
    coll("gnst", reads=())
    gst = ar.alloc("gst", 2048)
    gst_v = gst.view(F32, [32])[:8, :]
    dma(gst_v, cc_out["gnst"], reads=("ccout:gnst",), writes=gst.wfirst("g"))
    tot_v = gst.view(F32, [32], byteoff=128)[:1, :]
    mm(ps[5].ap()[:1, 0:32], selp2_sb, gst_v, start=True, stop=True,
       reads=(consts.key("selp2"), gst.key("g")), writes=(PSK[5],),
       defer=False)
    cp("dve", tot_v, ps[5].ap()[:1, 0:32],
       reads=(PSK[5],), writes=(gst.key("tot"),))
    NTK = float(T * KD)
    tot2 = gst.view(F32, [2, 2, CT], byteoff=128)[:1, :, :, :]
    mu_v = gst.view(F32, [2, CT], byteoff=256)[:1, :, :]
    rs_v = gst.view(F32, [2, CT], byteoff=320)[:1, :, :]
    ts("dve", mu_v, tot2[:, :, 0, :], 1.0 / NTK, None, ALU.mult, None,
       reads=(gst.key("tot"),), writes=(gst.key("mu"),))
    act("act", rs_v, mu_v, AF.Square,
        reads=(gst.key("mu"),), writes=(gst.key("mu2"),))
    stt("dve", rs_v, tot2[:, :, 1, :], 1.0 / NTK, rs_v,
        ALU.mult, ALU.subtract,
        reads=(gst.key("tot"), gst.key("mu2")), writes=(gst.key("var"),))
    act("act", rs_v, rs_v, AF.Ln, bias=epsv[:1, :],
        reads=(gst.key("var"),) + EPSK, writes=(gst.key("lnv"),))
    act("act", rs_v, rs_v, AF.Exp, scale=-0.5,
        reads=(gst.key("lnv"),), writes=(gst.key("rstd"),))
    mrb = gst.view(F32, [2, 2 * CT], byteoff=384)[:2, :, :]
    for hh in range(2):
        dma(cc_in["gnst"][:, hh * 16:(hh + 1) * 16]
            .rearrange("o (w c) -> o w c", w=2),
            gst.view(F32, [2, 2, CT], byteoff=256)[:1, :, hh, :],
            reads=(gst.key("mu"), gst.key("rstd")),
            writes=("hb_dram", "ccin:gnst") if hh == 0
            else ("hb_dram2", "ccin:gnst"))
    dma(gst.view(F32, [2 * CT], byteoff=384)[:2, :],
        cc_in["gnst"].rearrange("o (h c2) -> (o h) c2", h=2),
        reads=("hb_dram", "hb_dram2"), writes=(gst.key("mrb"),))
    mm(ps[5].ap()[:, 0:2 * CT], halfB_sb,
       gst.view(F32, [2 * CT], byteoff=384)[:2, :],
       start=True, stop=True,
       reads=(consts.key("halfB"), gst.key("mrb")), writes=(PSK[5],),
       defer=False)
    murs = ar.alloc("murs", 2 * CT * 4)
    murs_v = murs.view(F32, [2, CT])
    cp("dve", murs.view(F32, [2 * CT]), ps[5].ap()[:, 0:2 * CT],
       reads=(PSK[5],), writes=murs.wfirst("m"))
    tt("dve", sm("A"), murs_v[:, 1, :], sm("gnw"), ALU.mult,
       reads=(murs.key("m"), small.key("gnw")), writes=(small.key("A"),))
    tt("dve", sm("t1"), murs_v[:, 0, :], sm("A"), ALU.mult,
       reads=(murs.key("m"), small.key("A")), writes=(small.key("muA"),))
    tt("dve", sm("Bb"), sm("gnb"), sm("t1"), ALU.subtract,
       reads=(small.key("gnb"), small.key("muA")), writes=(small.key("Bb"),))
    if debug:
        dma(dbg_sc[2, :, 0:CT * 2], murs.view(F32, [2 * CT]),
            reads=(murs.key("m"),), writes=("dbgsc2",))
        dma(dbg_sc[3, :, 0:CT], sm("A"), reads=(small.key("A"),),
            writes=("dbgsc3",))
        dma(dbg_sc[3, :, CT:2 * CT], sm("Bb"), reads=(small.key("Bb"),),
            writes=("dbgsc4",))
    for ct in range(CT):
        ts("dve", out_v[:, ct, :], out_v[:, ct, :],
           sm("A")[:, ct:ct + 1], sm("Bb")[:, ct:ct + 1], ALU.mult, ALU.add,
           reads=(outst.key(f"o{ct}"), small.key("A"), small.key("Bb")),
           writes=(outst.key(f"y{ct}"),))

    for bb in (ab2[0], ab2[1], ab2[2], numb_[0], numb_[1], denb_[0],
               denb_[1], carp, cin2, hsum, gst, murs, stat2,
               rstore):
        ar.free(bb)
    wob = ar.alloc("wob", CT * D * 2)
    wob_v = wob.view(BF16, [CT, D])
    for hlf in range(2):
        sv = scr.view(F32, [4, 1024])
        rd = () if hlf == 0 else (wob.key("wo:done0"),)
        dma(sv, woT.rearrange("(c p) d -> p c d", p=128)[:, hlf * 4:
                                                         (hlf + 1) * 4, :],
            reads=rd, writes=(plk(0), plk(1)))
        for c in range(4):
            cp("dve" if c % 2 else "pool", wob_v[:, hlf * 4 + c, :],
               sv[:, c, :],
               reads=(plk(0), plk(1)),
               writes=wob.wfirst(f"wo:{hlf * 4 + c}") if hlf == 0 and c == 0
               else (wob.key(f"wo:{hlf * 4 + c}"),))
        p.state[wob.key(f"wo:done{hlf}")] = [[
            p.state[wob.key(f"wo:{hlf * 4 + c}")][0] for c in range(4)], []]

    attst = ar.alloc("attst", 2 * CT * TT * 2)
    for ti in range(NTT):
        sb = ti % 2
        att_sv = attst.view(BF16, [2, CT, TT])[:, sb]
        for co in range(CT):
            bank = co % 8
            pk = PSK[bank]
            for k in range(CT):
                mm(ps[bank].ap(), wob_v[:, k, co * 128:(co + 1) * 128],
                   out_v[:, k, ti * TT:(ti + 1) * TT],
                   start=(k == 0), stop=(k == CT - 1),
                   reads=(wob.key(f"wo:{k}"), outst.key(f"y{k}")),
                   writes=(pk,))
            act("act", att_sv[:, co, :], ps[bank].ap(), AF.Identity,
                reads=(pk,),
                writes=attst.wfirst(f"{sb}:{co}") if ti < 2 and co == 0
                else (attst.key(f"{sb}:{co}"),))
            pump(gen_g, 2)
        dma(att_dram[:, ti * TT:(ti + 1) * TT]
            .rearrange("(c p) t -> p c t", p=128),
            att_sv,
            reads=tuple(attst.key(f"{sb}:{co}") for co in range(CT)),
            writes=(f"att_dram:{ti}",))
        for co in range(CT):
            p.state[attst.key(f"{sb}:{co}")][1].append(
                p.state[f"att_dram:{ti}"][0])
    p.merge_deps("att_dram:all", [f"att_dram:{i}" for i in range(NTT)])

    for bb in (outst, attst, scr, wob):
        ar.free(bb)

    NG = 4
    xres = [ar.alloc(f"xres{s}", 4 * 1024 * 4 + 4 * 1024 * 2) for s in range(2)]
    t1st = ar.alloc("t1st", 16 * 1024 * 2)
    t1_v = t1st.view(BF16, [16, 1024])
    mxacc = ar.alloc("mxacc", 256)
    sqs2 = [ar.alloc(f"sqs2{s}", 1024 * 4) for s in range(2)]
    t1f = [ar.alloc(f"t1f{s}", 1024 * 4) for s in range(2)]
    stats2 = ar.alloc("stats2", 512)

    for g in range(NG):
        s = g % 2
        res_g = xres[s].view(F32, [4, 1024], byteoff=0)
        att_g = xres[s].view(BF16, [4, 1024], byteoff=16384)
        sv = stats2.view(F32, [2, 4, 8])[:, s, :, :]
        K = lambda nm: xres[s].key(nm)
        KS = lambda nm: stats2.key(f"{nm}{s}")
        rd = (K("cons"),) if g >= 2 else ()
        dma(res_g, xs[1 + g * TT:1 + (g + 1) * TT, :]
            .rearrange("(a p) d -> p a d", p=128),
            reads=rd, writes=(K("x"),))
        for pt in range(4):
            i = g * 4 + pt
            dmaT(att_g[:, pt, :], att_dram[:, i * 128:(i + 1) * 128],
                 reads=("att_dram:all",) + rd,
                 writes=(K(f"a{pt}"),))
        tt("dve", res_g, res_g, att_g, ALU.add,
           reads=(K("x"), K("a0"), K("a1"), K("a2"), K("a3")),
           writes=(K("r"),))
        dma(res_dram[g * TT:(g + 1) * TT, :]
            .rearrange("(a p) d -> p a d", p=128), res_g,
            reads=(K("r"),), writes=(f"res_dram:{g}",))
        red("dve", sv[:, :, 0], res_g, ALU.add,
            reads=(K("r"),), writes=(KS("s"),))
        for pt in range(4):
            act("act", sqs2[s].view(F32, [1024]), res_g[:, pt, :], AF.Square,
                reads=(K("r"),), writes=(sqs2[s].key("w"),),
                accum=sv[:, pt, 1:2])
        p.state[KS("q")] = [p.state[sqs2[s].key("w")][0], []]
        ts("dve", sv[:, :, 2], sv[:, :, 0], 1.0 / D, None, ALU.mult, None,
           reads=(KS("s"),), writes=(KS("m"),))
        act("act", sv[:, :, 3], sv[:, :, 2], AF.Square,
            reads=(KS("m"),), writes=(KS("mm"),))
        stt("dve", sv[:, :, 1], sv[:, :, 1], 1.0 / D, sv[:, :, 3],
            ALU.mult, ALU.subtract,
            reads=(KS("q"), KS("mm")), writes=(KS("v"),))
        act("act", sv[:, :, 1], sv[:, :, 1], AF.Ln, bias=epsv,
            reads=(KS("v"),) + EPSK, writes=(KS("lv"),))
        act("act", sv[:, :, 1], sv[:, :, 1], AF.Exp, scale=-0.5,
            reads=(KS("lv"),), writes=(KS("r"),))
        cons = []
        for pt in range(4):
            i = g * 4 + pt
            ts("dve", t1f[s].view(F32, [1024]), res_g[:, pt, :],
               sv[:, pt, 2:3], sv[:, pt, 1:2], ALU.subtract, ALU.mult,
               reads=(K("r"), KS("m"), KS("r")),
               writes=(t1f[s].key("w"),))
            red("dve", mxacc.view(F32, [16])[:, i:i + 1],
                t1f[s].view(F32, [1024]), ALU.max, absval=True,
                reads=(t1f[s].key("w"),), writes=(mxacc.key(f"{i}"),))
            act("act", t1_v[:, i, :], t1f[s].view(F32, [1024]), AF.Identity,
                reads=(t1f[s].key("w"),), writes=(t1st.key(f"{i}"),))
            cons.append(p.state[t1st.key(f"{i}")][0])
            cons.append(p.state[mxacc.key(f"{i}")][0])
        cons.append(p.state[f"res_dram:{g}"][0])
        p.state[K("cons")] = [cons, []]
        pump(gen_u, 8)
    p.merge_deps("res_dram:all", [f"res_dram:{g}" for g in range(NG)])
    for bb in (sqs2[0], sqs2[1], t1f[0], t1f[1], stats2):
        ar.free(bb)

    def global_scalar_max(nm, lane_ap, lane_keys, s_slot):
        mr = mxrow.view(F32, [128])[:1, :]
        m8 = mxrow.view(F32, [8], byteoff=512)[:1, :]
        dma(carry_red[None, 0:128].rearrange("o (p c) -> (o p) c", p=128),
            lane_ap, reads=lane_keys, writes=("carry_red", mxrow.key("b")))
        dma(mr, carry_red[None, 0:128],
            reads=(mxrow.key("b"),), writes=(mxrow.key("l"),))
        p.emit("dve", lambda e, o=m8, i=mr: e.max(o, i),
               reads=(mxrow.key("l"),), writes=(mxrow.key("m8"),))
        dma(cc_in[nm][:, 0:4], m8[:, 0:4],
            reads=(mxrow.key("m8"),), writes=(f"ccin:{nm}",))
        coll(nm, reads=())
        gv = mxrow.view(F32, [32], byteoff=576)[:1, :]
        dma(gv, cc_out[nm], reads=(f"ccout:{nm}",), writes=(mxrow.key("g"),))
        gvr = gv.rearrange("p (r c) -> p r c", c=4)[:, :, 0]
        p.emit("dve", lambda e, o=m8, i=gvr: e.max(o, i),
               reads=(mxrow.key("g"),), writes=(mxrow.key("gm"),))
        sloc = mxrow.view(F32, [4], byteoff=704)[:1, 0:1]
        iloc = mxrow.view(F32, [4], byteoff=704)[:1, 1:2]
        ts("dve", sloc, m8[:, 0:1], 1.0 / QMAX, 1e-12, ALU.mult, ALU.max,
           reads=(mxrow.key("gm"),), writes=(mxrow.key("s"),))
        p.emit("dve", lambda e, o=iloc, i=sloc: e.reciprocal(o, i),
               reads=(mxrow.key("s"),), writes=(mxrow.key("i"),))
        mm(ps[6].ap()[:, 0:2], ones_sb, mxrow.view(F32, [4], byteoff=704)[:1, 0:2],
           start=True, stop=True,
           reads=(consts.key("ones"), mxrow.key("s"), mxrow.key("i")),
           writes=(PSK[6],), defer=False)
        cp("dve", scal.view(F32, [256])[:, s_slot:s_slot + 2],
           ps[6].ap()[:, 0:2],
           reads=(PSK[6],), writes=(scal.key(nm),))
        return scal.key(nm)

    mx1 = ar.alloc("mx1", 64)
    p.emit("dve", lambda e, o=mx1.view(F32, [8])[:, 0:8],
           i=mxacc.view(F32, [16]): e.max(o, i),
           reads=tuple(mxacc.key(f"{i}") for i in range(16)),
           writes=mx1.wfirst("m"))
    kx = global_scalar_max("mx0", mx1.view(F32, [8])[:, 0:1],
                           (mx1.key("m"),), 0)
    SX, IX = 0, 1

    qscr = [ar.alloc(f"qscr{s}", 4 * 1024 * 4 * 2) for s in range(2)]
    xqt = [ar.alloc(f"xqt{s}", 4 * 1024 * 2) for s in range(2)]
    for g in range(NG):
        s = g % 2
        qf = qscr[s].view(F32, [4, 1024], byteoff=0)
        qi = qscr[s].view(I32, [4, 1024], byteoff=16384)
        xq_tv = xqt[s].view(BF16, [4, 1024])
        t1g = t1st.view(BF16, [16, 1024])[:, g * 4:(g + 1) * 4, :]
        rd = tuple(t1st.key(f"{g * 4 + pt}") for pt in range(4)) + (kx,)
        if g >= 2:
            rd = rd + (xqt[s].key("cons"),)
        ts("dve", qf, t1g, scv(IX), None, ALU.mult, None,
           reads=rd, writes=(qscr[s].key("f"),))
        cp("dve", qi, qf, reads=(qscr[s].key("f"),),
           writes=(qscr[s].key("i"),))
        cp("dve", xq_tv, qi, reads=(qscr[s].key("i"),),
           writes=(xqt[s].key("v"),))
        dma(xq_dram[g * TT:(g + 1) * TT, :]
            .rearrange("(a p) d -> p a d", p=128), xq_tv,
            reads=(xqt[s].key("v"),), writes=(f"xq_dram:{g}",))
        p.state[xqt[s].key("cons")] = [[p.state[f"xq_dram:{g}"][0]], []]
        pump(gen_u, 4)
    p.merge_deps("xq_dram:all", [f"xq_dram:{g}" for g in range(NG)])
    for bb in (t1st, mxacc, mx1, qscr[0], qscr[1], xqt[0], xqt[1],
               xres[0], xres[1], stats):
        ar.free(bb)

    lmx = ar.alloc("lmx", 128)
    gkabs = ar.alloc("gkabs", 64)
    xqf = ar.alloc("xqf", CT * TH * 2)
    xqf_v = xqf.view(BF16, [CT, TH])
    for c in range(CT):
        dmaT(xqf_v[:, c, :], xq_dram[:, c * 128:(c + 1) * 128],
             reads=("xq_dram:all",),
             writes=xqf.wfirst(f"{c}") if c == 0 else (xqf.key(f"{c}"),))
    XQK = [xqf.key(f"{c}") for c in range(CT)]

    gatest_ = [ar.alloc(f"gatest{i}", (FOT // 2) * TH * 2) for i in range(2)]

    class _GV:
        def __getitem__(self, idx):
            _, ot, sl = idx
            return gatest_[ot // 12].view(BF16, [FOT // 2, TH])[:, ot % 12, sl]

    gate_v = _GV()

    class _GK:
        def key(self, sub):
            ot = int(sub.split(":")[1]) if ":" in sub else int(sub)
            return gatest_[ot // 12].key(sub)
        def wfirst(self, sub):
            ot = int(sub.split(":")[1]) if ":" in sub else int(sub)
            return gatest_[ot // 12].wfirst(sub)

    gatest = _GK()
    FH = FI // 2
    wlhs = ar.alloc("wlhs", CT * FH * 2)
    wlhs_v = wlhs.view(BF16, [CT, FH])
    lmx_f = lmx.view(F32, [8])
    lmx_m8 = lmx.view(BF16, [8], byteoff=32)

    p.emit("dve", lambda e, o=lmx_f: e.memset(o, 0.0),
           reads=(), writes=lmx.wfirst("g0") + tuple(
               lmx.key(f"{nm}0") for nm in ("u", "p2", "p3")))

    def absmax8_lane(src_ap, src_key, lane_i, lane_nm, slot=0, eng="dve"):
        lane_key = lmx.key(f"{lane_nm}0")
        si = lane_i * 2 + (slot % 2)
        sc = gkabs.view(F32, [8])[:, si:si + 1]
        skey = gkabs.key(f"s{si}")
        red(eng, sc, src_ap, ALU.max, absval=True,
            reads=(src_key,), writes=(skey,))
        li = lane_i * 2
        tt("dve", lmx_f[:, li:li + 1], lmx_f[:, li:li + 1], sc, ALU.max,
           reads=(skey, lane_key), writes=(lane_key,))

    def lane_merged(lane_i, lane_nm):
        return lmx_f[:, lane_i * 2:lane_i * 2 + 1], (lmx.key(f"{lane_nm}0"),)

    pump(gen_g, 99)
    for hf in range(2):
        for c in range(CT):
            dmaT(wlhs_v[:, c, :], wq_g[hf * FH:(hf + 1) * FH,
                                       c * 128:(c + 1) * 128],
                 reads=("wqg:all",) + ((wlhs.key(f"mmg{hf - 1}"),)
                                       if hf else ()),
                 writes=(wlhs.key(f"g{c}"),))
        for ot in range(hf * 12, hf * 12 + 12):
            lo = (ot - hf * 12) * 128
            for k in range(CT):
                for t2 in range(NTT):
                    bank = t2 + (ot % 2) * 4
                    mm(ps[bank].ap(), wlhs_v[:, k, lo:lo + 128],
                       xqf_v[:, k, t2 * TT:(t2 + 1) * TT],
                       start=(k == 0), stop=(k == CT - 1),
                       reads=(wlhs.key(f"g{k}"), XQK[k]), writes=(PSK[bank],))
            for t2 in range(NTT):
                bank = t2 + (ot % 2) * 4
                act("act", gate_v[:, ot, t2 * TT:(t2 + 1) * TT],
                    ps[bank].ap(), AF.Identity, scale=scv(SX),
                    reads=(PSK[bank], kx),
                    writes=gatest.wfirst(f"{ot}") if ot == 0 and t2 == 0
                    else (gatest.key(f"{ot}"),))
            absmax8_lane(gate_v[:, ot, :], gatest.key(f"{ot}"), 0, "g",
                         ot % 2)
            pump(gen_u, 3)
        p.state[wlhs.key(f"mmg{hf}")] = [[
            p.state[wlhs.key(f"g{c}")][0] for c in range(CT)], []]

    pump(gen_u, 99)
    gap, gk_ = lane_merged(0, "g")
    kg = global_scalar_max("mx1", gap, gk_, 2)
    SG, IG, SU, IU = 2, 3, 4, 5
    SGC = 8
    ts("dve", scv(SGC), scv(SG), 1.0 / 65536.0, None, ALU.mult, None,
       reads=(kg,), writes=(scal.key("sgc"),))

    QT = TH // 4
    NU = FOT * 4
    cscr = ar.alloc("cscr", 12 * QT * 4)
    cpl = cscr.view(F32, [12, QT])

    def cpf(s, i):
        return cpl[:, s * 3 + i, :]

    def cpiv(s, i):
        return cscr.view(I32, [12, QT])[:, s * 3 + i, :]

    def gvh(ot, q):
        return gate_v[:, ot, q * QT:(q + 1) * QT]

    def g_A(u):
        ot, q = divmod(u, 4)
        s = u % 4
        P = lambda n: cscr.key(f"P{s}{n}")
        ts("dve", cpiv(s, 0), gvh(ot, q), scv(IG), None, ALU.mult, None,
           reads=(gatest.key(f"{ot}"), kg),
           writes=(P(0),) if u >= 4 else cscr.wfirst(f"P{s}0"))

    def g_B(u):
        ot, q = divmod(u, 4)
        s = u % 4
        P = lambda n: cscr.key(f"P{s}{n}")
        act("act", cpf(s, 1), cpiv(s, 0), AF.Sigmoid, scale=scv(SG),
            reads=(P(0), kg),
            writes=(P(1),) if u >= 4 else cscr.wfirst(f"P{s}1"))

    def g_C(u):
        ot, q = divmod(u, 4)
        s = u % 4
        P = lambda n: cscr.key(f"P{s}{n}")
        ts("dve", cpiv(s, 2), cpf(s, 1), 65536.0, 65535.0, ALU.mult, ALU.min,
           reads=(P(1),), writes=(P(2),) if u >= 4 else cscr.wfirst(f"P{s}2"))
        stt("dve", gvh(ot, q), cpiv(s, 2), scv(SGC), cpiv(s, 0),
            ALU.mult, ALU.mult,
            reads=(P(2), P(0), scal.key("sgc"), gatest.key(f"{ot}")),
            writes=(gatest.key(f"p2:{ot}:{q}"),))
        absmax8_lane(gvh(ot, q), gatest.key(f"p2:{ot}:{q}"), 2, "p2", u)

    def chain_gen(stages, nu):
        ns = len(stages)
        for t in range(nu + 2 * (ns - 1)):
            for k in range(ns - 1, -1, -1):
                u = t - 2 * k
                if 0 <= u < nu:
                    stages[k](u)
            yield t

    gen_gc = chain_gen([g_A, g_B, g_C], NU)

    upst = ar.alloc("upst", 2 * TH * 2)
    up_sv2 = upst.view(BF16, [2, TH])
    for hf in range(2):
        for c in range(CT):
            dmaT(wlhs_v[:, c, :], wq_u[hf * FH:(hf + 1) * FH,
                                       c * 128:(c + 1) * 128],
                 reads=("wqu:all",) + ((wlhs.key(f"mmu{hf - 1}"),)
                                       if hf else (wlhs.key("mmg1"),)),
                 writes=(wlhs.key(f"u{c}"), wlhs.key(f"g{c}")))
        for ot in range(hf * 12, hf * 12 + 12):
            lo = (ot - hf * 12) * 128
            for k in range(CT):
                for t2 in range(NTT):
                    bank = t2 + (ot % 2) * 4
                    mm(ps[bank].ap(), wlhs_v[:, k, lo:lo + 128],
                       xqf_v[:, k, t2 * TT:(t2 + 1) * TT],
                       start=(k == 0), stop=(k == CT - 1),
                       reads=(wlhs.key(f"u{k}"), XQK[k]), writes=(PSK[bank],))
            sb = ot % 2
            up_sv = up_sv2[:, sb, :]
            for t2 in range(NTT):
                bank = t2 + (ot % 2) * 4
                act("act", up_sv[:, t2 * TT:(t2 + 1) * TT],
                    ps[bank].ap(), AF.Identity, scale=scv(SX),
                    reads=(PSK[bank], kx),
                    writes=upst.wfirst(f"v{sb}") if ot < 2 and t2 == 0
                    else (upst.key(f"v{sb}"),))
            absmax8_lane(up_sv, upst.key(f"v{sb}"), 1, "u", ot % 2)
            pump(gen_d, 4)
            pump(gen_gc, 4)
            dma(up_dram[ot], up_sv, reads=(upst.key(f"v{sb}"),),
                writes=(f"up_dram:{ot}",))
            p.state[upst.key(f"v{sb}")][1].append(
                p.state[f"up_dram:{ot}"][0])
        p.state[wlhs.key(f"mmu{hf}")] = [[
            p.state[wlhs.key(f"u{c}")][0] for c in range(CT)], []]
    p.merge_deps("up_dram:all", [f"up_dram:{ot}" for ot in range(FOT)])
    pump(gen_d, 99)
    pump(gen_gc, 99)

    uap, uk_ = lane_merged(1, "u")
    ku = global_scalar_max("mx2", uap, uk_, 4)

    for bb in (xqf, upst, wlhs, deq):
        ar.free(bb)

    p2ap, p2k_ = lane_merged(2, "p2")
    kg2 = global_scalar_max("mx3", p2ap, p2k_, 6)
    SG2, IG2 = 6, 7
    SGU = 12
    tt("dve", scv(SGU), scv(SG2), scv(SU), ALU.mult,
       reads=(kg2, ku), writes=(scal.key("sgu"),))

    upld = ar.alloc("upld", 6 * TH * 2)
    upld_v = upld.view(BF16, [6, TH])
    wdl = [ar.alloc(f"wdl{i}", (FOT // 4) * D * 2) for i in range(4)]

    def wdl_v(c):
        return wdl[c // 6].view(BF16, [FOT // 4, D])[:, c % 6, :]

    def wdl_key(c):
        return wdl[c // 6].key(f"{c}")

    for c in range(FOT):
        dmaT(wdl_v(c), wq_d[:, c * 128:(c + 1) * 128],
             reads=("wqd:all",),
             writes=wdl[c // 6].wfirst(f"{c}") if c % 6 == 0
             else (wdl_key(c),),
             eng="sp" if c % 2 else "act")


    def o_A(u):
        ot, q = divmod(u, 4)
        s = u % 4
        P = lambda n: cscr.key(f"P{s}{n}")
        if q == 0:
            ib = ot % 6
            dma(upld_v[:, ib, :], up_dram[ot], reads=("up_dram:all",),
                writes=(upld.key(f"s{ib}"),) if ot >= 6
                else upld.wfirst(f"s{ib}"))
        ts("dve", cpiv(s, 0), gvh(ot, q), scv(IG2), None, ALU.mult, None,
           reads=(gatest.key(f"p2:{ot}:{q}"), kg2), writes=(P(0),))

    def o_B(u):
        ot, q = divmod(u, 4)
        s = u % 4
        P = lambda n: cscr.key(f"P{s}{n}")
        ts("dve", cpiv(s, 2), upld_v[:, ot % 6, q * QT:(q + 1) * QT],
           scv(IU), None, ALU.mult, None,
           reads=(upld.key(f"s{ot % 6}"), ku), writes=(P(2),))

    def o_C(u):
        ot, q = divmod(u, 4)
        s = u % 4
        P = lambda n: cscr.key(f"P{s}{n}")
        stt("dve", gvh(ot, q), cpiv(s, 0), scv(SGU), cpiv(s, 2),
            ALU.mult, ALU.mult,
            reads=(P(0), P(2), scal.key("sgu"),
                   gatest.key(f"p2:{ot}:{q}")),
            writes=(gatest.key(f"p3:{ot}:{q}"),))
        absmax8_lane(gvh(ot, q), gatest.key(f"p3:{ot}:{q}"), 3, "p3", u)

    for _ in chain_gen([o_A, o_B, o_C], NU):
        pass

    p3ap, p3k_ = lane_merged(3, "p3")
    ko = global_scalar_max("mx4", p3ap, p3k_, 10)
    SO, IO = 10, 11

    for bb in (upld, gkabs):
        ar.free(bb)

    if debug:
        dma(dbg_sc[1, :, 48:64], scal.view(F32, [256])[:, 0:16],
            reads=(kx, kg, ku, kg2, ko, scal.key("sgc")), writes=("dbgscal",))

    def q_s0(u):
        ot, q = divmod(u, 4)
        s = u % 4
        P = lambda n: cscr.key(f"P{s}{n}")
        ts("dve", cpiv(s, 0), gvh(ot, q), scv(IO), None, ALU.mult, None,
           reads=(gatest.key(f"p3:{ot}:{q}"), ko), writes=(P(0),))

    def q_s1(u):
        ot, q = divmod(u, 4)
        s = u % 4
        P = lambda n: cscr.key(f"P{s}{n}")
        cp("dve", gvh(ot, q), cpiv(s, 0),
           reads=(P(0), gatest.key(f"p3:{ot}:{q}")),
           writes=(gatest.key(f"oq:{ot}:{q}"),))

    gen_q = chain_gen([q_s0, q_s1], NU)
    qdone = [-1]

    def q_pump_until(kk):
        while qdone[0] < kk:
            t = next(gen_q, None)
            if t is None:
                qdone[0] = FOT
            else:
                qdone[0] = (t - 5) // 4 if t >= 5 else -1

    dstg = ar.alloc("dstg", 2 * CT * TT * 2)
    for t2 in range(NTT):
        dstg_v = dstg.view(BF16, [2, CT, TT])[:, t2 % 2]
        for k in range(FOT):
            if t2 == 0:
                q_pump_until(k)
            rk = (wdl_key(k),) + tuple(
                gatest.key(f"oq:{k}:{q}") for q in range(4))
            for od in range(CT):
                mm(ps[od].ap(), wdl_v(k)[:, od * 128:(od + 1) * 128],
                   gate_v[:, k, t2 * TT:(t2 + 1) * TT],
                   start=(k == 0), stop=(k == FOT - 1),
                   reads=rk, writes=(PSK[od],))
        for od in range(CT):
            act("act", dstg_v[:, od, :], ps[od].ap(), AF.Identity,
                scale=scv(SO),
                reads=(PSK[od], ko),
                writes=dstg.wfirst(f"{t2 % 2}:{od}") if t2 < 2 and od == 0
                else (dstg.key(f"{t2 % 2}:{od}"),))
        dma(ffn_dram[:, t2 * TT:(t2 + 1) * TT]
            .rearrange("(c p) t -> p c t", p=128),
            dstg_v,
            reads=tuple(dstg.key(f"{t2 % 2}:{od}") for od in range(CT)),
            writes=(f"ffn_dram:{t2}",))
        for od in range(CT):
            p.state[dstg.key(f"{t2 % 2}:{od}")][1].append(
                p.state[f"ffn_dram:{t2}"][0])
    p.merge_deps("ffn_dram:all", [f"ffn_dram:{t2}" for t2 in range(NTT)])

    for bb in (gatest_[0], gatest_[1], wdl[0], wdl[1], wdl[2], wdl[3],
               lmx, dstg, cscr):
        ar.free(bb)

    fin = ar.alloc("fin", 1024 * 4 + 1024 * 2 + 1024 * 4)
    fres = fin.view(F32, [1024], byteoff=0)
    fffn = fin.view(BF16, [1024], byteoff=4096)
    fout = fin.view(F32, [1024], byteoff=4096 + 2048)
    YK = []
    for i in range(16):
        rd = (fin.key(f"cons{i - 1}"),) if i else ()
        dma(fres, res_dram[i * 128:(i + 1) * 128, :],
            reads=("res_dram:all",) + rd,
            writes=fin.wfirst(f"x{i}") if i == 0 else (fin.key(f"x{i}"),))
        dmaT(fffn, ffn_dram[:, i * 128:(i + 1) * 128],
             reads=("ffn_dram:all",) + rd,
             writes=(fin.key(f"f{i}"),))
        tt("dve", fout, fres, fffn, ALU.add,
           reads=(fin.key(f"x{i}"), fin.key(f"f{i}")),
           writes=(fin.key(f"o{i}"),))
        dma(yout[i * 128:(i + 1) * 128, :], fout,
            reads=(fin.key(f"o{i}"),), writes=(f"y:{i}",))
        p.state[fin.key(f"cons{i}")] = [[p.state[f"y:{i}"][0]], []]
        YK.append(f"y:{i}")

    p.emit("sp", lambda e: e.nop(), reads=tuple(YK), writes=(), inc=0)

    import contextlib
    semnames = sorted({it[3] for it in p.items}
                      | {s for it in p.items for s in it[1]})
    with contextlib.ExitStack() as stx:
        sems = {s: stx.enter_context(nc.semaphore(s)) for s in semnames}
        block = stx.enter_context(nc.Block())
        p.replay(block, sems)
    build_nc.stats = {"n_items": len(p.items),
                      "arena_peak": ar.peak,
                      "sem_totals": dict(p.tot)}
    return nc


_NC_CACHE = []


def kernel(L0_attn_x, L0_attn_time_mix_r, L0_attn_time_mix_k,
           L0_attn_time_mix_v, L0_attn_time_decay, L0_attn_time_first,
           L0_attn_W_r, L0_attn_W_k, L0_attn_W_v, L0_attn_W_o,
           L0_attn_gn_weight, L0_attn_gn_bias,
           L0_ffn_w_gate, L0_ffn_w_up, L0_ffn_w_down):
    f32 = np.float32
    x = np.ascontiguousarray(np.asarray(L0_attn_x, f32))

    def pc(v):
        return np.ascontiguousarray(
            np.asarray(v, f32).reshape(CT, 128).T)

    tm = np.ascontiguousarray(np.stack(
        [pc(L0_attn_time_mix_r), pc(L0_attn_time_mix_k),
         pc(L0_attn_time_mix_v)], axis=0))
    decay = pc(np.asarray(L0_attn_time_decay, f32).reshape(-1))
    tf = pc(np.asarray(L0_attn_time_first, f32).reshape(-1))
    gnw_ = pc(L0_attn_gn_weight)
    gnb_ = pc(L0_attn_gn_bias)
    wrT_ = np.ascontiguousarray(np.asarray(L0_attn_W_r, f32).T)
    wkT_ = np.ascontiguousarray(np.asarray(L0_attn_W_k, f32).T)
    wvT_ = np.ascontiguousarray(np.asarray(L0_attn_W_v, f32).T)
    woT_ = np.ascontiguousarray(np.asarray(L0_attn_W_o, f32).T)
    wg_ = np.ascontiguousarray(np.asarray(L0_ffn_w_gate, f32))
    wu_ = np.ascontiguousarray(np.asarray(L0_ffn_w_up, f32))
    wd_ = np.ascontiguousarray(np.asarray(L0_ffn_w_down, f32))
    ones128 = np.ones((1, 128), f32)
    halfT_ = np.zeros((128, 2), f32)
    halfT_[:64, 0] = 1.0
    halfT_[64:, 1] = 1.0
    halfB_ = np.ascontiguousarray(halfT_.T)

    in_maps = []
    for c in range(NC):
        b, h = c // 2, c % 2
        prev = (np.zeros((1, D), f32) if h == 0
                else x[b, TH - 1:TH, :])
        xs_ = np.ascontiguousarray(
            np.concatenate([prev, x[b, h * TH:(h + 1) * TH, :]], axis=0))
        selv_ = np.zeros((8, 1), f32)
        if h == 1:
            selv_[c - 1, 0] = 1.0
        selp2_ = np.zeros((8, 1), f32)
        selp2_[2 * b, 0] = 1.0
        selp2_[2 * b + 1, 0] = 1.0
        in_maps.append({
            "xs": xs_, "wrT": wrT_, "wkT": wkT_, "wvT": wvT_, "woT": woT_,
            "tm": tm, "decay": decay, "tfirst": tf,
            "gnw": gnw_, "gnb": gnb_,
            "w_gate": wg_, "w_up": wu_, "w_down": wd_,
            "selv": selv_, "selp2": selp2_, "ones128": ones128,
            "halfT": halfT_, "halfB": halfB_,
        })

    if not _NC_CACHE:
        _NC_CACHE.append(build_nc())
    nc = _NC_CACHE[0]
    res = run_bass_kernel_spmd(nc, in_maps, core_ids=list(range(NC)))
    y = np.empty((B, T, D), f32)
    for c in range(NC):
        b, h = c // 2, c % 2
        y[b, h * TH:(h + 1) * TH, :] = res.results[c]["y"]
    return y


if __name__ == "__main__":
    nc = build_nc()
    print("build ok:", build_nc.stats)


# version 29
# speedup vs baseline: 1.1330x; 1.0263x over previous
import numpy as np
import concourse.bass as bass
import concourse.mybir as mybir
from concourse.bass_utils import run_bass_kernel_spmd

F32 = mybir.dt.float32
BF16 = mybir.dt.bfloat16
I32 = mybir.dt.int32
AF = mybir.ActivationFunctionType
ALU = mybir.AluOpType

B, T, D = 4, 4096, 1024
H, KD = 16, 64
NC = 8
TH = T // 2
TT = 512
NTT = TH // TT
CT = D // 128
FI = 3072
FOT = FI // 128
QMAX = 32767.0
EPS = 1e-5

ENG_ATTR = {"pe": "tensor", "dve": "vector", "act": "scalar",
            "pool": "gpsimd", "sp": "sync"}


NDMASEM = 24


class Prog:
    def __init__(self):
        self.items = []
        self.tot = {}
        self.state = {}
        self.wm = {e: {} for e in ENG_ATTR}
        self.bufghosts = {}
        self.dcur = 0
        self.ncoll = 0

    def emit(self, eng, fn, reads=(), writes=(), inc=1, sem=None, defer=False):
        if inc == 16 and sem is None:
            sem = f"d{self.dcur}"
            self.dcur = (self.dcur + 1) % NDMASEM
        sem = sem or eng
        waits = {}
        if inc == 16 and self.tot.get(sem, 0) > 0:
            waits[sem] = self.tot[sem]

        def need(s, v):
            if s == "pe" and eng == "pe":
                return
            if v > self.wm[eng].get(s, 0):
                waits[s] = max(waits.get(s, 0), v)

        def need_w(w):
            if w is None:
                return
            if isinstance(w, list):
                for r in w:
                    need(*r)
            else:
                need(*w)

        def hazard(k):
            st = self.state.get(k)
            if st:
                need_w(st[0])
                for s, v in st[1]:
                    need(s, v)

        for k in reads:
            st = self.state.get(k)
            if st:
                need_w(st[0])
        for k in writes:
            hazard(k)
            if k.startswith("B:"):
                for g in self.bufghosts.get(k.split(":", 2)[1], ()):
                    hazard(g)
        for s, v in waits.items():
            self.wm[eng][s] = v
        if defer:
            ref = (sem, self.tot.get(sem, 0) + 1)
            self.items.append((eng, waits, fn, sem, 0))
        else:
            self.tot[sem] = self.tot.get(sem, 0) + inc
            ref = (sem, self.tot[sem])
            self.items.append((eng, waits, fn, sem, inc))
        for k in reads:
            self.state.setdefault(k, [None, []])[1].append(ref)
        for k in writes:
            self.state[k] = [ref, []]

    def merge_deps(self, newkey, oldkeys):
        refs = []
        for k in oldkeys:
            st = self.state.pop(k, None)
            if st:
                if st[0] is not None:
                    refs.extend(st[0] if isinstance(st[0], list) else [st[0]])
                refs.extend(st[1])
        self.state[newkey] = [refs, []]

    def reads_of(self, key):
        return self.state.get(key)

    def replay(self, block, sems):
        for ename in ("sp", "pe", "dve", "act", "pool"):
            sub = [it for it in self.items if it[0] == ename]

            def runner(engobj, sub=sub):
                for _, waits, fn, sem, inc in sub:
                    for s in sorted(waits):
                        engobj.wait_ge(sems[s], waits[s])
                    ins = fn(engobj)
                    if inc:
                        ins.then_inc(sems[sem], inc)
            getattr(block, ENG_ATTR[ename])(runner)


class Arena:
    def __init__(self, nc, prog, name, nbytes):
        self.t = nc.alloc_sbuf_tensor(name, [128, nbytes], mybir.dt.int8)
        self.prog = prog
        self.free_list = [(0, nbytes)]
        self.owners = {}
        self.ghosts = []
        self.peak = 0

    def alloc(self, name, nbytes, align=64, end=False):
        nbytes = -(-nbytes // align) * align
        fits = [(sz, i) for i, (off, sz) in enumerate(self.free_list)
                if nbytes <= sz]
        for _, i in sorted(fits, reverse=end):
            off, sz = self.free_list[i]
            if end:
                aoff = off + sz - nbytes
                self.free_list[i:i + 1] = (
                    [(off, sz - nbytes)] if sz > nbytes else [])
            else:
                aoff = off
                self.free_list[i:i + 1] = (
                    [(off + nbytes, sz - nbytes)] if sz > nbytes else [])
            self.owners[name] = (aoff, nbytes)
            self.peak = max(self.peak, max(
                o + s for o, s in self.owners.values()))
            self.prog.bufghosts[name] = self.ghost_keys(aoff, nbytes)
            return Buf(self, name, aoff, nbytes)
        raise MemoryError(f"arena alloc {name}: {nbytes} doesn't fit "
                          f"(free={self.free_list})")

    def free(self, buf):
        off, sz = self.owners.pop(buf.name)
        gk = f"ghost:{buf.name}:{off}"
        pref = "B:" + buf.name + ":"
        self.prog.merge_deps(gk, [k for k in list(self.prog.state)
                                  if isinstance(k, str) and k.startswith(pref)])
        self.ghosts.append((off, sz, gk))
        self.free_list.append((off, sz))
        self.free_list.sort()
        merged = []
        for o, s in self.free_list:
            if merged and merged[-1][0] + merged[-1][1] == o:
                merged[-1] = (merged[-1][0], merged[-1][1] + s)
            else:
                merged.append((o, s))
        self.free_list = merged

    def ghost_keys(self, off, size):
        return tuple(gk for o, s, gk in self.ghosts
                     if o < off + size and off < o + s
                     and gk in self.prog.state)


class Buf:
    def __init__(self, arena, name, off, size):
        self.arena, self.name, self.off, self.size = arena, name, off, size

    def view(self, dtype, shape, byteoff=0):
        n = 1
        for s in shape:
            n *= s
        nb = n * (2 if dtype == BF16 else 4)
        assert byteoff + nb <= self.size, (self.name, shape, byteoff, self.size)
        ap = self.arena.t.ap()[:, self.off + byteoff: self.off + byteoff + nb]
        ap = ap.bitcast(dtype)
        if len(shape) > 1:
            ap = ap.rearrange(
                "p (" + " ".join(f"a{i}" for i in range(len(shape))) + ") -> p "
                + " ".join(f"a{i}" for i in range(len(shape))),
                **{f"a{i}": shape[i] for i in range(len(shape))})
        return ap

    def key(self, sub=""):
        return f"B:{self.name}:{sub}"

    def wfirst(self, sub=""):
        return (self.key(sub),) + self.arena.ghost_keys(self.off, self.size)


def build_nc(debug=False):
    nc = bass.Bass("TRN2", target_bir_lowering=False)
    p = Prog()

    def ein(name, shape, dt=F32):
        return nc.dram_tensor(name, shape, dt, kind="ExternalInput").ap()

    def idram(name, shape, dt):
        kind = "ExternalOutput" if debug else "Internal"
        return nc.dram_tensor(name, shape, dt, kind=kind).ap()

    xs = ein("xs", [TH + 1, D])
    wrT = ein("wrT", [D, D]); wkT = ein("wkT", [D, D])
    wvT = ein("wvT", [D, D]); woT = ein("woT", [D, D])
    tm_in = ein("tm", [3, 128, CT])
    decay = ein("decay", [128, CT]); tfirst = ein("tfirst", [128, CT])
    gnw = ein("gnw", [128, CT]); gnb = ein("gnb", [128, CT])
    wg = ein("w_gate", [FI, D]); wu = ein("w_up", [FI, D])
    wd = ein("w_down", [D, FI])
    selv = ein("selv", [8, 1])
    selp2 = ein("selp2", [8, 1])
    ones128 = ein("ones128", [1, 128])
    halfT = ein("halfT", [128, 2])
    halfB = ein("halfB", [2, 128])
    yout = nc.dram_tensor("y", [TH, D], F32, kind="ExternalOutput").ap()

    if debug:
        dbg_wkv = idram("dbg_wkv", [CT, 128, TH], BF16)
        dbg_out = idram("dbg_out", [CT, 128, TH], BF16)
        dbg_sc = idram("dbg_sc", [4, 128, 64], F32)
        dbg_p2 = idram("dbg_p2", [FOT, 128, TH], BF16)
        dbg_p3 = idram("dbg_p3", [FOT, 128, TH], BF16)
    a_dram = idram("a_dram", [CT, 128, TH], BF16)
    b_dram = idram("b_dram", [CT, 128, TH], BF16)
    wq_g = idram("wq_g", [FI, D], BF16)
    wq_u = idram("wq_u", [FI, D], BF16)
    wq_d = idram("wq_d", [D, FI], BF16)
    xln_dram = idram("xln_dram", [TH, D], BF16)
    xn0_dram = idram("xn0_dram", [D], F32)
    att_dram = idram("att_dram", [D, TH], BF16)
    res_dram = idram("res_dram", [TH, D], F32)
    xq_dram = idram("xq_dram", [TH, D], BF16)
    up_dram = idram("up_dram", [FOT, 128, TH], BF16)
    ffn_dram = idram("ffn_dram", [D, TH], BF16)
    carry_red = idram("carry_red", [2048], F32)
    cc_in = {}
    cc_out = {}
    for nm, ishape, oshape in (("carry", [1, 2048], [8, 2048]),
                               ("gnst", [1, 32], [8, 32]),
                               ("mx0", [1, 4], [8, 4]),
                               ("mx1", [1, 4], [8, 4]),
                               ("mx2", [1, 4], [8, 4]),
                               ("mx3", [1, 4], [8, 4]),
                               ("mx4", [1, 4], [8, 4])):
        cc_in[nm] = nc.dram_tensor(f"cc_{nm}_in", ishape, F32,
                                   kind="Internal").ap()
        cc_out[nm] = nc.dram_tensor(f"cc_{nm}_out", oshape, F32,
                                    kind="Internal").ap()

    SBB = 206 * 1024
    ar = Arena(nc, p, "arena", SBB)
    ps = [nc.alloc_psum_tensor(f"ps{i}", [128, 512], F32) for i in range(8)]
    PSK = [f"psum{i}" for i in range(8)]

    def dma(dst, src, reads, writes, eng="sp", slow=False):
        p.emit(eng, lambda e, d=dst, s=src, sl=slow: e.dma_start(
            out=d, in_=s, allow_slow_non_contiguous=sl),
               reads=reads, writes=writes, inc=16)

    def dmaT(dst, src, reads, writes, eng="sp"):
        p.emit(eng, lambda e, d=dst, s=src: e.dma_start_transpose(out=d, in_=s),
               reads=reads, writes=writes, inc=16)

    def ts(eng, out, in0, s1, s2, op0, op1, reads, writes, accum=None):
        if s2 is None:
            p.emit(eng, lambda e, o=out, i=in0, a=s1, oa=op0, ac=accum:
                   e.tensor_scalar(o, i, a, None, oa, accum_out=ac),
                   reads=reads, writes=writes)
        else:
            p.emit(eng, lambda e, o=out, i=in0, a=s1, b=s2, oa=op0, ob=op1,
                   ac=accum:
                   e.tensor_scalar(o, i, a, b, oa, ob, accum_out=ac),
                   reads=reads, writes=writes)

    def tt(eng, out, in0, in1, op, reads, writes):
        p.emit(eng, lambda e, o=out, a=in0, b=in1, oo=op:
               e.tensor_tensor(o, a, b, oo), reads=reads, writes=writes)

    def stt(eng, out, in0, s, in1, op0, op1, reads, writes, accum=None):
        p.emit(eng, lambda e, o=out, a=in0, sc=s, b=in1, oa=op0, ob=op1,
               ac=accum:
               e.scalar_tensor_tensor(o, a, sc, b, oa, ob, accum_out=ac),
               reads=reads, writes=writes)

    def act(eng, out, in_, func, reads, writes, bias=0.0, scale=1.0,
            accum=None):
        p.emit(eng, lambda e, o=out, i=in_, f=func, b=bias, s=scale, ac=accum:
               e.activation(o, i, f, bias=b, scale=s, accum_out=ac),
               reads=reads, writes=writes)

    def cp(eng, out, in_, reads, writes):
        p.emit(eng, lambda e, o=out, i=in_: e.tensor_copy(out=o, in_=i),
               reads=reads, writes=writes)

    def red(eng, out, in_, op, reads, writes, absval=None):
        p.emit(eng, lambda e, o=out, i=in_, oo=op, av=absval:
               e.tensor_reduce(o, i, mybir.AxisListType.X, oo,
                               apply_absolute_value=av),
               reads=reads, writes=writes)

    def mm(out, lhsT, rhs, start, stop, reads, writes, defer=True):
        p.emit("pe", lambda e, o=out, l=lhsT, r=rhs, st=start, sp_=stop:
               e.matmul(o, l, r, start=st, stop=sp_),
               reads=reads, writes=writes, defer=(defer and not stop))

    def scan(eng, out, d0, d1, init, reads, writes):
        p.emit(eng, lambda e, o=out, a=d0, b=d1, i=init:
               e.tensor_tensor_scan(o, a, b, i, ALU.mult, ALU.add),
               reads=reads, writes=writes)

    def coll(nm, reads):
        p.ncoll += 1
        p.emit("pool", lambda e, i=cc_in[nm], o=cc_out[nm]:
               e.collective_compute("AllGather", ALU.bypass,
                                    replica_groups=[list(range(NC))],
                                    ins=[i.opt()], outs=[o.opt()]),
               reads=reads + (f"ccin:{nm}",), writes=(f"ccout:{nm}",),
               inc=1, sem=f"cc{p.ncoll}")

    small = ar.alloc("small", 12 * CT * 4)
    SM = {"w": 0, "eu": 1, "tmr": 2, "tmk": 3, "tmv": 4, "gnw": 5, "gnb": 6,
          "A": 7, "Bb": 8, "t0": 9, "t1": 10, "t2": 11}

    def sm(nm):
        i = SM[nm]
        return small.view(F32, [12, CT])[:, i, :]

    consts = ar.alloc("consts", 2048)
    ones_sb = consts.view(F32, [128], byteoff=0)[:1, :]
    halfT_sb = consts.view(F32, [2], byteoff=512)
    halfB_sb = consts.view(F32, [128], byteoff=1024)[:2, :]
    selv_sb = consts.view(F32, [1], byteoff=1536)[:8, :]
    selp2_sb = consts.view(F32, [1], byteoff=1600)[:8, :]
    scal = ar.alloc("scal", 1024)

    def scv(i):
        return scal.view(F32, [256])[:, i:i + 1]

    mxrow = ar.alloc("mxrow", 1024)
    epsb = ar.alloc("epsb", 64)
    epsv = epsb.view(F32, [1])
    p.emit("pool", lambda e, o=epsv: e.memset(o, EPS),
           reads=(), writes=(epsb.key("v"),))
    EPSK = (epsb.key("v"),)

    prm = ar.alloc("prm", 2 * CT * 4)
    prm_v = prm.view(F32, [2, CT])
    dma(prm_v[:, 0, :], decay, reads=(), writes=prm.wfirst("decay"))
    dma(prm_v[:, 1, :], tfirst, reads=(), writes=(prm.key("tf"),))
    act("act", sm("t0"), prm_v[:, 0, :], AF.Exp,
        reads=(prm.key("decay"),), writes=small.wfirst("e1"))
    act("act", sm("w"), sm("t0"), AF.Exp, scale=-1.0,
        reads=(small.key("e1"),), writes=(small.key("w"),))
    act("act", sm("eu"), prm_v[:, 1, :], AF.Exp,
        reads=(prm.key("tf"),), writes=(small.key("eu"),))
    dma(small.view(F32, [12, CT])[:, 2:5, :],
        tm_in.rearrange("a p c -> p a c"),
        reads=(), writes=(small.key("tmr"), small.key("tmk"),
                          small.key("tmv")))
    dma(sm("gnw"), gnw, reads=(), writes=(small.key("gnw"),))
    dma(sm("gnb"), gnb, reads=(), writes=(small.key("gnb"),))
    dma(ones_sb, ones128, reads=(), writes=consts.wfirst("ones"))
    dma(halfT_sb, halfT, reads=(), writes=(consts.key("halfT"),))
    dma(halfB_sb, halfB, reads=(), writes=(consts.key("halfB"),))
    dma(selv_sb, selv, reads=(), writes=(consts.key("selv"),))
    dma(selp2_sb, selp2, reads=(), writes=(consts.key("selp2"),))

    rstore = ar.alloc("rstore", CT * TH * 2)
    rst_v = rstore.view(BF16, [CT, TH])
    wbuf = ar.alloc("wbuf", 3 * CT * D * 2)
    wb_v = wbuf.view(BF16, [3, CT, D])
    wtmp = ar.alloc("wtmp", CT * D * 4)
    for pr, src in enumerate((wrT, wkT, wvT)):
        wtf = wtmp.view(F32, [CT, D])
        dma(wtf, src.rearrange("(c p) d -> p c d", p=128),
            reads=tuple(wtmp.key(f"r{pr - 1}:{c}") for c in range(CT))
            if pr else (),
            writes=wtmp.wfirst(f"w{pr}") if pr == 0 else (wtmp.key(f"w{pr}"),),
            eng="act")
        for c in range(CT):
            cp("dve" if c % 2 == 0 else "pool", wb_v[:, pr, c, :],
               wtf[:, c, :],
               reads=(wtmp.key(f"w{pr}"),),
               writes=wbuf.wfirst(f"{pr}:{c}") if pr == 0 and c == 0
               else (wbuf.key(f"{pr}:{c}"),))
            p.state[wtmp.key(f"r{pr}:{c}")] = [
                p.state[wbuf.key(f"{pr}:{c}")][0], []]

    ar.free(wtmp)
    stats = ar.alloc("stats", 4096)
    XNW = 528
    xnf = [ar.alloc(f"xnf{i}", CT * XNW * 2) for i in range(2)]

    prow = ar.alloc("prow", 4096 + 64)
    prow_f = prow.view(F32, [1024])[:1, :]
    prow_s = prow.view(F32, [16], byteoff=4096)[:1, :]
    dma(prow_f, xs[0:1, :], reads=(), writes=prow.wfirst("x"))
    red("dve", prow_s[:, 0:1], prow_f, ALU.add,
        reads=(prow.key("x"),), writes=(prow.key("sum"),))
    act("act", prow.view(F32, [1024], byteoff=0)[:1, :], prow_f, AF.Square,
        reads=(prow.key("x"),), writes=(prow.key("x"),),
        accum=prow_s[:, 1:2])
    p.state[prow.key("sq")] = [p.state[prow.key("x")][0], []]
    dma(prow_f, xs[0:1, :], reads=(prow.key("sq"),), writes=(prow.key("x2"),))
    ts("dve", prow_s[:, 2:3], prow_s[:, 0:1], 1.0 / D, None, ALU.mult, None,
       reads=(prow.key("sum"),), writes=(prow.key("mean"),))
    act("act", prow_s[:, 5:6], prow_s[:, 2:3], AF.Square,
        reads=(prow.key("mean"),), writes=(prow.key("m2"),))
    stt("dve", prow_s[:, 3:4], prow_s[:, 1:2], 1.0 / D, prow_s[:, 5:6],
        ALU.mult, ALU.subtract,
        reads=(prow.key("sq"), prow.key("m2")), writes=(prow.key("var"),))
    act("act", prow_s[:, 6:7], prow_s[:, 3:4], AF.Ln, bias=epsv[:1, :],
        reads=(prow.key("var"),) + EPSK, writes=(prow.key("lnv"),))
    act("act", prow_s[:, 7:8], prow_s[:, 6:7], AF.Exp, scale=-0.5,
        reads=(prow.key("lnv"),), writes=(prow.key("rstd"),))
    ts("dve", prow_f, prow_f, prow_s[:, 2:3], prow_s[:, 7:8],
       ALU.subtract, ALU.mult,
       reads=(prow.key("x2"), prow.key("mean"), prow.key("rstd")),
       writes=(prow.key("xn0"),))
    dma(xn0_dram[None, :], prow_f, reads=(prow.key("xn0"),),
        writes=("xn0_dram",))
    x0tmp = stats.view(F32, [64])[:, 32:40]
    dma(x0tmp, xn0_dram.rearrange("(c p) -> p c", p=128),
        reads=("xn0_dram",), writes=(stats.key("xn0"),), slow=True)
    xnfv = [xnf[i].view(BF16, [CT, XNW]) for i in range(2)]
    cp("dve", xnfv[0][:, :, 15], x0tmp,
       reads=(stats.key("xn0"),), writes=xnf[0].wfirst("prevcol"))
    ar.free(prow)

    xtok = ar.alloc("xtok", 4 * 1024 * 4)
    xln_t = ar.alloc("xln_t", 4 * 1024 * 2)
    sqs = ar.alloc("sqs", 1024 * 4)
    mixb = ar.alloc("mixb", 2 * 3 * CT * TT * 2)
    dscr = ar.alloc("dscr", CT * TT * 2)
    abst = ar.alloc("abst", 2 * CT * TT * 2)
    asp = abst.view(BF16, [2, CT, TT])

    def stat_v(i):
        return stats.view(F32, [64])[:, i * 4:(i + 1) * 4]

    def a1_S0(ti):
        xkey = xtok.key(f"t{ti}")
        dma(xtok.view(F32, [4, 1024]),
            xs[1 + ti * TT: 1 + (ti + 1) * TT, :]
            .rearrange("(a p) d -> p a d", p=128),
            reads=(xtok.key(f"done{ti - 1}"),) if ti else (),
            writes=xtok.wfirst(f"t{ti}") if ti == 0 else (xkey,))

    def a1_S1(ti):
        xkey = xtok.key(f"t{ti}")
        xl_done = []
        for pt in range(4):
            sv = stat_v(pt)
            xv = xtok.view(F32, [4, 1024])[:, pt, :]
            red("dve", sv[:, 0:1], xv, ALU.add,
                reads=(xkey,), writes=(stats.key(f"s{pt}"),))
            act("act", sqs.view(F32, [1024]), xv, AF.Square,
                reads=(xkey,), writes=(sqs.key("w"),) if ti or pt
                else sqs.wfirst("w"), accum=sv[:, 1:2])
            p.state[stats.key(f"q{pt}")] = [p.state[sqs.key("w")][0], []]
            ts("dve", sv[:, 2:3], sv[:, 0:1], 1.0 / D, None, ALU.mult, None,
               reads=(stats.key(f"s{pt}"),), writes=(stats.key(f"m{pt}"),))
            act("act", sv[:, 0:1], sv[:, 2:3], AF.Square,
                reads=(stats.key(f"m{pt}"),), writes=(stats.key(f"mm{pt}"),))
            stt("dve", sv[:, 1:2], sv[:, 1:2], 1.0 / D, sv[:, 0:1],
                ALU.mult, ALU.subtract,
                reads=(stats.key(f"q{pt}"), stats.key(f"mm{pt}")),
                writes=(stats.key(f"v{pt}"),))
            act("act", sv[:, 1:2], sv[:, 1:2], AF.Ln, bias=epsv,
                reads=(stats.key(f"v{pt}"),) + EPSK, writes=(stats.key(f"lv{pt}"),))
            act("act", sv[:, 1:2], sv[:, 1:2], AF.Exp, scale=-0.5,
                reads=(stats.key(f"lv{pt}"),), writes=(stats.key(f"r{pt}"),))
            ts("dve", xln_t.view(BF16, [4, 1024])[:, pt, :], xv,
               sv[:, 2:3], sv[:, 1:2], ALU.subtract, ALU.mult,
               reads=(xkey, stats.key(f"m{pt}"), stats.key(f"r{pt}")),
               writes=xln_t.wfirst(f"{pt}") if ti == 0 and pt == 0
               else (xln_t.key(f"{pt}"),))
            xl_done.append(p.state[xln_t.key(f"{pt}")][0])
        p.state[xtok.key(f"done{ti}")] = [list(xl_done), []]
        dma(xln_dram[ti * TT:(ti + 1) * TT, :]
            .rearrange("(a p) d -> p a d", p=128),
            xln_t.view(BF16, [4, 1024]),
            reads=tuple(xln_t.key(f"{pt}") for pt in range(4)),
            writes=(f"xln_dram:{ti}",))
        ib = ti % 2
        for c in range(CT):
            dmaT(xnfv[ib][:, c, 16:16 + TT],
                 xln_dram[ti * TT:(ti + 1) * TT, c * 128:(c + 1) * 128],
                 reads=(f"xln_dram:{ti}",) + ((xnf[ib].key(f"used{ti - 2}"),)
                                              if ti >= 2 else ()),
                 writes=xnf[ib].wfirst(f"d:{c}") if ti < 2 and c == 0
                 else (xnf[ib].key(f"d:{c}"),))

    def a1_S2(ti):
        ib = ti % 2
        xnk = tuple(xnf[ib].key(f"d:{c}") for c in range(CT))
        prevk = (xnf[ib].key("prevcol"),)
        sb = ti % 2
        dsv = dscr.view(BF16, [CT, TT])
        mixv = mixb.view(BF16, [2, 3, CT, TT])[:, sb]
        for c in range(CT):
            tt("dve" if c % 2 else "pool", dsv[:, c, :],
               xnfv[ib][:, c, 16:16 + TT], xnfv[ib][:, c, 15:15 + TT],
               ALU.subtract,
               reads=(xnk[c],) + prevk,
               writes=dscr.wfirst(f"{c}") if ti == 0 and c == 0
               else (dscr.key(f"{c}"),))
        for pr in range(3):
            tmk_ = small.key(("tmr", "tmk", "tmv")[pr])
            tmv_ = small.view(F32, [12, CT])[:, 2 + pr, :]
            for c in range(CT):
                stt("dve", mixv[:, pr, c, :],
                    dsv[:, c, :], tmv_[:, c:c + 1],
                    xnfv[ib][:, c, 15:15 + TT], ALU.mult, ALU.add,
                    reads=(dscr.key(f"{c}"), tmk_, xnk[c]) + prevk,
                    writes=mixb.wfirst(f"{sb}:0:0")
                    if ti < 2 and pr == 0 and c == 0
                    else (mixb.key(f"{sb}:{pr}:{c}"),))
        ob = (ti + 1) % 2
        cp("dve", xnfv[ob][:, :, 15], xnfv[ib][:, :, 16 + TT - 1],
           reads=xnk, writes=(xnf[ob].key("prevcol"),)
           if ti > 0 else xnf[ob].wfirst("prevcol"))
        p.state[xnf[ib].key(f"used{ti}")] = [[
            p.state[mixb.key(f"{sb}:{pr}:{c}")][0]
            for pr in range(3) for c in range(CT)], []]

    def a1_S3(ti):
        ib = ti % 2
        sb = ti % 2
        mixv = mixb.view(BF16, [2, 3, CT, TT])[:, sb]
        for pr in range(3):
            for co in range(CT):
                bank = (pr * CT + co) % 8
                pk = PSK[bank]
                for k in range(CT):
                    mm(ps[bank].ap(), wb_v[:, pr, k, co * 128:(co + 1) * 128],
                       mixv[:, pr, k, :],
                       start=(k == 0), stop=(k == CT - 1),
                       reads=(wbuf.key(f"{pr}:{k}"),
                              mixb.key(f"{sb}:{pr}:{k}")),
                       writes=(pk,))
                if pr == 0:
                    act("act", rst_v[:, co, ti * TT:(ti + 1) * TT],
                        ps[bank].ap(), AF.Identity, reads=(pk,),
                        writes=rstore.wfirst(f"{co}:{ti}")
                        if ti == 0 and co == 0 else (rstore.key(f"{co}:{ti}"),))
                elif pr == 1:
                    act("act", asp[:, 1, co, :], ps[bank].ap(), AF.Exp,
                        reads=(pk,),
                        writes=abst.wfirst(f"b{co}") if ti == 0 and co == 0
                        else (abst.key(f"b{co}"),))
                else:
                    tt("dve", asp[:, 0, co, :], asp[:, 1, co, :],
                       ps[bank].ap(), ALU.mult,
                       reads=(abst.key(f"b{co}"), pk),
                       writes=abst.wfirst(f"a{co}") if ti == 0 and co == 0
                       else (abst.key(f"a{co}"),))
        dma(a_dram[:, :, ti * TT:(ti + 1) * TT].rearrange("c p t -> p c t"),
            asp[:, 0, :, :],
            reads=tuple(abst.key(f"a{c}") for c in range(CT)),
            writes=(f"a_dram:{ti}",))
        dma(b_dram[:, :, ti * TT:(ti + 1) * TT].rearrange("c p t -> p c t"),
            asp[:, 1, :, :],
            reads=tuple(abst.key(f"b{c}") for c in range(CT)),
            writes=(f"b_dram:{ti}",))
        for c in range(CT):
            p.state[abst.key(f"b{c}")][1].append(p.state[f"b_dram:{ti}"][0])
            p.state[abst.key(f"a{c}")][1].append(p.state[f"a_dram:{ti}"][0])

    a1_S0(0)
    a1_S1(0)
    for ti in range(NTT):
        if ti + 1 < NTT:
            a1_S0(ti + 1)
        a1_S2(ti)
        if ti + 1 < NTT:
            a1_S1(ti + 1)
        a1_S3(ti)

    for bb in (xtok, xln_t, sqs, mixb, dscr, xnf[0], xnf[1], abst):
        ar.free(bb)
    p.merge_deps("a_dram:all", [f"a_dram:{i}" for i in range(NTT)])
    p.merge_deps("b_dram:all", [f"b_dram:{i}" for i in range(NTT)])

    DQI = 256
    NDQS = 4
    deq = ar.alloc("deq", NDQS * (DQI * 4 * 3 + DQI * 2 + 512), end=True)
    DSLOT = DQI * 4 * 3 + DQI * 2 + 512

    def dequant_gen(tag, w_src, wq2, O, I):
        nb = DQI // 32
        units = [(oc, ic) for oc in range(O // 128) for ic in range(I // DQI)]
        nuu = len(units)
        wq2v = wq2.rearrange("(o p) i -> o p i", p=128)

        def views(ui):
            s = ui % NDQS
            bo = s * DSLOT
            return (s, bo,
                    deq.view(F32, [DQI], byteoff=bo),
                    deq.view(F32, [DQI], byteoff=bo + DQI * 4),
                    deq.view(I32, [DQI], byteoff=bo + DQI * 8),
                    deq.view(BF16, [DQI], byteoff=bo + DQI * 12),
                    deq.view(F32, [nb * 2], byteoff=bo + DQI * 14))

        def st_L(ui):
            oc, ic = units[ui]
            s, bo, wn, q3, qiw, st_, bm = views(ui)
            K = lambda nm: deq.key(f"{tag}{nm}{s}")
            dma(wn, w_src[oc * 128:(oc + 1) * 128, ic * DQI:(ic + 1) * DQI],
                reads=(K("done"),) if ui >= NDQS else (),
                writes=(K("w"),))

        def st_R(ui):
            s, bo, wn, q3, qiw, st_, bm = views(ui)
            K = lambda nm: deq.key(f"{tag}{nm}{s}")
            red("dve", bm[:, 0:nb], wn.rearrange("p (n b) -> p n b", b=32),
                ALU.max, absval=True,
                reads=(K("w"),), writes=(K("bm"),))
            ts("dve", bm[:, 0:nb], bm[:, 0:nb], 1.0 / 7.0, 1e-12,
               ALU.mult, ALU.max,
               reads=(K("bm"),), writes=(K("sw"),))
            p.emit("dve", lambda e, o=bm[:, nb:2 * nb], i=bm[:, 0:nb]:
                   e.reciprocal(o, i),
                   reads=(K("sw"),), writes=(K("inv"),))

        def st_Q(ui):
            s, bo, wn, q3, qiw, st_, bm = views(ui)
            K = lambda nm: deq.key(f"{tag}{nm}{s}")
            tt("pool", q3.rearrange("p (n b) -> p n b", b=32),
               wn.rearrange("p (n b) -> p n b", b=32),
               bm[:, nb:2 * nb].rearrange("p n -> p n ()")
               .broadcast_to([128, nb, 32]), ALU.mult,
               reads=(K("w"), K("inv")), writes=(K("q"),))

        def st_S(ui):
            oc, ic = units[ui]
            s, bo, wn, q3, qiw, st_, bm = views(ui)
            K = lambda nm: deq.key(f"{tag}{nm}{s}")
            cp("dve", qiw, q3, reads=(K("q"),), writes=(K("qi"),))
            tt("dve", st_.rearrange("p (n b) -> p n b", b=32),
               qiw.rearrange("p (n b) -> p n b", b=32),
               bm[:, 0:nb].rearrange("p n -> p n ()")
               .broadcast_to([128, nb, 32]), ALU.mult,
               reads=(K("qi"), K("sw")), writes=(K("st"),))
            dma(wq2v[oc, :, ic * DQI:(ic + 1) * DQI], st_,
                reads=(K("st"),), writes=(f"wq{tag}:{oc}:{ic}",))
            p.state[K("done")] = [[
                p.state[f"wq{tag}:{oc}:{ic}"][0],
                p.state[K("st")][0]], []]

        STG = [st_S, st_Q, st_R, st_L]
        for t in range(nuu + 3):
            for k, fn in enumerate(STG):
                u = t - (3 - k)
                if 0 <= u < nuu:
                    fn(u)
            yield
        p.merge_deps(f"wq{tag}:all",
                     [f"wq{tag}:{oc}:{ic}" for oc in range(O // 128)
                      for ic in range(I // DQI)])

    gen_g = dequant_gen("g", wg, wq_g, FI, D)
    gen_u = dequant_gen("u", wu, wq_u, FI, D)
    gen_d = dequant_gen("d", wd, wq_d, D, FI)

    def pump(gg, n=1):
        for _ in range(n):
            if next(gg, "END") == "END":
                break

    ar.free(wbuf)
    ab2 = [ar.alloc(f"ab2_{i}", 2 * TH * 2) for i in range(3)]
    numb_ = [ar.alloc(f"numb{i}", (TH + 16) * 4) for i in range(2)]
    denb_ = [ar.alloc(f"denb{i}", (TH + 16) * 4) for i in range(2)]
    carp = ar.alloc("carp", 2 * CT * 4)
    outst = ar.alloc("outst", CT * TH * 2)
    out_v = outst.view(BF16, [CT, TH])
    scr = ar.alloc("scr", 6 * TH * 4)

    def scrv(i):
        return scr.view(F32, [6, TH])[:, i, :]

    def plk(i):
        return scr.key(f"pl{i}")

    wv_ = sm("w")
    euv = sm("eu")

    def ld_ab(ct, phase):
        ib = ct % 3
        v = ab2[ib].view(BF16, [2, TH])
        prior = ab2[ib].key(f"used:{phase}:{ct - 3}") if ct >= 3 else (
            ab2[ib].key(f"used:p1:{max(c for c in range(CT) if c % 3 == ib)}")
            if phase == "p2" else None)
        rd = ("a_dram:all", "b_dram:all") + ((prior,) if prior else ())
        first = (phase == "p1" and ct < 3)
        dma(v[:, 0, :], a_dram[ct], reads=rd,
            writes=ab2[ib].wfirst(f"a:{ct}") if first
            else (ab2[ib].key(f"a:{ct}"),))
        dma(v[:, 1, :], b_dram[ct], reads=rd,
            writes=ab2[ib].wfirst(f"b:{ct}") if first
            else (ab2[ib].key(f"b:{ct}"),))
        return v, ab2[ib].key(f"a:{ct}"), ab2[ib].key(f"b:{ct}")

    cv = carp.view(F32, [2 * CT])
    for ct in range(CT):
        v, ak, bk = ld_ab(ct, "p1")
        numb, denb = numb_[ct % 2], denb_[ct % 2]
        nv = numb.view(F32, [TH + 16])
        dv = denb.view(F32, [TH + 16])
        scan("dve", nv[:, 16:16 + TH],
             wv_[:, ct:ct + 1].broadcast_to([128, TH]), v[:, 0, :], 0.0,
             reads=(ak, small.key("w")),
             writes=numb.wfirst("s") if ct == 0 else (numb.key("s"),))
        scan("dve", dv[:, 16:16 + TH],
             wv_[:, ct:ct + 1].broadcast_to([128, TH]), v[:, 1, :], 0.0,
             reads=(bk, small.key("w")),
             writes=denb.wfirst("s") if ct == 0 else (denb.key("s"),))
        cp("dve", cv[:, ct:ct + 1], nv[:, 16 + TH - 1:16 + TH],
           reads=(numb.key("s"),),
           writes=carp.wfirst(f"n{ct}") if ct == 0 else (carp.key(f"n{ct}"),))
        cp("pool", cv[:, CT + ct:CT + ct + 1], dv[:, 16 + TH - 1:16 + TH],
           reads=(denb.key("s"),), writes=(carp.key(f"d{ct}"),))
        p.state[ab2[ct % 3].key(f"used:p1:{ct}")] = [[
            p.state[numb.key("s")][0], p.state[denb.key("s")][0]], []]

    CARPK = tuple(carp.key(f"{sd}{c}") for sd in "nd" for c in range(CT))
    if debug:
        dma(dbg_sc[1, :, 32:48], cv, reads=CARPK, writes=("dbgsc1c",))
    dma(cc_in["carry"].rearrange("o (p c) -> (o p) c", p=128),
        cv, reads=CARPK, writes=("ccin:carry",))
    coll("carry", reads=())
    gsb = None
    gsb_v = scr.view(F32, [6, TH])[:8, 4, :]
    dma(gsb_v, cc_out["carry"], reads=("ccout:carry",),
        writes=scr.wfirst("pl4"))
    cirow = None
    cirow_v = scr.view(F32, [6, TH])[:1, 5, :]
    for j in range(4):
        mm(ps[4].ap()[:1, :], selv_sb, gsb_v[:, j * 512:(j + 1) * 512],
           start=True, stop=True,
           reads=(consts.key("selv"), plk(4)), writes=(PSK[4],),
           defer=False)
        cp("dve", cirow_v[:, j * 512:(j + 1) * 512], ps[4].ap()[:1, :],
           reads=(PSK[4],),
           writes=(plk(5),) if j else scr.wfirst("pl5"))
    dma(carry_red[None, :], cirow_v, reads=(plk(5),),
        writes=("carry_red",))
    cin2 = ar.alloc("cin2", 2 * CT * 4)
    cin2_v = cin2.view(F32, [2 * CT])
    dma(cin2_v, carry_red.rearrange("(p c) -> p c", p=128),
        reads=("carry_red",), writes=cin2.wfirst("c"))

    stat2 = ar.alloc("stat2", 2 * CT * 4)
    st2v = stat2.view(F32, [2 * CT])

    def a2_T0(ct):
        v, ak, bk = ld_ab(ct, "p2")
        numb, denb = numb_[ct % 2], denb_[ct % 2]
        nv = numb.view(F32, [TH + 16])
        dv = denb.view(F32, [TH + 16])
        s = ct % 2
        cp("dve", nv[:, 15:16], cin2_v[:, ct:ct + 1],
           reads=(cin2.key("c"),), writes=(numb.key("seed"),))
        cp("pool", dv[:, 15:16], cin2_v[:, CT + ct:CT + ct + 1],
           reads=(cin2.key("c"),), writes=(denb.key("seed"),))
        scan("dve", nv[:, 16:16 + TH],
             wv_[:, ct:ct + 1].broadcast_to([128, TH]), v[:, 0, :],
             cin2_v[:, ct:ct + 1],
             reads=(ak, small.key("w"), cin2.key("c")),
             writes=(numb.key("s"),))
        scan("dve", dv[:, 16:16 + TH],
             wv_[:, ct:ct + 1].broadcast_to([128, TH]), v[:, 1, :],
             cin2_v[:, CT + ct:CT + ct + 1],
             reads=(bk, small.key("w"), cin2.key("c")),
             writes=(denb.key("s"),))
        stt("dve", scrv(3 * s), v[:, 0, :], euv[:, ct:ct + 1],
            nv[:, 15:15 + TH], ALU.mult, ALU.add,
            reads=(ak, small.key("eu"), numb.key("s"), numb.key("seed")),
            writes=scr.wfirst(f"pl{3 * s}") if ct < 2 else (plk(3 * s),))
        ts("pool", scrv(3 * s + 1), v[:, 1, :], euv[:, ct:ct + 1], 1e-9,
           ALU.mult, ALU.add,
           reads=(bk, small.key("eu")),
           writes=(plk(3 * s + 1),) if ct >= 2
           else scr.wfirst(f"pl{3 * s + 1}"))
        tt("pool", scrv(3 * s + 1), scrv(3 * s + 1), dv[:, 15:15 + TH],
           ALU.add,
           reads=(plk(3 * s + 1), denb.key("s"), denb.key("seed")),
           writes=(plk(3 * s + 1),))
        p.state[ab2[ct % 3].key(f"used:p2:{ct}")] = [[
            p.state[plk(3 * s)][0], p.state[plk(3 * s + 1)][0]], []]

    def a2_T1(ct):
        s = ct % 2
        act("act", scrv(3 * s + 2), scrv(3 * s + 1), AF.Ln,
            reads=(plk(3 * s + 1),),
            writes=(plk(3 * s + 2),) if ct >= 2
            else scr.wfirst(f"pl{3 * s + 2}"))
        act("act", scrv(3 * s + 2), scrv(3 * s + 2), AF.Exp, scale=-1.0,
            reads=(plk(3 * s + 2),), writes=(plk(3 * s + 2),))

    def a2_T2(ct):
        s = ct % 2
        tt("dve", out_v[:, ct, :], scrv(3 * s), scrv(3 * s + 2), ALU.mult,
           reads=(plk(3 * s), plk(3 * s + 2)),
           writes=outst.wfirst(f"wkv{ct}") if ct == 0
           else (outst.key(f"wkv{ct}"),))
        if debug:
            dma(dbg_wkv[ct], out_v[:, ct, :],
                reads=(outst.key(f"wkv{ct}"),), writes=(f"dbgwkv:{ct}",))

    def a2_T3(ct):
        act("act", rst_v[:, ct, :], rst_v[:, ct, :], AF.Sigmoid,
            reads=tuple(rstore.key(f"{ct}:{i}") for i in range(NTT)),
            writes=(rstore.key(f"sig{ct}"),))

    def a2_T4(ct):
        s = ct % 2
        stt("dve", out_v[:, ct, :], out_v[:, ct, :], 1.0, rst_v[:, ct, :],
            ALU.mult, ALU.mult,
            reads=(outst.key(f"wkv{ct}"), rstore.key(f"sig{ct}")),
            writes=(outst.key(f"o{ct}"),),
            accum=st2v[:, ct:ct + 1])
        p.state[stat2.key(f"s{ct}")] = [p.state[outst.key(f"o{ct}")][0], []]
        act("act", scrv(3 * s + 1), out_v[:, ct, :], AF.Square,
            reads=(outst.key(f"o{ct}"),),
            writes=(plk(3 * s + 1),),
            accum=st2v[:, CT + ct:CT + ct + 1])
        p.state[stat2.key(f"q{ct}")] = [p.state[plk(3 * s + 1)][0], []]

    A2S = [a2_T0, a2_T1, a2_T2, a2_T3, a2_T4]
    for t in range(CT + 4):
        for k in (4, 3, 2, 1, 0):
            u = t - k
            if 0 <= u < CT:
                A2S[k](u)
        pump(gen_g, 4)

    if debug:
        for ct in range(CT):
            dma(dbg_out[ct], out_v[:, ct, :],
                reads=(outst.key(f"o{ct}"),), writes=(f"dbgout:{ct}",))
        dma(dbg_sc[0, :, 0:16], st2v,
            reads=tuple(stat2.key(f"{sd}{c}") for sd in "sq"
                        for c in range(CT)),
            writes=("dbgsc0",))
        dma(dbg_sc[1, :, 0:16], cin2_v,
            reads=(cin2.key("c"),), writes=("dbgsc1",))
        dma(dbg_sc[1, :, 16:32], cv,
            reads=CARPK, writes=("dbgsc1b",))
    STK = tuple(stat2.key(f"{sd}{c}") for sd in "sq" for c in range(CT))
    hsum = ar.alloc("hsum", 256)
    hs_v = hsum.view(F32, [2 * CT])[:2, :]
    mm(ps[5].ap()[:2, 0:2 * CT], halfT_sb, st2v, start=True, stop=True,
       reads=(consts.key("halfT"),) + STK, writes=(PSK[5],), defer=False)
    cp("dve", hs_v, ps[5].ap()[:2, 0:2 * CT],
       reads=(PSK[5],), writes=hsum.wfirst("h"))
    dma(cc_in["gnst"].rearrange("o (h c) -> (o h) c", h=2), hs_v,
        reads=(hsum.key("h"),), writes=("ccin:gnst",))
    coll("gnst", reads=())
    gst = ar.alloc("gst", 2048)
    gst_v = gst.view(F32, [32])[:8, :]
    dma(gst_v, cc_out["gnst"], reads=("ccout:gnst",), writes=gst.wfirst("g"))
    tot_v = gst.view(F32, [32], byteoff=128)[:1, :]
    mm(ps[5].ap()[:1, 0:32], selp2_sb, gst_v, start=True, stop=True,
       reads=(consts.key("selp2"), gst.key("g")), writes=(PSK[5],),
       defer=False)
    cp("dve", tot_v, ps[5].ap()[:1, 0:32],
       reads=(PSK[5],), writes=(gst.key("tot"),))
    NTK = float(T * KD)
    tot2 = gst.view(F32, [2, 2, CT], byteoff=128)[:1, :, :, :]
    mu_v = gst.view(F32, [2, CT], byteoff=256)[:1, :, :]
    rs_v = gst.view(F32, [2, CT], byteoff=320)[:1, :, :]
    ts("dve", mu_v, tot2[:, :, 0, :], 1.0 / NTK, None, ALU.mult, None,
       reads=(gst.key("tot"),), writes=(gst.key("mu"),))
    act("act", rs_v, mu_v, AF.Square,
        reads=(gst.key("mu"),), writes=(gst.key("mu2"),))
    stt("dve", rs_v, tot2[:, :, 1, :], 1.0 / NTK, rs_v,
        ALU.mult, ALU.subtract,
        reads=(gst.key("tot"), gst.key("mu2")), writes=(gst.key("var"),))
    act("act", rs_v, rs_v, AF.Ln, bias=epsv[:1, :],
        reads=(gst.key("var"),) + EPSK, writes=(gst.key("lnv"),))
    act("act", rs_v, rs_v, AF.Exp, scale=-0.5,
        reads=(gst.key("lnv"),), writes=(gst.key("rstd"),))
    mrb = gst.view(F32, [2, 2 * CT], byteoff=384)[:2, :, :]
    for hh in range(2):
        dma(cc_in["gnst"][:, hh * 16:(hh + 1) * 16]
            .rearrange("o (w c) -> o w c", w=2),
            gst.view(F32, [2, 2, CT], byteoff=256)[:1, :, hh, :],
            reads=(gst.key("mu"), gst.key("rstd")),
            writes=("hb_dram", "ccin:gnst") if hh == 0
            else ("hb_dram2", "ccin:gnst"))
    dma(gst.view(F32, [2 * CT], byteoff=384)[:2, :],
        cc_in["gnst"].rearrange("o (h c2) -> (o h) c2", h=2),
        reads=("hb_dram", "hb_dram2"), writes=(gst.key("mrb"),))
    mm(ps[5].ap()[:, 0:2 * CT], halfB_sb,
       gst.view(F32, [2 * CT], byteoff=384)[:2, :],
       start=True, stop=True,
       reads=(consts.key("halfB"), gst.key("mrb")), writes=(PSK[5],),
       defer=False)
    murs = ar.alloc("murs", 2 * CT * 4)
    murs_v = murs.view(F32, [2, CT])
    cp("dve", murs.view(F32, [2 * CT]), ps[5].ap()[:, 0:2 * CT],
       reads=(PSK[5],), writes=murs.wfirst("m"))
    tt("dve", sm("A"), murs_v[:, 1, :], sm("gnw"), ALU.mult,
       reads=(murs.key("m"), small.key("gnw")), writes=(small.key("A"),))
    tt("dve", sm("t1"), murs_v[:, 0, :], sm("A"), ALU.mult,
       reads=(murs.key("m"), small.key("A")), writes=(small.key("muA"),))
    tt("dve", sm("Bb"), sm("gnb"), sm("t1"), ALU.subtract,
       reads=(small.key("gnb"), small.key("muA")), writes=(small.key("Bb"),))
    if debug:
        dma(dbg_sc[2, :, 0:CT * 2], murs.view(F32, [2 * CT]),
            reads=(murs.key("m"),), writes=("dbgsc2",))
        dma(dbg_sc[3, :, 0:CT], sm("A"), reads=(small.key("A"),),
            writes=("dbgsc3",))
        dma(dbg_sc[3, :, CT:2 * CT], sm("Bb"), reads=(small.key("Bb"),),
            writes=("dbgsc4",))
    for ct in range(CT):
        ts("dve", out_v[:, ct, :], out_v[:, ct, :],
           sm("A")[:, ct:ct + 1], sm("Bb")[:, ct:ct + 1], ALU.mult, ALU.add,
           reads=(outst.key(f"o{ct}"), small.key("A"), small.key("Bb")),
           writes=(outst.key(f"y{ct}"),))

    for bb in (ab2[0], ab2[1], ab2[2], numb_[0], numb_[1], denb_[0],
               denb_[1], carp, cin2, hsum, gst, murs, stat2,
               rstore):
        ar.free(bb)
    wob = ar.alloc("wob", CT * D * 2)
    wob_v = wob.view(BF16, [CT, D])
    for hlf in range(2):
        sv = scr.view(F32, [4, 1024])
        rd = () if hlf == 0 else (wob.key("wo:done0"),)
        dma(sv, woT.rearrange("(c p) d -> p c d", p=128)[:, hlf * 4:
                                                         (hlf + 1) * 4, :],
            reads=rd, writes=(plk(0), plk(1)))
        for c in range(4):
            cp("dve" if c % 2 else "pool", wob_v[:, hlf * 4 + c, :],
               sv[:, c, :],
               reads=(plk(0), plk(1)),
               writes=wob.wfirst(f"wo:{hlf * 4 + c}") if hlf == 0 and c == 0
               else (wob.key(f"wo:{hlf * 4 + c}"),))
        p.state[wob.key(f"wo:done{hlf}")] = [[
            p.state[wob.key(f"wo:{hlf * 4 + c}")][0] for c in range(4)], []]

    attst = ar.alloc("attst", 2 * CT * TT * 2)
    for ti in range(NTT):
        sb = ti % 2
        att_sv = attst.view(BF16, [2, CT, TT])[:, sb]
        for co in range(CT):
            bank = co % 8
            pk = PSK[bank]
            for k in range(CT):
                mm(ps[bank].ap(), wob_v[:, k, co * 128:(co + 1) * 128],
                   out_v[:, k, ti * TT:(ti + 1) * TT],
                   start=(k == 0), stop=(k == CT - 1),
                   reads=(wob.key(f"wo:{k}"), outst.key(f"y{k}")),
                   writes=(pk,))
            act("act", att_sv[:, co, :], ps[bank].ap(), AF.Identity,
                reads=(pk,),
                writes=attst.wfirst(f"{sb}:{co}") if ti < 2 and co == 0
                else (attst.key(f"{sb}:{co}"),))
            pump(gen_g, 2)
        dma(att_dram[:, ti * TT:(ti + 1) * TT]
            .rearrange("(c p) t -> p c t", p=128),
            att_sv,
            reads=tuple(attst.key(f"{sb}:{co}") for co in range(CT)),
            writes=(f"att_dram:{ti}",))
        for co in range(CT):
            p.state[attst.key(f"{sb}:{co}")][1].append(
                p.state[f"att_dram:{ti}"][0])
    p.merge_deps("att_dram:all", [f"att_dram:{i}" for i in range(NTT)])

    for bb in (outst, attst, scr, wob):
        ar.free(bb)

    NG = 4
    xres = [ar.alloc(f"xres{s}", 4 * 1024 * 4 + 4 * 1024 * 2) for s in range(2)]
    t1st = ar.alloc("t1st", 16 * 1024 * 2)
    t1_v = t1st.view(BF16, [16, 1024])
    mxacc = ar.alloc("mxacc", 256)
    sqs2 = [ar.alloc(f"sqs2{s}", 1024 * 4) for s in range(2)]
    t1f = [ar.alloc(f"t1f{s}", 1024 * 4) for s in range(2)]
    stats2 = ar.alloc("stats2", 512)

    for g in range(NG):
        s = g % 2
        res_g = xres[s].view(F32, [4, 1024], byteoff=0)
        att_g = xres[s].view(BF16, [4, 1024], byteoff=16384)
        sv = stats2.view(F32, [2, 4, 8])[:, s, :, :]
        K = lambda nm: xres[s].key(nm)
        KS = lambda nm: stats2.key(f"{nm}{s}")
        rd = (K("cons"),) if g >= 2 else ()
        dma(res_g, xs[1 + g * TT:1 + (g + 1) * TT, :]
            .rearrange("(a p) d -> p a d", p=128),
            reads=rd, writes=(K("x"),))
        for pt in range(4):
            i = g * 4 + pt
            dmaT(att_g[:, pt, :], att_dram[:, i * 128:(i + 1) * 128],
                 reads=("att_dram:all",) + rd,
                 writes=(K(f"a{pt}"),))
        tt("dve", res_g, res_g, att_g, ALU.add,
           reads=(K("x"), K("a0"), K("a1"), K("a2"), K("a3")),
           writes=(K("r"),))
        dma(res_dram[g * TT:(g + 1) * TT, :]
            .rearrange("(a p) d -> p a d", p=128), res_g,
            reads=(K("r"),), writes=(f"res_dram:{g}",))
        red("dve", sv[:, :, 0], res_g, ALU.add,
            reads=(K("r"),), writes=(KS("s"),))
        for pt in range(4):
            act("act", sqs2[s].view(F32, [1024]), res_g[:, pt, :], AF.Square,
                reads=(K("r"),), writes=(sqs2[s].key("w"),),
                accum=sv[:, pt, 1:2])
        p.state[KS("q")] = [p.state[sqs2[s].key("w")][0], []]
        ts("dve", sv[:, :, 2], sv[:, :, 0], 1.0 / D, None, ALU.mult, None,
           reads=(KS("s"),), writes=(KS("m"),))
        act("act", sv[:, :, 3], sv[:, :, 2], AF.Square,
            reads=(KS("m"),), writes=(KS("mm"),))
        stt("dve", sv[:, :, 1], sv[:, :, 1], 1.0 / D, sv[:, :, 3],
            ALU.mult, ALU.subtract,
            reads=(KS("q"), KS("mm")), writes=(KS("v"),))
        act("act", sv[:, :, 1], sv[:, :, 1], AF.Ln, bias=epsv,
            reads=(KS("v"),) + EPSK, writes=(KS("lv"),))
        act("act", sv[:, :, 1], sv[:, :, 1], AF.Exp, scale=-0.5,
            reads=(KS("lv"),), writes=(KS("r"),))
        cons = []
        for pt in range(4):
            i = g * 4 + pt
            ts("dve", t1f[s].view(F32, [1024]), res_g[:, pt, :],
               sv[:, pt, 2:3], sv[:, pt, 1:2], ALU.subtract, ALU.mult,
               reads=(K("r"), KS("m"), KS("r")),
               writes=(t1f[s].key("w"),))
            red("dve", mxacc.view(F32, [16])[:, i:i + 1],
                t1f[s].view(F32, [1024]), ALU.max, absval=True,
                reads=(t1f[s].key("w"),), writes=(mxacc.key(f"{i}"),))
            act("act", t1_v[:, i, :], t1f[s].view(F32, [1024]), AF.Identity,
                reads=(t1f[s].key("w"),), writes=(t1st.key(f"{i}"),))
            cons.append(p.state[t1st.key(f"{i}")][0])
            cons.append(p.state[mxacc.key(f"{i}")][0])
        cons.append(p.state[f"res_dram:{g}"][0])
        p.state[K("cons")] = [cons, []]
        pump(gen_u, 8)
    p.merge_deps("res_dram:all", [f"res_dram:{g}" for g in range(NG)])
    for bb in (sqs2[0], sqs2[1], t1f[0], t1f[1], stats2):
        ar.free(bb)

    def global_scalar_max(nm, lane_ap, lane_keys, s_slot):
        mr = mxrow.view(F32, [128])[:1, :]
        m8 = mxrow.view(F32, [8], byteoff=512)[:1, :]
        dma(carry_red[None, 0:128].rearrange("o (p c) -> (o p) c", p=128),
            lane_ap, reads=lane_keys, writes=("carry_red", mxrow.key("b")))
        dma(mr, carry_red[None, 0:128],
            reads=(mxrow.key("b"),), writes=(mxrow.key("l"),))
        p.emit("dve", lambda e, o=m8, i=mr: e.max(o, i),
               reads=(mxrow.key("l"),), writes=(mxrow.key("m8"),))
        dma(cc_in[nm][:, 0:4], m8[:, 0:4],
            reads=(mxrow.key("m8"),), writes=(f"ccin:{nm}",))
        coll(nm, reads=())
        gv = mxrow.view(F32, [32], byteoff=576)[:1, :]
        dma(gv, cc_out[nm], reads=(f"ccout:{nm}",), writes=(mxrow.key("g"),))
        gvr = gv.rearrange("p (r c) -> p r c", c=4)[:, :, 0]
        p.emit("dve", lambda e, o=m8, i=gvr: e.max(o, i),
               reads=(mxrow.key("g"),), writes=(mxrow.key("gm"),))
        sloc = mxrow.view(F32, [4], byteoff=704)[:1, 0:1]
        iloc = mxrow.view(F32, [4], byteoff=704)[:1, 1:2]
        ts("dve", sloc, m8[:, 0:1], 1.0 / QMAX, 1e-12, ALU.mult, ALU.max,
           reads=(mxrow.key("gm"),), writes=(mxrow.key("s"),))
        p.emit("dve", lambda e, o=iloc, i=sloc: e.reciprocal(o, i),
               reads=(mxrow.key("s"),), writes=(mxrow.key("i"),))
        mm(ps[6].ap()[:, 0:2], ones_sb, mxrow.view(F32, [4], byteoff=704)[:1, 0:2],
           start=True, stop=True,
           reads=(consts.key("ones"), mxrow.key("s"), mxrow.key("i")),
           writes=(PSK[6],), defer=False)
        cp("dve", scal.view(F32, [256])[:, s_slot:s_slot + 2],
           ps[6].ap()[:, 0:2],
           reads=(PSK[6],), writes=(scal.key(nm),))
        return scal.key(nm)

    mx1 = ar.alloc("mx1", 64)
    p.emit("dve", lambda e, o=mx1.view(F32, [8])[:, 0:8],
           i=mxacc.view(F32, [16]): e.max(o, i),
           reads=tuple(mxacc.key(f"{i}") for i in range(16)),
           writes=mx1.wfirst("m"))
    kx = global_scalar_max("mx0", mx1.view(F32, [8])[:, 0:1],
                           (mx1.key("m"),), 0)
    SX, IX = 0, 1

    qscr = [ar.alloc(f"qscr{s}", 4 * 1024 * 4 * 2) for s in range(2)]
    xqt = [ar.alloc(f"xqt{s}", 4 * 1024 * 2) for s in range(2)]
    for g in range(NG):
        s = g % 2
        qf = qscr[s].view(F32, [4, 1024], byteoff=0)
        qi = qscr[s].view(I32, [4, 1024], byteoff=16384)
        xq_tv = xqt[s].view(BF16, [4, 1024])
        t1g = t1st.view(BF16, [16, 1024])[:, g * 4:(g + 1) * 4, :]
        rd = tuple(t1st.key(f"{g * 4 + pt}") for pt in range(4)) + (kx,)
        if g >= 2:
            rd = rd + (xqt[s].key("cons"),)
        ts("dve", qf, t1g, scv(IX), None, ALU.mult, None,
           reads=rd, writes=(qscr[s].key("f"),))
        cp("dve", qi, qf, reads=(qscr[s].key("f"),),
           writes=(qscr[s].key("i"),))
        cp("dve", xq_tv, qi, reads=(qscr[s].key("i"),),
           writes=(xqt[s].key("v"),))
        dma(xq_dram[g * TT:(g + 1) * TT, :]
            .rearrange("(a p) d -> p a d", p=128), xq_tv,
            reads=(xqt[s].key("v"),), writes=(f"xq_dram:{g}",))
        p.state[xqt[s].key("cons")] = [[p.state[f"xq_dram:{g}"][0]], []]
        pump(gen_u, 4)
    p.merge_deps("xq_dram:all", [f"xq_dram:{g}" for g in range(NG)])
    for bb in (t1st, mxacc, mx1, qscr[0], qscr[1], xqt[0], xqt[1],
               xres[0], xres[1], stats):
        ar.free(bb)

    lmx = ar.alloc("lmx", 128)
    gkabs = ar.alloc("gkabs", 64)
    xqf = ar.alloc("xqf", CT * TH * 2)
    xqf_v = xqf.view(BF16, [CT, TH])
    for c in range(CT):
        dmaT(xqf_v[:, c, :], xq_dram[:, c * 128:(c + 1) * 128],
             reads=("xq_dram:all",),
             writes=xqf.wfirst(f"{c}") if c == 0 else (xqf.key(f"{c}"),))
    XQK = [xqf.key(f"{c}") for c in range(CT)]

    gatest_ = [ar.alloc(f"gatest{i}", (FOT // 2) * TH * 2) for i in range(2)]

    class _GV:
        def __getitem__(self, idx):
            _, ot, sl = idx
            return gatest_[ot // 12].view(BF16, [FOT // 2, TH])[:, ot % 12, sl]

    gate_v = _GV()

    class _GK:
        def key(self, sub):
            ot = int(sub.split(":")[1]) if ":" in sub else int(sub)
            return gatest_[ot // 12].key(sub)
        def wfirst(self, sub):
            ot = int(sub.split(":")[1]) if ":" in sub else int(sub)
            return gatest_[ot // 12].wfirst(sub)

    gatest = _GK()
    FH = FI // 2
    wlhs = ar.alloc("wlhs", CT * FH * 2)
    wlhs_v = wlhs.view(BF16, [CT, FH])
    lmx_f = lmx.view(F32, [8])
    lmx_m8 = lmx.view(BF16, [8], byteoff=32)

    p.emit("dve", lambda e, o=lmx_f: e.memset(o, 0.0),
           reads=(), writes=lmx.wfirst("g0") + tuple(
               lmx.key(f"{nm}0") for nm in ("u", "p2", "p3")))

    def absmax8_lane(src_ap, src_key, lane_i, lane_nm, slot=0, eng="dve"):
        lane_key = lmx.key(f"{lane_nm}0")
        si = lane_i * 2 + (slot % 2)
        sc = gkabs.view(F32, [8])[:, si:si + 1]
        skey = gkabs.key(f"s{si}")
        red(eng, sc, src_ap, ALU.max, absval=True,
            reads=(src_key,), writes=(skey,))
        li = lane_i * 2
        tt("dve", lmx_f[:, li:li + 1], lmx_f[:, li:li + 1], sc, ALU.max,
           reads=(skey, lane_key), writes=(lane_key,))

    def lane_merged(lane_i, lane_nm):
        return lmx_f[:, lane_i * 2:lane_i * 2 + 1], (lmx.key(f"{lane_nm}0"),)

    pump(gen_g, 99)
    for hf in range(2):
        for c in range(CT):
            dmaT(wlhs_v[:, c, :], wq_g[hf * FH:(hf + 1) * FH,
                                       c * 128:(c + 1) * 128],
                 reads=("wqg:all",) + ((wlhs.key(f"mmg{hf - 1}"),)
                                       if hf else ()),
                 writes=(wlhs.key(f"g{c}"),))
        for ot in range(hf * 12, hf * 12 + 12):
            lo = (ot - hf * 12) * 128
            for k in range(CT):
                for t2 in range(NTT):
                    bank = t2 + (ot % 2) * 4
                    mm(ps[bank].ap(), wlhs_v[:, k, lo:lo + 128],
                       xqf_v[:, k, t2 * TT:(t2 + 1) * TT],
                       start=(k == 0), stop=(k == CT - 1),
                       reads=(wlhs.key(f"g{k}"), XQK[k]), writes=(PSK[bank],))
            for t2 in range(NTT):
                bank = t2 + (ot % 2) * 4
                act("act", gate_v[:, ot, t2 * TT:(t2 + 1) * TT],
                    ps[bank].ap(), AF.Identity, scale=scv(SX),
                    reads=(PSK[bank], kx),
                    writes=gatest.wfirst(f"{ot}") if ot == 0 and t2 == 0
                    else (gatest.key(f"{ot}"),))
            absmax8_lane(gate_v[:, ot, :], gatest.key(f"{ot}"), 0, "g",
                         ot % 2)
            pump(gen_u, 3)
        p.state[wlhs.key(f"mmg{hf}")] = [[
            p.state[wlhs.key(f"g{c}")][0] for c in range(CT)], []]

    pump(gen_u, 99)
    gap, gk_ = lane_merged(0, "g")
    kg = global_scalar_max("mx1", gap, gk_, 2)
    SG, IG, SU, IU = 2, 3, 4, 5
    SGC = 8
    ts("dve", scv(SGC), scv(SG), 1.0 / 65536.0, None, ALU.mult, None,
       reads=(kg,), writes=(scal.key("sgc"),))

    QT = TH // 4
    NU = FOT * 4
    cscr = ar.alloc("cscr", 12 * QT * 4)
    cpl = cscr.view(F32, [12, QT])

    def cpf(s, i):
        return cpl[:, s * 3 + i, :]

    def cpiv(s, i):
        return cscr.view(I32, [12, QT])[:, s * 3 + i, :]

    def gvh(ot, q):
        return gate_v[:, ot, q * QT:(q + 1) * QT]

    def g_A(u):
        ot, q = divmod(u, 4)
        s = u % 4
        P = lambda n: cscr.key(f"P{s}{n}")
        ts("dve", cpiv(s, 0), gvh(ot, q), scv(IG), None, ALU.mult, None,
           reads=(gatest.key(f"{ot}"), kg),
           writes=(P(0),) if u >= 4 else cscr.wfirst(f"P{s}0"))

    def g_B(u):
        ot, q = divmod(u, 4)
        s = u % 4
        P = lambda n: cscr.key(f"P{s}{n}")
        act("act", cpf(s, 1), cpiv(s, 0), AF.Sigmoid, scale=scv(SG),
            reads=(P(0), kg),
            writes=(P(1),) if u >= 4 else cscr.wfirst(f"P{s}1"))

    def g_C(u):
        ot, q = divmod(u, 4)
        s = u % 4
        P = lambda n: cscr.key(f"P{s}{n}")
        ts("dve", cpiv(s, 2), cpf(s, 1), 65536.0, 65535.0, ALU.mult, ALU.min,
           reads=(P(1),), writes=(P(2),) if u >= 4 else cscr.wfirst(f"P{s}2"))
        stt("dve", gvh(ot, q), cpiv(s, 2), scv(SGC), cpiv(s, 0),
            ALU.mult, ALU.mult,
            reads=(P(2), P(0), scal.key("sgc"), gatest.key(f"{ot}")),
            writes=(gatest.key(f"p2:{ot}:{q}"),))
        absmax8_lane(gvh(ot, q), gatest.key(f"p2:{ot}:{q}"), 2, "p2", u)

    def chain_gen(stages, nu):
        ns = len(stages)
        for t in range(nu + 2 * (ns - 1)):
            for k in range(ns - 1, -1, -1):
                u = t - 2 * k
                if 0 <= u < nu:
                    stages[k](u)
            yield t

    gen_gc = chain_gen([g_A, g_B, g_C], NU)

    upst = ar.alloc("upst", 2 * TH * 2)
    up_sv2 = upst.view(BF16, [2, TH])
    for hf in range(2):
        for c in range(CT):
            dmaT(wlhs_v[:, c, :], wq_u[hf * FH:(hf + 1) * FH,
                                       c * 128:(c + 1) * 128],
                 reads=("wqu:all",) + ((wlhs.key(f"mmu{hf - 1}"),)
                                       if hf else (wlhs.key("mmg1"),)),
                 writes=(wlhs.key(f"u{c}"), wlhs.key(f"g{c}")))
        for ot in range(hf * 12, hf * 12 + 12):
            lo = (ot - hf * 12) * 128
            for k in range(CT):
                for t2 in range(NTT):
                    bank = t2 + (ot % 2) * 4
                    mm(ps[bank].ap(), wlhs_v[:, k, lo:lo + 128],
                       xqf_v[:, k, t2 * TT:(t2 + 1) * TT],
                       start=(k == 0), stop=(k == CT - 1),
                       reads=(wlhs.key(f"u{k}"), XQK[k]), writes=(PSK[bank],))
            sb = ot % 2
            up_sv = up_sv2[:, sb, :]
            for t2 in range(NTT):
                bank = t2 + (ot % 2) * 4
                act("act", up_sv[:, t2 * TT:(t2 + 1) * TT],
                    ps[bank].ap(), AF.Identity, scale=scv(SX),
                    reads=(PSK[bank], kx),
                    writes=upst.wfirst(f"v{sb}") if ot < 2 and t2 == 0
                    else (upst.key(f"v{sb}"),))
            absmax8_lane(up_sv, upst.key(f"v{sb}"), 1, "u", ot % 2)
            pump(gen_d, 4)
            pump(gen_gc, 4)
            dma(up_dram[ot], up_sv, reads=(upst.key(f"v{sb}"),),
                writes=(f"up_dram:{ot}",))
            p.state[upst.key(f"v{sb}")][1].append(
                p.state[f"up_dram:{ot}"][0])
        p.state[wlhs.key(f"mmu{hf}")] = [[
            p.state[wlhs.key(f"u{c}")][0] for c in range(CT)], []]
    p.merge_deps("up_dram:all", [f"up_dram:{ot}" for ot in range(FOT)])
    pump(gen_d, 99)
    pump(gen_gc, 99)

    uap, uk_ = lane_merged(1, "u")
    ku = global_scalar_max("mx2", uap, uk_, 4)

    for bb in (xqf, upst, wlhs, deq):
        ar.free(bb)

    p2ap, p2k_ = lane_merged(2, "p2")
    kg2 = global_scalar_max("mx3", p2ap, p2k_, 6)
    SG2, IG2 = 6, 7
    SGU = 12
    tt("dve", scv(SGU), scv(SG2), scv(SU), ALU.mult,
       reads=(kg2, ku), writes=(scal.key("sgu"),))

    upld = ar.alloc("upld", 6 * TH * 2)
    upld_v = upld.view(BF16, [6, TH])
    wdl = [ar.alloc(f"wdl{i}", (FOT // 4) * D * 2) for i in range(4)]

    def wdl_v(c):
        return wdl[c // 6].view(BF16, [FOT // 4, D])[:, c % 6, :]

    def wdl_key(c):
        return wdl[c // 6].key(f"{c}")

    for c in range(FOT):
        dmaT(wdl_v(c), wq_d[:, c * 128:(c + 1) * 128],
             reads=("wqd:all",),
             writes=wdl[c // 6].wfirst(f"{c}") if c % 6 == 0
             else (wdl_key(c),),
             eng="sp" if c % 2 else "act")


    def o_A(u):
        ot, q = divmod(u, 4)
        s = u % 4
        P = lambda n: cscr.key(f"P{s}{n}")
        if q == 0:
            ib = ot % 6
            dma(upld_v[:, ib, :], up_dram[ot], reads=("up_dram:all",),
                writes=(upld.key(f"s{ib}"),) if ot >= 6
                else upld.wfirst(f"s{ib}"))
        ts("dve", cpiv(s, 0), gvh(ot, q), scv(IG2), None, ALU.mult, None,
           reads=(gatest.key(f"p2:{ot}:{q}"), kg2), writes=(P(0),))

    def o_B(u):
        ot, q = divmod(u, 4)
        s = u % 4
        P = lambda n: cscr.key(f"P{s}{n}")
        ts("dve", cpiv(s, 2), upld_v[:, ot % 6, q * QT:(q + 1) * QT],
           scv(IU), None, ALU.mult, None,
           reads=(upld.key(f"s{ot % 6}"), ku), writes=(P(2),))

    def o_C(u):
        ot, q = divmod(u, 4)
        s = u % 4
        P = lambda n: cscr.key(f"P{s}{n}")
        stt("dve", gvh(ot, q), cpiv(s, 0), scv(SGU), cpiv(s, 2),
            ALU.mult, ALU.mult,
            reads=(P(0), P(2), scal.key("sgu"),
                   gatest.key(f"p2:{ot}:{q}")),
            writes=(gatest.key(f"p3:{ot}:{q}"),))
        absmax8_lane(gvh(ot, q), gatest.key(f"p3:{ot}:{q}"), 3, "p3", u)

    for _ in chain_gen([o_A, o_B, o_C], NU):
        pass

    p3ap, p3k_ = lane_merged(3, "p3")
    ko = global_scalar_max("mx4", p3ap, p3k_, 10)
    SO, IO = 10, 11

    for bb in (upld, gkabs):
        ar.free(bb)

    if debug:
        dma(dbg_sc[1, :, 48:64], scal.view(F32, [256])[:, 0:16],
            reads=(kx, kg, ku, kg2, ko, scal.key("sgc")), writes=("dbgscal",))

    def q_s0(u):
        ot, q = divmod(u, 4)
        s = u % 4
        P = lambda n: cscr.key(f"P{s}{n}")
        ts("dve", cpiv(s, 0), gvh(ot, q), scv(IO), None, ALU.mult, None,
           reads=(gatest.key(f"p3:{ot}:{q}"), ko), writes=(P(0),))

    def q_s1(u):
        ot, q = divmod(u, 4)
        s = u % 4
        P = lambda n: cscr.key(f"P{s}{n}")
        cp("dve", gvh(ot, q), cpiv(s, 0),
           reads=(P(0), gatest.key(f"p3:{ot}:{q}")),
           writes=(gatest.key(f"oq:{ot}:{q}"),))

    gen_q = chain_gen([q_s0, q_s1], NU)
    qdone = [-1]

    def q_pump_until(kk):
        while qdone[0] < kk:
            t = next(gen_q, None)
            if t is None:
                qdone[0] = FOT
            else:
                qdone[0] = (t - 5) // 4 if t >= 5 else -1

    dstg = ar.alloc("dstg", 2 * CT * TT * 2)
    fin = ar.alloc("fin", 2 * 1024 * 4)
    finb = ar.alloc("finb", 2 * 1024 * 2)
    YK = []

    def fin_chunk(i, t2):
        sbf = i % 2
        fres = fin.view(F32, [2, 1024])[:, sbf, :]
        fffn = finb.view(BF16, [2, 1024])[:, sbf, :]
        rd = (fin.key(f"cons{i - 2}"),) if i >= 2 else ()
        dma(fres, res_dram[i * 128:(i + 1) * 128, :],
            reads=("res_dram:all",) + rd,
            writes=fin.wfirst(f"x{i}") if i < 2 else (fin.key(f"x{i}"),),
            eng="act")
        dmaT(fffn, ffn_dram[:, i * 128:(i + 1) * 128],
             reads=(f"ffn_dram:{t2}",) + rd,
             writes=finb.wfirst(f"f{i}") if i < 2 else (finb.key(f"f{i}"),))
        tt("dve", fres, fres, fffn, ALU.add,
           reads=(fin.key(f"x{i}"), finb.key(f"f{i}")),
           writes=(fin.key(f"o{i}"),))
        dma(yout[i * 128:(i + 1) * 128, :], fres,
            reads=(fin.key(f"o{i}"),), writes=(f"y:{i}",))
        p.state[fin.key(f"cons{i}")] = [[p.state[f"y:{i}"][0]], []]
        YK.append(f"y:{i}")

    for t2 in range(NTT):
        dstg_v = dstg.view(BF16, [2, CT, TT])[:, t2 % 2]
        for k in range(FOT):
            if t2 == 0:
                q_pump_until(k)
            rk = (wdl_key(k),) + tuple(
                gatest.key(f"oq:{k}:{q}") for q in range(4))
            for od in range(CT):
                mm(ps[od].ap(), wdl_v(k)[:, od * 128:(od + 1) * 128],
                   gate_v[:, k, t2 * TT:(t2 + 1) * TT],
                   start=(k == 0), stop=(k == FOT - 1),
                   reads=rk, writes=(PSK[od],))
        for od in range(CT):
            act("act", dstg_v[:, od, :], ps[od].ap(), AF.Identity,
                scale=scv(SO),
                reads=(PSK[od], ko),
                writes=dstg.wfirst(f"{t2 % 2}:{od}") if t2 < 2 and od == 0
                else (dstg.key(f"{t2 % 2}:{od}"),))
        dma(ffn_dram[:, t2 * TT:(t2 + 1) * TT]
            .rearrange("(c p) t -> p c t", p=128),
            dstg_v,
            reads=tuple(dstg.key(f"{t2 % 2}:{od}") for od in range(CT)),
            writes=(f"ffn_dram:{t2}",))
        for od in range(CT):
            p.state[dstg.key(f"{t2 % 2}:{od}")][1].append(
                p.state[f"ffn_dram:{t2}"][0])
        for i in range(4 * t2, 4 * t2 + 4):
            fin_chunk(i, t2)
    p.merge_deps("ffn_dram:all", [f"ffn_dram:{t2}" for t2 in range(NTT)])

    for bb in (gatest_[0], gatest_[1], wdl[0], wdl[1], wdl[2], wdl[3],
               lmx, dstg, cscr, fin, finb):
        ar.free(bb)

    p.emit("sp", lambda e: e.nop(), reads=tuple(YK), writes=(), inc=0)

    import contextlib
    semnames = sorted({it[3] for it in p.items}
                      | {s for it in p.items for s in it[1]})
    with contextlib.ExitStack() as stx:
        sems = {s: stx.enter_context(nc.semaphore(s)) for s in semnames}
        block = stx.enter_context(nc.Block())
        p.replay(block, sems)
    build_nc.stats = {"n_items": len(p.items),
                      "arena_peak": ar.peak,
                      "sem_totals": dict(p.tot)}
    return nc


_NC_CACHE = []


def kernel(L0_attn_x, L0_attn_time_mix_r, L0_attn_time_mix_k,
           L0_attn_time_mix_v, L0_attn_time_decay, L0_attn_time_first,
           L0_attn_W_r, L0_attn_W_k, L0_attn_W_v, L0_attn_W_o,
           L0_attn_gn_weight, L0_attn_gn_bias,
           L0_ffn_w_gate, L0_ffn_w_up, L0_ffn_w_down):
    f32 = np.float32
    x = np.ascontiguousarray(np.asarray(L0_attn_x, f32))

    def pc(v):
        return np.ascontiguousarray(
            np.asarray(v, f32).reshape(CT, 128).T)

    tm = np.ascontiguousarray(np.stack(
        [pc(L0_attn_time_mix_r), pc(L0_attn_time_mix_k),
         pc(L0_attn_time_mix_v)], axis=0))
    decay = pc(np.asarray(L0_attn_time_decay, f32).reshape(-1))
    tf = pc(np.asarray(L0_attn_time_first, f32).reshape(-1))
    gnw_ = pc(L0_attn_gn_weight)
    gnb_ = pc(L0_attn_gn_bias)
    wrT_ = np.ascontiguousarray(np.asarray(L0_attn_W_r, f32).T)
    wkT_ = np.ascontiguousarray(np.asarray(L0_attn_W_k, f32).T)
    wvT_ = np.ascontiguousarray(np.asarray(L0_attn_W_v, f32).T)
    woT_ = np.ascontiguousarray(np.asarray(L0_attn_W_o, f32).T)
    wg_ = np.ascontiguousarray(np.asarray(L0_ffn_w_gate, f32))
    wu_ = np.ascontiguousarray(np.asarray(L0_ffn_w_up, f32))
    wd_ = np.ascontiguousarray(np.asarray(L0_ffn_w_down, f32))
    ones128 = np.ones((1, 128), f32)
    halfT_ = np.zeros((128, 2), f32)
    halfT_[:64, 0] = 1.0
    halfT_[64:, 1] = 1.0
    halfB_ = np.ascontiguousarray(halfT_.T)

    in_maps = []
    for c in range(NC):
        b, h = c // 2, c % 2
        prev = (np.zeros((1, D), f32) if h == 0
                else x[b, TH - 1:TH, :])
        xs_ = np.ascontiguousarray(
            np.concatenate([prev, x[b, h * TH:(h + 1) * TH, :]], axis=0))
        selv_ = np.zeros((8, 1), f32)
        if h == 1:
            selv_[c - 1, 0] = 1.0
        selp2_ = np.zeros((8, 1), f32)
        selp2_[2 * b, 0] = 1.0
        selp2_[2 * b + 1, 0] = 1.0
        in_maps.append({
            "xs": xs_, "wrT": wrT_, "wkT": wkT_, "wvT": wvT_, "woT": woT_,
            "tm": tm, "decay": decay, "tfirst": tf,
            "gnw": gnw_, "gnb": gnb_,
            "w_gate": wg_, "w_up": wu_, "w_down": wd_,
            "selv": selv_, "selp2": selp2_, "ones128": ones128,
            "halfT": halfT_, "halfB": halfB_,
        })

    if not _NC_CACHE:
        _NC_CACHE.append(build_nc())
    nc = _NC_CACHE[0]
    res = run_bass_kernel_spmd(nc, in_maps, core_ids=list(range(NC)))
    y = np.empty((B, T, D), f32)
    for c in range(NC):
        b, h = c // 2, c % 2
        y[b, h * TH:(h + 1) * TH, :] = res.results[c]["y"]
    return y


if __name__ == "__main__":
    nc = build_nc()
    print("build ok:", build_nc.stats)
